# version 02
# baseline (speedup 1.0000x reference)
import numpy as np
import ml_dtypes
from contextlib import ExitStack, contextmanager as _contextmanager

import concourse.bass as bass
import concourse.mybir as mybir
from concourse.bass_utils import run_bass_kernel_spmd

F32 = mybir.dt.float32
BF16 = mybir.dt.bfloat16
AF = mybir.ActivationFunctionType
OP = mybir.AluOpType
AX = mybir.AxisListType

D = 2048
T = 2048
KC = D // 128
NSLAB = 4
SLAB = 512
EPS_LN = 1e-5

ENGS = ("tensor", "vector", "scalar", "sync", "gpsimd")


class Sem:
    def __init__(self, stack, nc, name):
        self.h = stack.enter_context(nc.semaphore(name))
        self.n = 0

    def inc(self, k=1):
        self.n += k
        return self.n


class Ph:

    def __init__(self, nc):
        self.nc = nc
        self.ops = {e: [] for e in ENGS}

    def wait(self, eng, sem, tok):
        if tok is None:
            return
        self.ops[eng].append(lambda h: h.wait_ge(sem.h, tok))

    def dma(self, eng, out, in_, sem, **kw):
        tok = sem.inc(16)
        self.ops[eng].append(
            lambda h: h.dma_start(out=out, in_=in_, **kw).then_inc(sem.h, 16))
        return tok

    def op(self, eng, fn, sem=None, k=1):
        if sem is None:
            self.ops[eng].append(lambda h: fn(h))
            return None
        tok = sem.inc(k)
        self.ops[eng].append(lambda h: fn(h).then_inc(sem.h, k))
        return tok

    def gop(self, eng, fn, sem, k=1):
        tok = self.op(eng, fn, sem=sem, k=k)
        self.wait(eng, sem, tok)
        return tok

    def emit(self):
        with self.nc.Block() as blk:
            for e in ENGS:
                lst = self.ops[e]
                if not lst:
                    continue

                def body(h, lst=lst):
                    for fn in lst:
                        fn(h)

                getattr(blk, e)(body)


class Ring:

    def __init__(self, nbuf):
        self.n = nbuf
        self.free = [None] * nbuf
        self.i = 0

    def acquire(self, ph, prod_eng):
        b = self.i % self.n
        self.i += 1
        if self.free[b] is not None:
            sem, tok = self.free[b]
            ph.wait(prod_eng, sem, tok)
        return b

    def release(self, b, sem, tok):
        self.free[b] = (sem, tok)


class Ctx:
    def __init__(self, nc, stack):
        self.nc = nc
        self.stack = stack
        self._scopes = []
        self._semcache = {}

    def sem(self, name):
        if name not in self._semcache:
            self._semcache[name] = Sem(self.stack, self.nc, name)
        return self._semcache[name]

    def _cur(self):
        return self._scopes[-1] if self._scopes else self.stack

    @_contextmanager
    def scope(self):
        st = ExitStack()
        self._scopes.append(st)
        try:
            with st:
                yield
        finally:
            self._scopes.pop()

    def sbuf(self, name, shape, dtype):
        return self._cur().enter_context(self.nc.sbuf_tensor(name, list(shape), dtype))

    def psum(self, name, shape, dtype=F32):
        return self.stack.enter_context(self.nc.psum_tensor(name, list(shape), dtype))



def make_common(ctx):
    c = {}
    c["PS"] = [ctx.psum(f"PS{i}", [128, 512]) for i in range(8)]
    c["ones_col"] = ctx.sbuf("ones_col", [128, 1], F32)
    c["ones_row"] = ctx.sbuf("ones_row", [1, 128], F32)
    c["ones_col_bf"] = ctx.sbuf("ones_col_bf", [128, 1], BF16)
    c["ones_row_bf"] = ctx.sbuf("ones_row_bf", [1, 128], BF16)
    c["xs"] = [ctx.sbuf(f"xs{i}", [128, 1, 512], F32) for i in range(2)]
    c["sq"] = [ctx.sbuf(f"sq{i}", [128, 512], F32) for i in range(2)]
    c["mean"] = ctx.sbuf("mean_row", [1, T], F32)
    c["rstd"] = ctx.sbuf("rstd_row", [1, T], F32)
    c["srow"] = ctx.sbuf("scratch_row", [1, 512], F32)
    c["srow2"] = ctx.sbuf("scratch_row2", [1, 512], F32)
    c["eps_ln"] = ctx.sbuf("eps_ln", [1, 1], F32)
    c["ident"] = ctx.sbuf("ident", [128, 128], F32)
    c["wstg"] = [ctx.sbuf(f"wstg{i}", [128, 2048], BF16) for i in range(2)]
    ph = Ph(ctx.nc)
    init = ctx.sem("init")
    ph.op("gpsimd", lambda h: h.memset(c["ones_col"][:, :], 1.0))
    ph.op("gpsimd", lambda h: h.memset(c["ones_row"][:, :], 1.0))
    ph.op("gpsimd", lambda h: h.memset(c["ones_col_bf"][:, :], 1.0))
    ph.op("gpsimd", lambda h: h.memset(c["ones_row_bf"][:, :], 1.0))
    ph.op("gpsimd", lambda h: h.memset(c["eps_ln"][:, :], EPS_LN))
    ph.op("gpsimd", lambda h: h.iota(c["ident"][:, :], [[1, 128]], channel_multiplier=0,
                                     allow_small_or_imprecise_dtypes=True))
    ph.op("gpsimd", lambda h: h.iota(c["sq"][0][:, 0:128], [[0, 128]], channel_multiplier=1,
                                     allow_small_or_imprecise_dtypes=True), sem=init)
    ph2 = Ph(ctx.nc)
    ph2.wait("vector", init, init.n)
    ph2.op("vector", lambda h: h.tensor_tensor(c["ident"][:, :], c["ident"][:, :],
                                               c["sq"][0][:, 0:128], OP.is_equal), sem=init)
    ph.emit()
    ph2.emit()
    c["init_tok"] = init.n
    c["init_sem"] = init
    return c


def emit_ln(ctx, c, pfx, x_ap, h_sb):
    nc = ctx.nc
    PS = c["PS"]
    s_x = ctx.sem(pfx + "x")
    s_mm = ctx.sem(pfx + "mm")
    s_act = ctx.sem(pfx + "act")
    s_dve = ctx.sem(pfx + "dve")
    xr = Ring(2)
    sqr = Ring(2)
    x_grp = x_ap.rearrange("(c p) t -> p c t", p=128)

    ph = Ph(nc)
    ph.wait("tensor", c["init_sem"], c["init_tok"])
    stat_free_tok = None
    for sl in range(NSLAB):
        s0 = sl * SLAB
        if stat_free_tok is not None:
            ph.wait("tensor", s_dve, stat_free_tok)
        for kc in range(KC):
            b = xr.acquire(ph, "gpsimd")
            xtok = ph.dma("gpsimd", c["xs"][b][:, 0, :],
                          x_grp[:, kc, s0:s0 + SLAB], s_x)
            cur = {"b": b, "tok": xtok}
            xs_ap = c["xs"][b][:, 0, :]
            sb = sqr.acquire(ph, "scalar")
            ph.wait("scalar", s_x, cur["tok"])
            atok = ph.op("scalar", lambda h, o=c["sq"][sb][:, :], i=xs_ap:
                         h.activation(o, i, AF.Square), sem=s_act)
            ph.wait("tensor", s_x, cur["tok"])
            ph.op("tensor", lambda h, i=xs_ap, st=(kc == 0), sp=(kc == KC - 1):
                  h.matmul(PS[0][0:1, :], c["ones_col"][:, :], i, start=st, stop=sp))
            ph.wait("tensor", s_act, atok)
            mtok = ph.op("tensor", lambda h, i=c["sq"][sb][:, :], st=(kc == 0), sp=(kc == KC - 1):
                         h.matmul(PS[1][0:1, :], c["ones_col"][:, :], i, start=st, stop=sp),
                         sem=s_mm)
            sqr.release(sb, s_mm, mtok)
            xr.release(b, s_mm, mtok)
        sums_done = mtok
        ph.wait("vector", s_mm, sums_done)
        ph.op("vector", lambda h, o=c["mean"][0:1, s0:s0 + SLAB]:
              h.tensor_scalar(o, PS[0][0:1, :], 1.0 / D, None, OP.mult))
        ph.op("vector", lambda h:
              h.tensor_scalar(c["srow"][0:1, :], PS[1][0:1, :], 1.0 / D, None, OP.mult))
        ph.op("vector", lambda h, m=c["mean"][0:1, s0:s0 + SLAB]:
              h.tensor_tensor(c["srow2"][0:1, :], m, m, OP.mult))
        vtok = ph.op("vector", lambda h:
                     h.tensor_tensor(c["srow"][0:1, :], c["srow"][0:1, :],
                                     c["srow2"][0:1, :], OP.subtract), sem=s_dve)
        stat_free_tok = vtok
        ph.wait("scalar", s_dve, vtok)
        atok = ph.op("scalar", lambda h:
                     h.activation(c["srow2"][0:1, :], c["srow"][0:1, :], AF.Sqrt,
                                  bias=c["eps_ln"][0:1, :]), sem=s_act)
        ph.wait("vector", s_act, atok)
        ph.op("vector", lambda h, o=c["rstd"][0:1, s0:s0 + SLAB]:
              h.reciprocal(o, c["srow2"][0:1, :]), sem=s_dve)
    rstd_tok = s_dve.n
    bc_free = rstd_tok
    for sl in range(NSLAB):
        s0 = sl * SLAB
        ph.wait("tensor", s_dve, bc_free)
        ph.op("tensor", lambda h, i=c["mean"][0:1, s0:s0 + SLAB]:
              h.matmul(PS[2][:, :], c["ones_row"][:, :], i, start=True, stop=True))
        btok = ph.op("tensor", lambda h, i=c["rstd"][0:1, s0:s0 + SLAB]:
                     h.matmul(PS[3][:, :], c["ones_row"][:, :], i, start=True, stop=True),
                     sem=s_mm)
        ph.wait("vector", s_mm, btok)
        for kc in range(KC):
            b = xr.acquire(ph, "gpsimd")
            xtok = ph.dma("gpsimd", c["xs"][b][:, 0, :],
                          x_grp[:, kc, s0:s0 + SLAB], s_x)
            cur = {"b": b, "tok": xtok}
            xs_ap = c["xs"][b][:, 0, :]
            ph.wait("vector", s_x, cur["tok"])
            ph.op("vector", lambda h, o=c["sq"][0][:, :], i=xs_ap:
                  h.tensor_tensor(o, i, PS[2][:, :], OP.subtract))
            dtok = ph.op("vector", lambda h, o=h_sb[:, kc, s0:s0 + SLAB], i=c["sq"][0][:, :]:
                         h.tensor_tensor(o, i, PS[3][:, :], OP.mult), sem=s_dve)
            xr.release(b, s_dve, dtok)
        bc_free = dtok
    ph.emit()
    return s_dve.n


def emit_l1ffn(ctx, c, pfx, x_ap, y_ap, wg_ap, wu_ap, wd_ap, h_sb):
    nc = ctx.nc
    PS = c["PS"]
    I = 8192
    NQ = 4
    ICQ = 16

    s_w = ctx.sem(pfx + "w")
    s_xb = ctx.sem(pfx + "xb")
    s_mm = ctx.sem(pfx + "mm")
    s_act = ctx.sem(pfx + "act")
    s_dve = ctx.sem(pfx + "dve")
    s_acc = ctx.sem(pfx + "acc")

    act_sb = ctx.sbuf(pfx + "act", [128, ICQ, T], BF16)
    wgt = [ctx.sbuf(pfx + f"wgt{i}", [128, 16, 128], BF16) for i in range(2)]
    wut = [ctx.sbuf(pfx + f"wut{i}", [128, 16, 128], BF16) for i in range(2)]
    wdt = [ctx.sbuf(pfx + f"wdt{i}", [128, 16, 128], BF16) for i in range(2)]
    sil = [ctx.sbuf(pfx + f"sil{i}", [128, 512], F32) for i in range(2)]
    outstg = [ctx.sbuf(pfx + f"outstg{i}", [128, 1024], F32) for i in range(2)]

    ph = Ph(nc)
    ytok = ph.dma("gpsimd", y_ap[:, :], x_ap[:, :], s_acc)
    ph.emit()

    h_done = emit_ln(ctx, c, pfx + "ln_", x_ap, h_sb)
    s_lndve = ctx.sem(pfx + "ln_dve")

    ph = Ph(nc)
    ph.wait("tensor", s_lndve, h_done)

    wstg_r = Ring(2)
    wgt_r = Ring(2)
    wut_r = Ring(2)
    wdt_r = Ring(2)
    sil_r = Ring(2)
    out_r = Ring(2)
    ps_free = [None] * 8

    prev_q_acc_tok = ytok
    act_done_tok = None

    for q in range(NQ):
        for ic in range(ICQ):
            icg = q * ICQ + ic
            b1 = wstg_r.acquire(ph, "gpsimd")
            wtok = ph.dma("gpsimd", c["wstg"][b1][:, :],
                          wg_ap[icg * 128:(icg + 1) * 128, :], s_w)
            g1 = wgt_r.acquire(ph, "sync")
            ph.wait("sync", s_w, wtok)
            xtok = ph.dma("sync", wgt[g1][:, :, :], c["wstg"][b1][:, :], s_xb,
                          transpose=True)
            wstg_r.release(b1, s_xb, xtok)
            b2 = wstg_r.acquire(ph, "gpsimd")
            wtok2 = ph.dma("gpsimd", c["wstg"][b2][:, :],
                           wu_ap[icg * 128:(icg + 1) * 128, :], s_w)
            g2 = wut_r.acquire(ph, "sync")
            ph.wait("sync", s_w, wtok2)
            xtok2 = ph.dma("sync", wut[g2][:, :, :], c["wstg"][b2][:, :], s_xb,
                           transpose=True)
            wstg_r.release(b2, s_xb, xtok2)

            ph.wait("tensor", s_xb, xtok2)
            gu_mm = []
            for sl in range(NSLAB):
                s0 = sl * SLAB
                if ps_free[sl] is not None:
                    ph.wait("tensor", *ps_free[sl])
                gtok = None
                for kc in range(KC):
                    last = kc == KC - 1
                    gtok = ph.op("tensor",
                                 lambda h, w=wgt[g1][:, kc, :], r=h_sb[:, kc, s0:s0 + SLAB],
                                 p=PS[sl][:, :], st=(kc == 0), sp=last:
                                 h.matmul(p, w, r, start=st, stop=sp),
                                 sem=s_mm if last else None)
                if ps_free[4 + sl] is not None:
                    ph.wait("tensor", *ps_free[4 + sl])
                utok = None
                for kc in range(KC):
                    last = kc == KC - 1
                    utok = ph.op("tensor",
                                 lambda h, w=wut[g2][:, kc, :], r=h_sb[:, kc, s0:s0 + SLAB],
                                 p=PS[4 + sl][:, :], st=(kc == 0), sp=last:
                                 h.matmul(p, w, r, start=st, stop=sp),
                                 sem=s_mm if last else None)
                gu_mm.append((gtok, utok))
            wgt_r.release(g1, s_mm, gu_mm[-1][0])
            wut_r.release(g2, s_mm, gu_mm[-1][1])
            for sl in range(NSLAB):
                s0 = sl * SLAB
                gtok, utok = gu_mm[sl]
                sb = sil_r.acquire(ph, "scalar")
                ph.wait("scalar", s_mm, gtok)
                atok = ph.op("scalar", lambda h, o=sil[sb][:, :], p=PS[sl][:, :]:
                             h.activation(o, p, AF.Silu), sem=s_act)
                ps_free[sl] = (s_act, atok)
                ph.wait("vector", s_act, atok)
                ph.wait("vector", s_mm, utok)
                dtok = ph.op("vector", lambda h, o=act_sb[:, ic, s0:s0 + SLAB],
                             i=sil[sb][:, :], p=PS[4 + sl][:, :]:
                             h.tensor_tensor(o, i, p, OP.mult), sem=s_dve)
                ps_free[4 + sl] = (s_dve, dtok)
                sil_r.release(sb, s_dve, dtok)
                act_done_tok = dtok

        for oc in range(KC):
            b1 = wstg_r.acquire(ph, "gpsimd")
            src = wd_ap[oc * 128:(oc + 1) * 128, q * 2048:(q + 1) * 2048]
            wtok = ph.dma("gpsimd", c["wstg"][b1][:, :], src, s_w)
            g1 = wdt_r.acquire(ph, "sync")
            ph.wait("sync", s_w, wtok)
            xtok = ph.dma("sync", wdt[g1][:, :, :], c["wstg"][b1][:, :], s_xb,
                          transpose=True)
            wstg_r.release(b1, s_xb, xtok)

            ph.wait("tensor", s_xb, xtok)
            if oc == 0:
                ph.wait("tensor", s_dve, act_done_tok)
            mtok = etok = None
            for half in range(2):
                ob = out_r.acquire(ph, "scalar")
                for hs in range(2):
                    sl = half * 2 + hs
                    s0 = sl * SLAB
                    pb = (oc * NSLAB + sl) % 2
                    if ps_free[pb] is not None:
                        ph.wait("tensor", *ps_free[pb])
                    for ic in range(ICQ):
                        last = ic == ICQ - 1
                        mtok = ph.op("tensor",
                                     lambda h, w=wdt[g1][:, ic, :], r=act_sb[:, ic, s0:s0 + SLAB],
                                     p=PS[pb][:, :], st=(ic == 0), sp=last:
                                     h.matmul(p, w, r, start=st, stop=sp),
                                     sem=s_mm if last else None)
                    ph.wait("scalar", s_mm, mtok)
                    etok = ph.op("scalar",
                                 lambda h, o=outstg[ob][:, hs * 512:(hs + 1) * 512], p=PS[pb][:, :]:
                                 h.copy(o, p), sem=s_act)
                    ps_free[pb] = (s_act, etok)
                ph.wait("gpsimd", s_act, etok)
                ph.wait("gpsimd", s_acc, prev_q_acc_tok)
                acctok = ph.dma("gpsimd",
                                y_ap[oc * 128:(oc + 1) * 128, half * 1024:(half + 1) * 1024],
                                outstg[ob][:, :], s_acc, accum_op=OP.add)
                out_r.release(ob, s_acc, acctok)
            wdt_r.release(g1, s_mm, mtok)
        prev_q_acc_tok = s_acc.n
        ph.wait("vector", s_mm, s_mm.n)

    ph.wait("sync", s_acc, s_acc.n)
    ph.emit()
    return s_acc.n



def emit_l1attn(ctx, c, pfx, x_ap, y_ap, w_aps, b_aps, rope_cos_ap, rope_sin_ap,
                cmask_ap, h_sb, attn_sb):
    nc = ctx.nc
    PS = c["PS"]
    NH = 16
    HD = 128
    INV_S = 1.0 / float(np.sqrt(HD))

    s_w = ctx.sem(pfx + "w")
    s_xb = ctx.sem(pfx + "xb")
    s_mm = ctx.sem(pfx + "mm")
    s_act = ctx.sem(pfx + "act")
    s_dve = ctx.sem(pfx + "dve")
    s_x = ctx.sem(pfx + "x")
    s_st = ctx.sem(pfx + "st")

    wt = [ctx.sbuf(pfx + f"wt{i}", [128, 16, 128], BF16) for i in range(3)]
    qraw = [ctx.sbuf(pfx + f"qraw{i}", [128, 512], F32) for i in range(1)]
    qfm = ctx.sbuf(pfx + "qfm", [128, T], BF16)
    kfm = ctx.sbuf(pfx + "kfm", [128, T], BF16)
    vtm = ctx.sbuf(pfx + "vtm", [128, 16, 128], BF16)
    sqb = [ctx.sbuf(pfx + f"sqb{i}", [128, 512], BF16) for i in range(1)]
    negmb = [ctx.sbuf(pfx + f"negmb{i}", [128, 512], BF16) for i in range(2)]
    rb = [ctx.sbuf(pfx + f"rb{i}", [128, 128], F32) for i in range(2)]
    ptile = [ctx.sbuf(pfx + f"p{i}", [128, 128], BF16) for i in range(3)]
    stile = [ctx.sbuf(pfx + f"s{i}", [128, 128], F32) for i in range(2)]
    bcols = ctx.sbuf(pfx + "bcols", [128, 4, 16], F32)
    bvrow = ctx.sbuf(pfx + "bvrow", [1, 128], BF16)
    ystg = [ctx.sbuf(pfx + f"ystg{i}", [128, 512], F32) for i in range(1)]
    cosd = ctx.sbuf(pfx + "cosd", [128, T], BF16)
    sind = ctx.sbuf(pfx + "sind", [128, T], BF16)
    cmask = ctx.sbuf(pfx + "cmask", [128, 128], BF16)

    ph = Ph(nc)
    ph.wait("gpsimd", c["init_sem"], c["init_tok"])
    ph.dma("gpsimd", cosd[0:64, :], rope_cos_ap[:, :], s_w)
    ph.dma("gpsimd", cosd[64:128, :], rope_cos_ap[:, :], s_w)
    ph.dma("gpsimd", sind[0:64, :], rope_sin_ap[:, :], s_w)
    ph.dma("gpsimd", sind[64:128, :], rope_sin_ap[:, :], s_w)
    ph.dma("gpsimd", cmask[:, :], cmask_ap[:, :], s_w)
    for bi, bn in enumerate(("bq", "bk", "bo")):
        btok = ph.dma("gpsimd", c["sq"][0][0:16, 0:128],
                      b_aps[bn].rearrange("(a b) -> a b", b=128), s_w)
        ph.wait("tensor", s_w, btok)
        ph.wait("tensor", c["init_sem"], c["init_tok"])
        mtok = ph.op("tensor", lambda h: h.transpose(PS[7][0:128, 0:16], c["sq"][0][0:16, 0:128],
                                                     c["ident"][0:16, 0:16]), sem=s_mm)
        ph.wait("vector", s_mm, mtok)
        vtok = ph.op("vector", lambda h, o=bcols[:, bi, :]: h.tensor_copy(o, PS[7][:, 0:16]),
                     sem=s_dve)
        ph.wait("tensor", s_dve, vtok)
        ph.wait("gpsimd", s_dve, vtok)
    ph.emit()
    const_dve = s_dve.n
    const_w = s_w.n

    h_done = emit_ln(ctx, c, pfx + "ln_", x_ap, h_sb)
    s_lndve = ctx.sem(pfx + "ln_dve")

    ph = Ph(nc)
    ph.wait("tensor", s_lndve, h_done)
    ph.wait("vector", s_w, const_w)
    ph.wait("vector", s_dve, const_dve)
    ph.wait("scalar", s_w, const_w)

    wstg_r = Ring(2)
    wt_r = Ring(3)
    qraw_r = Ring(1)
    sqb_r = Ring(1)
    negmb_r = Ring(2)
    rb_r = Ring(2)
    p_r = Ring(3)
    s_r = Ring(2)
    ystg_r = Ring(1)
    ps01_free = [None, None]
    ps23_free = [None, None]
    ps45_free = [None, None]
    ps6_free = [None]
    ps7_free = [None]

    def load_wt(wname, row0):
        bstg = wstg_r.acquire(ph, "gpsimd")
        wtok = ph.dma("gpsimd", c["wstg"][bstg][:, :], w_aps[wname][row0:row0 + 128, :], s_w)
        g = wt_r.acquire(ph, "sync")
        ph.wait("sync", s_w, wtok)
        xtok = ph.dma("sync", wt[g][:, :, :], c["wstg"][bstg][:, :], s_xb, transpose=True)
        wstg_r.release(bstg, s_xb, xtok)
        return g, xtok

    prev_out_tok = None

    for head in range(NH):
        r0 = head * HD
        if prev_out_tok is not None:
            ph.wait("gpsimd", s_mm, prev_out_tok)
        bvtok = ph.dma("gpsimd", bvrow[0:1, :],
                       b_aps["bv"][r0:r0 + 128].rearrange("(a b) -> a b", a=1), s_w)
        gq, xtq = load_wt("Wq", r0)
        gk, xtk = load_wt("Wk", r0)
        gv, xtv = load_wt("Wv", r0)
        first_rope_wait = True
        for which, g, xt, out_fm, bcol_i, scale2 in (
                ("q", gq, xtq, qfm, 0, INV_S), ("k", gk, xtk, kfm, 1, None)):
            ph.wait("tensor", s_xb, xt)
            for sl in range(NSLAB):
                s0 = sl * SLAB
                pb = sl % 2
                if ps01_free[pb] is not None:
                    ph.wait("tensor", *ps01_free[pb])
                mtok = None
                for kc in range(KC):
                    last = kc == KC - 1
                    mtok = ph.op("tensor",
                                 lambda h, w=wt[g][:, kc, :], r=h_sb[:, kc, s0:s0 + SLAB],
                                 p=PS[pb][:, :], st=(kc == 0), sp=last:
                                 h.matmul(p, w, r, start=st, stop=sp),
                                 sem=s_mm if last else None)
                qb = qraw_r.acquire(ph, "vector")
                ph.wait("vector", s_mm, mtok)
                if first_rope_wait and prev_out_tok is not None:
                    ph.wait("vector", s_mm, prev_out_tok)
                first_rope_wait = False
                if scale2 is None:
                    dtok = ph.op("vector", lambda h, o=qraw[qb][:, :], p=PS[pb][:, :],
                                 s1=bcols[:, bcol_i, head:head + 1]:
                                 h.tensor_scalar(o, p, s1, None, OP.add), sem=s_dve)
                else:
                    dtok = ph.op("vector", lambda h, o=qraw[qb][:, :], p=PS[pb][:, :],
                                 s1=bcols[:, bcol_i, head:head + 1], s2=scale2:
                                 h.tensor_scalar(o, p, s1, s2, OP.add, OP.mult), sem=s_dve)
                ps01_free[pb] = (s_dve, dtok)
                q0 = qraw[qb][0:64, :]
                q1 = qraw[qb][64:128, :]
                cs0 = cosd[0:64, s0:s0 + SLAB]
                cs1 = cosd[64:128, s0:s0 + SLAB]
                sn0 = sind[0:64, s0:s0 + SLAB]
                sn1 = sind[64:128, s0:s0 + SLAB]
                ta0 = c["sq"][0][0:64, :]
                tb0 = c["sq"][1][0:64, :]
                ta1 = c["sq"][0][64:128, :]
                tb1 = c["sq"][1][64:128, :]
                ph.op("vector", lambda h, o=ta0, a=q0, b=cs0: h.tensor_tensor(o, a, b, OP.mult))
                ph.op("vector", lambda h, o=tb0, a=q1, b=sn1: h.tensor_tensor(o, a, b, OP.mult))
                ph.op("vector", lambda h, o=out_fm[0:64, s0:s0 + SLAB], a=ta0, b=tb0:
                      h.tensor_tensor(o, a, b, OP.subtract))
                ph.op("vector", lambda h, o=ta1, a=q1, b=cs1: h.tensor_tensor(o, a, b, OP.mult))
                ph.op("vector", lambda h, o=tb1, a=q0, b=sn0: h.tensor_tensor(o, a, b, OP.mult))
                dtok = ph.op("vector", lambda h, o=out_fm[64:128, s0:s0 + SLAB], a=ta1, b=tb1:
                             h.tensor_tensor(o, a, b, OP.add), sem=s_dve)
                qraw_r.release(qb, s_dve, dtok)
        rope_done = s_dve.n
        wt_r.release(gq, s_dve, rope_done)
        wt_r.release(gk, s_dve, rope_done)
        ph.wait("tensor", s_xb, xtv)
        first_v = True
        for tc in range(KC):
            t0c = tc * 128
            pb = tc % 2
            if ps01_free[pb] is not None:
                ph.wait("tensor", *ps01_free[pb])
            for kc in range(KC):
                ph.op("tensor", lambda h, w=h_sb[:, kc, t0c:t0c + 128], r=wt[gv][:, kc, :],
                      p=PS[pb][:, 0:128], st=(kc == 0):
                      h.matmul(p, w, r, start=st, stop=False))
            if tc == 0:
                ph.wait("tensor", s_w, bvtok)
            mtok = ph.op("tensor", lambda h, p=PS[pb][:, 0:128], r=bvrow[0:1, 0:128]:
                         h.matmul(p, c["ones_row_bf"][:, :], r, start=False, stop=True),
                         sem=s_mm)
            ph.wait("scalar", s_mm, mtok)
            if first_v and prev_out_tok is not None:
                ph.wait("scalar", s_mm, prev_out_tok)
            first_v = False
            atok = ph.op("scalar", lambda h, o=vtm[:, tc, :], p=PS[pb][:, 0:128]:
                         h.copy(o, p), sem=s_act)
            ps01_free[pb] = (s_act, atok)
        v_done = s_act.n
        wt_r.release(gv, s_act, v_done)

        for which, fm, row in (("q", qfm, c["mean"]), ("k", kfm, c["rstd"])):
            for sl in range(NSLAB):
                s0 = sl * SLAB
                sb = sqb_r.acquire(ph, "scalar")
                if which == "q" and sl == 0:
                    ph.wait("scalar", s_dve, rope_done)
                atok = ph.op("scalar", lambda h, o=sqb[sb][:, :], i=fm[:, s0:s0 + SLAB]:
                             h.activation(o, i, AF.Square), sem=s_act)
                ph.wait("tensor", s_act, atok)
                if ps6_free[0] is not None:
                    ph.wait("tensor", *ps6_free[0])
                mtok = ph.op("tensor", lambda h, i=sqb[sb][:, :]:
                             h.matmul(PS[6][0:1, :], c["ones_col_bf"][:, :], i,
                                      start=True, stop=True), sem=s_mm)
                sqb_r.release(sb, s_mm, mtok)
                ph.wait("vector", s_mm, mtok)
                vtok = ph.op("vector", lambda h, o=row[0:1, s0:s0 + SLAB]:
                             h.tensor_copy(o, PS[6][0:1, :]), sem=s_dve)
                ps6_free[0] = (s_dve, vtok)
            ph.wait("scalar", s_dve, s_dve.n)
            ph.op("scalar", lambda h, o=row[0:1, :]: h.activation(o, o, AF.Sqrt), sem=s_act)
        norm_act = s_act.n
        ph.wait("vector", s_act, norm_act)
        gtok_ = ph.op("vector", lambda h: h.tensor_reduce(
            c["srow"][0:1, 0:16],
            c["rstd"][0:1, :].rearrange("p (a b) -> p a b", b=128), AX.X, OP.max), sem=s_dve)
        ph.wait("vector", s_dve, gtok_)
        ph.gop("vector", lambda h: h.tensor_copy(c["srow"][0:1, 16:17], c["srow"][0:1, 0:1]),
               s_dve)
        for i in range(1, 16):
            ph.gop("vector", lambda h, i=i: h.tensor_tensor(
                c["srow"][0:1, 16 + i:17 + i], c["srow"][0:1, 15 + i:16 + i],
                c["srow"][0:1, i:i + 1], OP.max), s_dve)
        negm_tok = None
        for qt in range(16):
            negm_tok = ph.op("vector", lambda h, qt=qt: h.tensor_scalar(
                c["rstd"][0:1, qt * 128:(qt + 1) * 128],
                c["mean"][0:1, qt * 128:(qt + 1) * 128],
                c["srow"][0:1, 16 + qt:17 + qt], -1.0, OP.mult, OP.mult),
                sem=s_dve if qt == 15 else None)

        negmb_cur = None
        negmb_tok = None
        last_sub_tok = None
        for qt in range(16):
            q0c = qt * 128
            slb = qt // 4
            if qt % 4 == 0:
                if negmb_cur is not None:
                    negmb_r.release(negmb_cur, s_dve, last_sub_tok)
                nb = negmb_r.acquire(ph, "tensor")
                if ps7_free[0] is not None:
                    ph.wait("tensor", *ps7_free[0])
                ph.wait("tensor", s_dve, negm_tok)
                mtok = ph.op("tensor", lambda h, i=c["rstd"][0:1, slb * 512:(slb + 1) * 512]:
                             h.matmul(PS[7][:, :], c["ones_row"][:, :], i,
                                      start=True, stop=True), sem=s_mm)
                ph.wait("scalar", s_mm, mtok)
                btok = ph.op("scalar", lambda h, o=negmb[nb][:, :]:
                             h.copy(o, PS[7][:, :]), sem=s_act)
                ps7_free[0] = (s_act, btok)
                negmb_tok = btok
                negmb_cur = nb
            if ps6_free[0] is not None:
                ph.wait("tensor", *ps6_free[0])
            pout = qt % 2
            if ps45_free[pout] is not None:
                ph.wait("tensor", *ps45_free[pout])
            out_tok = None
            for st in range(qt + 1):
                psc = s_r.acquire(ph, "vector")
                pb23 = st % 2
                if ps23_free[pb23] is not None:
                    ph.wait("tensor", *ps23_free[pb23])
                mtok = ph.op("tensor", lambda h, w=kfm[:, st * 128:(st + 1) * 128],
                             r=qfm[:, q0c:q0c + 128], p=PS[2 + pb23][:, 0:128]:
                             h.matmul(p, w, r, start=True, stop=True), sem=s_mm)
                ph.wait("vector", s_mm, mtok)
                if st == 0:
                    ph.wait("vector", s_act, negmb_tok)
                vtok = ph.op("vector", lambda h, o=stile[psc][:, :], p=PS[2 + pb23][:, 0:128],
                             nm=negmb[negmb_cur][:, (qt % 4) * 128:(qt % 4 + 1) * 128]:
                             h.tensor_tensor(o, p, nm, OP.add), sem=s_dve)
                ps23_free[pb23] = (s_dve, vtok)
                last_sub_tok = vtok
                if st == qt:
                    vtok = ph.op("vector", lambda h, o=stile[psc][:, :]:
                                 h.tensor_tensor(o, o, cmask[:, :], OP.add), sem=s_dve)
                    last_sub_tok = vtok
                pt = p_r.acquire(ph, "scalar")
                ph.wait("scalar", s_dve, vtok)
                atok = ph.op("scalar", lambda h, o=ptile[pt][:, :], i=stile[psc][:, :]:
                             h.activation(o, i, AF.Exp), sem=s_act)
                s_r.release(psc, s_act, atok)
                ph.wait("tensor", s_act, atok)
                dn_tok = ph.op("tensor", lambda h, r=ptile[pt][:, :],
                               st_=(st == 0), sp=(st == qt):
                               h.matmul(PS[6][0:1, 0:128], c["ones_col_bf"][:, :], r,
                                        start=st_, stop=sp),
                               sem=s_mm if st == qt else None)
                out_tok = ph.op("tensor", lambda h, w=vtm[:, st, :], r=ptile[pt][:, :],
                                p=PS[4 + pout][:, 0:128], st_=(st == 0), sp=(st == qt):
                                h.matmul(p, w, r, start=st_, stop=sp), sem=s_mm)
                p_r.release(pt, s_mm, out_tok)
            ph.wait("vector", s_mm, dn_tok)
            vtok = ph.op("vector", lambda h, o=c["mean"][0:1, q0c:q0c + 128]:
                         h.tensor_copy(o, PS[6][0:1, 0:128]), sem=s_dve)
            ps6_free[0] = (s_dve, vtok)
            rtok = ph.op("vector", lambda h, o=c["srow2"][0:1, 32:160],
                         i=c["mean"][0:1, q0c:q0c + 128]:
                         h.reciprocal(o, i), sem=s_dve)
            rbb = rb_r.acquire(ph, "tensor")
            if ps7_free[0] is not None:
                ph.wait("tensor", *ps7_free[0])
            ph.wait("tensor", s_dve, rtok)
            mtok = ph.op("tensor", lambda h:
                         h.matmul(PS[7][:, 0:128], c["ones_row"][:, :],
                                  c["srow2"][0:1, 32:160], start=True, stop=True), sem=s_mm)
            ph.wait("scalar", s_mm, mtok)
            btok = ph.op("scalar", lambda h, o=rb[rbb][:, :]: h.copy(o, PS[7][:, 0:128]),
                         sem=s_act)
            ps7_free[0] = (s_act, btok)
            ph.wait("vector", s_act, btok)
            ph.wait("vector", s_mm, out_tok)
            ntok = ph.op("vector", lambda h, o=attn_sb[:, head, q0c:q0c + 128],
                         p=PS[4 + pout][:, 0:128], r=rb[rbb][:, :]:
                         h.tensor_tensor(o, p, r, OP.mult), sem=s_dve)
            ps45_free[pout] = (s_dve, ntok)
            rb_r.release(rbb, s_dve, ntok)
            prev_out_tok = out_tok
    attn_done = s_dve.n

    x_grp = x_ap.rearrange("(c p) t -> p c t", p=128)
    y_grp = y_ap.rearrange("(c p) t -> p c t", p=128)
    xr = Ring(2)
    ph.wait("tensor", s_dve, attn_done)
    for oc in range(KC):
        go, xto = load_wt("Wo", oc * 128)
        ph.wait("tensor", s_xb, xto)
        mtok = None
        for sl in range(NSLAB):
            s0 = sl * SLAB
            pb = sl % 2
            if ps01_free[pb] is not None:
                ph.wait("tensor", *ps01_free[pb])
            for kc in range(KC):
                last = kc == KC - 1
                mtok = ph.op("tensor", lambda h, w=wt[go][:, kc, :],
                             r=attn_sb[:, kc, s0:s0 + SLAB], p=PS[pb][:, :],
                             st=(kc == 0), sp=last:
                             h.matmul(p, w, r, start=st, stop=sp),
                             sem=s_mm if last else None)
            xb = xr.acquire(ph, "gpsimd")
            xtok = ph.dma("gpsimd", c["xs"][xb][:, 0, :], x_grp[:, oc, s0:s0 + SLAB], s_x)
            yb = ystg_r.acquire(ph, "vector")
            ph.wait("vector", s_mm, mtok)
            ph.wait("vector", s_x, xtok)
            vtok = ph.op("vector", lambda h, o=ystg[yb][:, :], p=PS[pb][:, :],
                         s1=bcols[:, 2, oc:oc + 1], xc=c["xs"][xb][:, 0, :]:
                         h.scalar_tensor_tensor(o, p, s1, xc, OP.add, OP.add), sem=s_dve)
            ps01_free[pb] = (s_dve, vtok)
            xr.release(xb, s_dve, vtok)
            ph.wait("gpsimd", s_dve, vtok)
            stok = ph.dma("gpsimd", y_grp[:, oc, s0:s0 + SLAB], ystg[yb][:, :], s_st)
            ystg_r.release(yb, s_st, stok)
        wt_r.release(go, s_mm, mtok)
    ph.wait("sync", s_st, s_st.n)
    ph.emit()
    return s_st.n



def emit_l0attn(ctx, c, pfx, x_ap, y_ap, wqc_ap, wqu_ap, wk_ap, wv_ap,
                qnw_ap, knw_ap, sink_ap, w1_ap, w2_ap, cos_ap, sin_ap,
                mdiag_ap, medge_ap, h_sb, attn_half, dbg=None):
    nc = ctx.nc
    PS = c["PS"]
    NH = 16
    WIN_T = 4
    INV_S = 1.0 / float(np.sqrt(128.0))
    EPS_RMS = 1e-6

    s_w = ctx.sem(pfx + "w")
    s_mm = ctx.sem(pfx + "mm")
    s_act = ctx.sem(pfx + "act")
    s_dve = ctx.sem(pfx + "dve")
    s_acc = ctx.sem(pfx + "acc")

    qc_sb = ctx.sbuf(pfx + "qc", [128, 4, T], BF16)
    wkv_t = ctx.sbuf(pfx + "wkv", [128, 16, 256], BF16)
    wqu_t = [ctx.sbuf(pfx + f"wqu{i}", [128, 4, 128], BF16) for i in range(2)]
    kfm = ctx.sbuf(pfx + "kfm", [128, T], BF16)
    qfm = ctx.sbuf(pfx + "qfm", [128, T], BF16)
    vtm = ctx.sbuf(pfx + "vtm", [128, 16, 128], BF16)
    kraw = ctx.sbuf(pfx + "kraw", [128, 512], F32)
    cosd = ctx.sbuf(pfx + "cosd", [64, T], BF16)
    sind = ctx.sbuf(pfx + "sind", [64, T], BF16)
    mdiag = ctx.sbuf(pfx + "mdiag", [128, 128], BF16)
    medge = ctx.sbuf(pfx + "medge", [128, 128], BF16)
    sinkr = ctx.sbuf(pfx + "sink", [1, 16], F32)
    ncol = ctx.sbuf(pfx + "ncol", [128, 2], F32)
    epsr = ctx.sbuf(pfx + "epsr", [1, 1], F32)
    sqb = [ctx.sbuf(pfx + f"sqb{i}", [128, 512], BF16) for i in range(2)]
    negmb = [ctx.sbuf(pfx + f"negmb{i}", [128, 512], BF16) for i in range(2)]
    rbt = [ctx.sbuf(pfx + f"rb{i}", [128, 128], F32) for i in range(2)]
    ptile = [ctx.sbuf(pfx + f"p{i}", [128, 128], BF16) for i in range(3)]
    stile = [ctx.sbuf(pfx + f"s{i}", [128, 128], F32) for i in range(2)]
    w2t = [ctx.sbuf(pfx + f"w2t{i}", [128, 16, 128], BF16) for i in range(2)]
    outstg = [ctx.sbuf(pfx + f"outstg{i}", [128, 1024], F32) for i in range(2)]

    ph = Ph(nc)
    ytok = ph.dma("sync", y_ap[:, :], x_ap[:, :], s_acc)
    ph.wait("gpsimd", c["init_sem"], c["init_tok"])
    ph.dma("gpsimd", cosd[0:32, :], cos_ap[:, :], s_w)
    ph.dma("gpsimd", cosd[32:64, :], cos_ap[:, :], s_w)
    ph.dma("gpsimd", sind[0:32, :], sin_ap[:, :], s_w)
    ph.dma("gpsimd", sind[32:64, :], sin_ap[:, :], s_w)
    ph.dma("gpsimd", mdiag[:, :], mdiag_ap[:, :], s_w)
    ph.dma("gpsimd", medge[:, :], medge_ap[:, :], s_w)
    ph.dma("gpsimd", sinkr[0:1, :], sink_ap.rearrange("(a b) -> a b", a=1), s_w)
    ph.op("gpsimd", lambda h: h.memset(epsr[:, :], EPS_RMS))
    for i, (nap, scl) in enumerate(((qnw_ap, INV_S), (knw_ap, None))):
        btok = ph.dma("gpsimd", c["sq"][0][0:1, 0:128],
                      nap.rearrange("(a b) -> a b", a=1), s_w)
        ph.wait("tensor", s_w, btok)
        ph.wait("tensor", c["init_sem"], c["init_tok"])
        mtok = ph.op("tensor", lambda h: h.transpose(PS[7][0:128, 0:1], c["sq"][0][0:1, 0:128],
                                                     c["ident"][0:1, 0:1]), sem=s_mm)
        ph.wait("vector", s_mm, mtok)
        if scl is None:
            vtok = ph.op("vector", lambda h, o=ncol[:, i:i + 1]:
                         h.tensor_copy(o, PS[7][:, 0:1]), sem=s_dve)
        else:
            vtok = ph.op("vector", lambda h, o=ncol[:, i:i + 1], s2=scl:
                         h.tensor_scalar(o, PS[7][:, 0:1], s2, None, OP.mult), sem=s_dve)
        ph.wait("tensor", s_dve, vtok)
        ph.wait("gpsimd", s_dve, vtok)
    ph.emit()
    const_w = s_w.n

    h_done = emit_ln(ctx, c, pfx + "ln_", x_ap, h_sb)
    s_lndve = ctx.sem(pfx + "ln_dve")

    ph = Ph(nc)
    ph.wait("tensor", s_lndve, h_done)
    ph.wait("tensor", s_w, const_w)
    ph.wait("vector", s_w, const_w)
    ph.wait("scalar", s_w, const_w)
    wstg_r = Ring(2)
    sqb_r = Ring(2)
    ps01_free = [None, None]
    ps6_free = [None]
    ps7_free = [None]

    for sl in range(NSLAB):
        s0 = sl * SLAB
        gu_done = None
        for kc in range(KC):
            b1 = wstg_r.acquire(ph, "gpsimd")
            wtok = ph.dma("gpsimd", c["wstg"][b1][:, 0:512],
                          wqc_ap[kc * 128:(kc + 1) * 128, :], s_w)
            ph.wait("tensor", s_w, wtok)
            tok = None
            for oc in range(4):
                tok = ph.op("tensor", lambda h, w=c["wstg"][b1][:, oc * 128:(oc + 1) * 128],
                            r=h_sb[:, kc, s0:s0 + SLAB], p=PS[oc][:, :],
                            st=(kc == 0), sp=(kc == KC - 1):
                            h.matmul(p, w, r, start=st, stop=(kc == KC - 1)),
                            sem=s_mm if oc == 3 else None)
            wstg_r.release(b1, s_mm, tok)
            if kc == KC - 1:
                gu_done = tok
        vtok = None
        for oc in range(4):
            ph.wait("vector", s_mm, gu_done)
            vtok = ph.op("vector", lambda h, o=qc_sb[:, oc, s0:s0 + SLAB], p=PS[oc][:, :]:
                         h.tensor_copy(o, p), sem=s_dve)
        ph.wait("tensor", s_dve, vtok)
    wtok = ph.dma("gpsimd", wkv_t[:, :, 0:128], wk_ap.rearrange("(c p) d -> p c d", p=128), s_w)
    wtok2 = ph.dma("gpsimd", wkv_t[:, :, 128:256], wv_ap.rearrange("(c p) d -> p c d", p=128), s_w)
    ph.wait("tensor", s_w, wtok)
    k_ps_guard = None
    k_sq_tok = None
    for sl in range(NSLAB):
        s0 = sl * SLAB
        if k_ps_guard is not None:
            ph.wait("tensor", s_dve, k_ps_guard)
        mtok = None
        for kc in range(KC):
            last = kc == KC - 1
            mtok = ph.op("tensor", lambda h, w=wkv_t[:, kc, 0:128], r=h_sb[:, kc, s0:s0 + SLAB],
                         p=PS[0][:, :], st=(kc == 0), sp=last:
                         h.matmul(p, w, r, start=st, stop=sp), sem=s_mm if last else None)
        ph.wait("vector", s_mm, mtok)
        if k_sq_tok is not None:
            ph.wait("vector", s_act, k_sq_tok)
        vtok = ph.op("vector", lambda h: h.tensor_copy(kraw[:, :], PS[0][:, :]), sem=s_dve)
        k_ps_guard = vtok
        x1 = kraw[0:32, :]
        x2 = kraw[32:64, :]
        pa0 = c["sq"][0][0:32, :]
        pb0 = c["sq"][1][0:32, :]
        pa1 = c["sq"][0][32:64, :]
        pb1 = c["sq"][1][32:64, :]
        ph.op("vector", lambda h, o=pa0, a=x1, b=cosd[0:32, s0:s0 + SLAB]:
              h.tensor_tensor(o, a, b, OP.mult))
        ph.op("vector", lambda h, o=pb0, a=x2, b=sind[32:64, s0:s0 + SLAB]:
              h.tensor_tensor(o, a, b, OP.mult))
        ph.op("vector", lambda h, o=pa1, a=x1, b=sind[0:32, s0:s0 + SLAB]:
              h.tensor_tensor(o, a, b, OP.mult))
        ph.op("vector", lambda h, o=pb1, a=x2, b=cosd[32:64, s0:s0 + SLAB]:
              h.tensor_tensor(o, a, b, OP.mult))
        ph.op("vector", lambda h, o=kraw[0:32, :], a=pa0, b=pb0:
              h.tensor_tensor(o, a, b, OP.subtract))
        rtok = ph.op("vector", lambda h, o=kraw[32:64, :], a=pa1, b=pb1:
              h.tensor_tensor(o, a, b, OP.add), sem=s_dve)
        sb = sqb_r.acquire(ph, "scalar")
        ph.wait("scalar", s_dve, rtok)
        atok = ph.op("scalar", lambda h, o=sqb[sb][:, :]: h.activation(o, kraw[:, :], AF.Square),
                     sem=s_act)
        k_sq_tok = atok
        ph.wait("tensor", s_act, atok)
        if ps6_free[0] is not None:
            ph.wait("tensor", *ps6_free[0])
        mtok = ph.op("tensor", lambda h, i=sqb[sb][:, :]:
                     h.matmul(PS[6][0:1, :], c["ones_col_bf"][:, :], i, start=True, stop=True),
                     sem=s_mm)
        sqb_r.release(sb, s_mm, mtok)
        ph.wait("vector", s_mm, mtok)
        vtok = ph.op("vector", lambda h: h.tensor_scalar(c["srow"][0:1, :], PS[6][0:1, :],
                                                         1.0 / 128.0, None, OP.mult), sem=s_dve)
        ps6_free[0] = (s_dve, vtok)
        ph.wait("scalar", s_dve, vtok)
        atok = ph.op("scalar", lambda h: h.activation(c["srow2"][0:1, :], c["srow"][0:1, :],
                                                      AF.Sqrt, bias=epsr[0:1, :]), sem=s_act)
        ph.wait("vector", s_act, atok)
        rtok2 = ph.op("vector", lambda h: h.reciprocal(c["srow"][0:1, :], c["srow2"][0:1, :]),
                      sem=s_dve)
        if ps7_free[0] is not None:
            ph.wait("tensor", *ps7_free[0])
        ph.wait("tensor", s_dve, rtok2)
        mtok = ph.op("tensor", lambda h: h.matmul(PS[7][:, :], c["ones_row"][:, :],
                                                  c["srow"][0:1, :], start=True, stop=True),
                     sem=s_mm)
        ph.wait("vector", s_mm, mtok)
        ph.op("vector", lambda h: h.tensor_tensor(c["sq"][0][:, :], kraw[:, :], PS[7][:, :],
                                                  OP.mult))
        vtok = ph.op("vector", lambda h, o=kfm[:, s0:s0 + SLAB]:
                     h.tensor_scalar(o, c["sq"][0][:, :], ncol[:, 1:2], None, OP.mult),
                     sem=s_dve)
        ps7_free[0] = (s_dve, vtok)
    k_done = s_dve.n
    ph.wait("tensor", s_w, wtok2)
    ps01_free[0] = (s_dve, k_ps_guard)
    for tc in range(KC):
        t0c = tc * 128
        pb = tc % 2
        if ps01_free[pb] is not None:
            ph.wait("tensor", *ps01_free[pb])
        mtok = None
        for kc in range(KC):
            last = kc == KC - 1
            mtok = ph.op("tensor", lambda h, w=h_sb[:, kc, t0c:t0c + 128], r=wkv_t[:, kc, 128:256],
                         p=PS[pb][:, 0:128], st=(kc == 0), sp=last:
                         h.matmul(p, w, r, start=st, stop=sp), sem=s_mm if last else None)
        ph.wait("scalar", s_mm, mtok)
        atok = ph.op("scalar", lambda h, o=vtm[:, tc, :], p=PS[pb][:, 0:128]: h.copy(o, p),
                     sem=s_act)
        ps01_free[pb] = (s_act, atok)
    v_done = s_act.n
    for sl in range(NSLAB):
        s0 = sl * SLAB
        sb = sqb_r.acquire(ph, "scalar")
        if sl == 0:
            ph.wait("scalar", s_dve, k_done)
        atok = ph.op("scalar", lambda h, o=sqb[sb][:, :], i=kfm[:, s0:s0 + SLAB]:
                     h.activation(o, i, AF.Square), sem=s_act)
        ph.wait("tensor", s_act, atok)
        if ps6_free[0] is not None:
            ph.wait("tensor", *ps6_free[0])
        mtok = ph.op("tensor", lambda h, i=sqb[sb][:, :]:
                     h.matmul(PS[6][0:1, :], c["ones_col_bf"][:, :], i, start=True, stop=True),
                     sem=s_mm)
        sqb_r.release(sb, s_mm, mtok)
        ph.wait("vector", s_mm, mtok)
        vtok = ph.op("vector", lambda h, o=c["mean"][0:1, s0:s0 + SLAB]:
                     h.tensor_copy(o, PS[6][0:1, :]), sem=s_dve)
        ps6_free[0] = (s_dve, vtok)
    ph.wait("scalar", s_dve, s_dve.n)
    antok = ph.op("scalar", lambda h: h.activation(c["mean"][0:1, :], c["mean"][0:1, :], AF.Sqrt),
                  sem=s_act)
    ph.wait("vector", s_act, antok)
    gtok_ = ph.op("vector", lambda h: h.tensor_reduce(
        c["srow"][0:1, 0:16],
        c["mean"][0:1, :].rearrange("p (a b) -> p a b", b=128), AX.X, OP.max), sem=s_dve)
    ph.wait("vector", s_dve, gtok_)
    ph.gop("vector", lambda h: h.tensor_copy(c["srow"][0:1, 16:32], c["srow"][0:1, 0:16]),
           s_dve)
    for j in range(1, 5):
        ph.gop("vector", lambda h, j=j: h.tensor_tensor(
            c["srow"][0:1, 16 + j:32], c["srow"][0:1, 16 + j:32],
            c["srow"][0:1, 0:16 - j], OP.max), s_dve)
    kwin_tok = ph.gop("vector", lambda h: h.tensor_copy(c["srow2"][0:1, 0:1],
                                                        c["srow"][0:1, 16:17]), s_dve)
    ph.emit()

    prev_out_tok = None
    acc_tok_prev_grp = ytok
    for half in range(2):
        ph = Ph(nc)
        wqu_r = Ring(2)
        negmb_r = Ring(2)
        rb_r = Ring(2)
        p_r = Ring(3)
        s_r = Ring(2)
        ps01_free_h = [None, None]
        ps23_free = [None, None]
        ps45_free = [None, None]
        q_sq_tok = None
        for hh in range(8):
            head = half * 8 + hh
            gq = wqu_r.acquire(ph, "gpsimd")
            wtok = ph.dma("gpsimd", wqu_t[gq][:, :, :],
                          wqu_ap.rearrange("(c p) o -> p c o", p=128)[:, :, head * 128:(head + 1) * 128],
                          s_w)
            ph.wait("tensor", s_w, wtok)
            for sl in range(NSLAB):
                s0 = sl * SLAB
                pb = sl % 2
                if ps01_free_h[pb] is not None:
                    ph.wait("tensor", *ps01_free_h[pb])
                elif ps01_free[pb] is not None:
                    ph.wait("tensor", *ps01_free[pb])
                mtok = None
                for kc in range(4):
                    last = kc == 3
                    mtok = ph.op("tensor", lambda h, w=wqu_t[gq][:, kc, :],
                                 r=qc_sb[:, kc, s0:s0 + SLAB], p=PS[pb][:, :],
                                 st=(kc == 0), sp=last:
                                 h.matmul(p, w, r, start=st, stop=sp),
                                 sem=s_mm if last else None)
                ph.wait("vector", s_mm, mtok)
                if sl == 0 and prev_out_tok is not None:
                    ph.wait("vector", s_mm, prev_out_tok)
                if q_sq_tok is not None:
                    ph.wait("vector", s_act, q_sq_tok)
                vtok = ph.op("vector", lambda h, p=PS[pb][:, :]: h.tensor_copy(kraw[:, :], p),
                             sem=s_dve)
                ps01_free_h[pb] = (s_dve, vtok)
                x1 = kraw[0:32, :]
                x2 = kraw[32:64, :]
                pa0 = c["sq"][0][0:32, :]
                pb0 = c["sq"][1][0:32, :]
                pa1 = c["sq"][0][32:64, :]
                pb1 = c["sq"][1][32:64, :]
                s0sl = s0
                ph.op("vector", lambda h, o=pa0, a=x1, b=cosd[0:32, s0sl:s0sl + SLAB]:
                      h.tensor_tensor(o, a, b, OP.mult))
                ph.op("vector", lambda h, o=pb0, a=x2, b=sind[32:64, s0sl:s0sl + SLAB]:
                      h.tensor_tensor(o, a, b, OP.mult))
                ph.op("vector", lambda h, o=pa1, a=x1, b=sind[0:32, s0sl:s0sl + SLAB]:
                      h.tensor_tensor(o, a, b, OP.mult))
                ph.op("vector", lambda h, o=pb1, a=x2, b=cosd[32:64, s0sl:s0sl + SLAB]:
                      h.tensor_tensor(o, a, b, OP.mult))
                ph.op("vector", lambda h, o=kraw[0:32, :], a=pa0, b=pb0:
                      h.tensor_tensor(o, a, b, OP.subtract))
                rtok = ph.op("vector", lambda h, o=kraw[32:64, :], a=pa1, b=pb1:
                             h.tensor_tensor(o, a, b, OP.add), sem=s_dve)
                sb = sqb_r.acquire(ph, "scalar")
                ph.wait("scalar", s_dve, rtok)
                atok = ph.op("scalar", lambda h, o=sqb[sb][:, :]:
                             h.activation(o, kraw[:, :], AF.Square), sem=s_act)
                q_sq_tok = atok
                ph.wait("tensor", s_act, atok)
                if ps6_free[0] is not None:
                    ph.wait("tensor", *ps6_free[0])
                mtok = ph.op("tensor", lambda h, i=sqb[sb][:, :]:
                             h.matmul(PS[6][0:1, :], c["ones_col_bf"][:, :], i,
                                      start=True, stop=True), sem=s_mm)
                sqb_r.release(sb, s_mm, mtok)
                ph.wait("vector", s_mm, mtok)
                vtok = ph.op("vector", lambda h: h.tensor_scalar(c["srow2"][0:1, :], PS[6][0:1, :],
                                                                 1.0 / 128.0, None, OP.mult),
                             sem=s_dve)
                ps6_free[0] = (s_dve, vtok)
                ph.wait("scalar", s_dve, vtok)
                atok = ph.op("scalar", lambda h: h.activation(c["srow2"][0:1, :], c["srow2"][0:1, :],
                                                              AF.Sqrt, bias=epsr[0:1, :]), sem=s_act)
                ph.wait("vector", s_act, atok)
                rtok2 = ph.op("vector", lambda h: h.reciprocal(c["srow2"][0:1, :], c["srow2"][0:1, :]),
                              sem=s_dve)
                if ps7_free[0] is not None:
                    ph.wait("tensor", *ps7_free[0])
                ph.wait("tensor", s_dve, rtok2)
                mtok = ph.op("tensor", lambda h: h.matmul(PS[7][:, :], c["ones_row"][:, :],
                                                          c["srow2"][0:1, :], start=True, stop=True),
                             sem=s_mm)
                ph.wait("vector", s_mm, mtok)
                ph.op("vector", lambda h: h.tensor_tensor(c["sq"][0][:, :], kraw[:, :],
                                                          PS[7][:, :], OP.mult))
                vtok = ph.op("vector", lambda h, o=qfm[:, s0sl:s0sl + SLAB]:
                             h.tensor_scalar(o, c["sq"][0][:, :], ncol[:, 0:1], None, OP.mult),
                             sem=s_dve)
                ps7_free[0] = (s_dve, vtok)
            q_done = s_dve.n
            wqu_r.release(gq, s_dve, q_done)
            for sl in range(NSLAB):
                s0 = sl * SLAB
                sb = sqb_r.acquire(ph, "scalar")
                if sl == 0:
                    ph.wait("scalar", s_dve, q_done)
                atok = ph.op("scalar", lambda h, o=sqb[sb][:, :], i=qfm[:, s0:s0 + SLAB]:
                             h.activation(o, i, AF.Square), sem=s_act)
                ph.wait("tensor", s_act, atok)
                if ps6_free[0] is not None:
                    ph.wait("tensor", *ps6_free[0])
                mtok = ph.op("tensor", lambda h, i=sqb[sb][:, :]:
                             h.matmul(PS[6][0:1, :], c["ones_col_bf"][:, :], i,
                                      start=True, stop=True), sem=s_mm)
                sqb_r.release(sb, s_mm, mtok)
                ph.wait("vector", s_mm, mtok)
                vtok = ph.op("vector", lambda h, o=c["mean"][0:1, s0:s0 + SLAB]:
                             h.tensor_copy(o, PS[6][0:1, :]), sem=s_dve)
                ps6_free[0] = (s_dve, vtok)
            ph.wait("scalar", s_dve, s_dve.n)
            qn_tok = ph.op("scalar", lambda h: h.activation(c["mean"][0:1, :], c["mean"][0:1, :],
                                                            AF.Sqrt), sem=s_act)
            ph.wait("vector", s_act, qn_tok)
            negm_tok = None
            for qt in range(16):
                negm_tok = ph.op("vector", lambda h, qt=qt: h.tensor_scalar(
                    c["rstd"][0:1, qt * 128:(qt + 1) * 128],
                    c["mean"][0:1, qt * 128:(qt + 1) * 128],
                    c["srow"][0:1, 16 + qt:17 + qt], -1.0, OP.mult, OP.mult),
                    sem=s_dve if qt == 15 else None)
            negmb_cur = None
            negmb_tok = None
            last_sub_tok = None
            for qt in range(16):
                q0c = qt * 128
                slb = qt // 4
                if qt % 4 == 0:
                    if negmb_cur is not None:
                        negmb_r.release(negmb_cur, s_dve, last_sub_tok)
                    nb = negmb_r.acquire(ph, "tensor")
                    if ps7_free[0] is not None:
                        ph.wait("tensor", *ps7_free[0])
                    ph.wait("tensor", s_dve, negm_tok)
                    mtok = ph.op("tensor", lambda h, i=c["rstd"][0:1, slb * 512:(slb + 1) * 512]:
                                 h.matmul(PS[7][:, :], c["ones_row"][:, :], i,
                                          start=True, stop=True), sem=s_mm)
                    ph.wait("scalar", s_mm, mtok)
                    btok = ph.op("scalar", lambda h, o=negmb[nb][:, :]:
                                 h.copy(o, PS[7][:, :]), sem=s_act)
                    ps7_free[0] = (s_act, btok)
                    negmb_tok = btok
                    negmb_cur = nb
                if ps6_free[0] is not None:
                    ph.wait("tensor", *ps6_free[0])
                pout = qt % 2
                if ps45_free[pout] is not None:
                    ph.wait("tensor", *ps45_free[pout])
                st0 = max(0, qt - WIN_T)
                out_tok = None
                dn_tok = None
                for st in range(st0, qt + 1):
                    psc = s_r.acquire(ph, "vector")
                    pb23 = st % 2
                    if ps23_free[pb23] is not None:
                        ph.wait("tensor", *ps23_free[pb23])
                    if head == 0 and qt == 0:
                        ph.wait("tensor", s_dve, kwin_tok)
                    mtok = ph.op("tensor", lambda h, w=kfm[:, st * 128:(st + 1) * 128],
                                 r=qfm[:, q0c:q0c + 128], p=PS[2 + pb23][:, 0:128]:
                                 h.matmul(p, w, r, start=True, stop=True), sem=s_mm)
                    ph.wait("vector", s_mm, mtok)
                    if st == st0:
                        ph.wait("vector", s_act, negmb_tok)
                    vtok = ph.op("vector", lambda h, o=stile[psc][:, :], p=PS[2 + pb23][:, 0:128],
                                 nm=negmb[negmb_cur][:, (qt % 4) * 128:(qt % 4 + 1) * 128]:
                                 h.tensor_tensor(o, p, nm, OP.add), sem=s_dve)
                    ps23_free[pb23] = (s_dve, vtok)
                    last_sub_tok = vtok
                    msk = None
                    if st == qt:
                        msk = mdiag
                    elif st == qt - WIN_T:
                        msk = medge
                    if msk is not None:
                        vtok = ph.op("vector", lambda h, o=stile[psc][:, :], m=msk:
                                     h.tensor_tensor(o, o, m[:, :], OP.add), sem=s_dve)
                        last_sub_tok = vtok
                    pt = p_r.acquire(ph, "scalar")
                    ph.wait("scalar", s_dve, vtok)
                    atok = ph.op("scalar", lambda h, o=ptile[pt][:, :], i=stile[psc][:, :]:
                                 h.activation(o, i, AF.Exp), sem=s_act)
                    s_r.release(psc, s_act, atok)
                    ph.wait("tensor", s_act, atok)
                    if st == st0 and head == 0 and qt < 2:
                        ph.wait("tensor", s_act, v_done)
                    dn_tok = ph.op("tensor", lambda h, r=ptile[pt][:, :],
                                   st_=(st == st0), sp=(st == qt):
                                   h.matmul(PS[6][0:1, 0:128], c["ones_col_bf"][:, :], r,
                                            start=st_, stop=sp),
                                   sem=s_mm if st == qt else None)
                    out_tok = ph.op("tensor", lambda h, w=vtm[:, st, :], r=ptile[pt][:, :],
                                    p=PS[4 + pout][:, 0:128], st_=(st == st0), sp=(st == qt):
                                    h.matmul(p, w, r, start=st_, stop=sp), sem=s_mm)
                    p_r.release(pt, s_mm, out_tok)
                sk = ph.op("scalar", lambda h, o=c["srow2"][0:1, 128:256],
                           i=c["rstd"][0:1, q0c:q0c + 128], b=sinkr[0:1, head:head + 1]:
                           h.activation(o, i, AF.Exp, bias=b), sem=s_act)
                ph.wait("vector", s_mm, dn_tok)
                ph.wait("vector", s_act, sk)
                vtok = ph.op("vector", lambda h, o=c["mean"][0:1, q0c:q0c + 128]:
                             h.tensor_tensor(o, PS[6][0:1, 0:128], c["srow2"][0:1, 128:256],
                                             OP.add), sem=s_dve)
                ps6_free[0] = (s_dve, vtok)
                rtok = ph.op("vector", lambda h, o=c["srow2"][0:1, 256:384],
                             i=c["mean"][0:1, q0c:q0c + 128]: h.reciprocal(o, i), sem=s_dve)
                rbb = rb_r.acquire(ph, "tensor")
                if ps7_free[0] is not None:
                    ph.wait("tensor", *ps7_free[0])
                ph.wait("tensor", s_dve, rtok)
                mtok = ph.op("tensor", lambda h:
                             h.matmul(PS[7][:, 0:128], c["ones_row"][:, :],
                                      c["srow2"][0:1, 256:384], start=True, stop=True), sem=s_mm)
                ph.wait("scalar", s_mm, mtok)
                btok = ph.op("scalar", lambda h, o=rbt[rbb][:, :]: h.copy(o, PS[7][:, 0:128]),
                             sem=s_act)
                ps7_free[0] = (s_act, btok)
                ph.wait("vector", s_act, btok)
                ph.wait("vector", s_mm, out_tok)
                ntok = ph.op("vector", lambda h, o=attn_half[:, hh, q0c:q0c + 128],
                             p=PS[4 + pout][:, 0:128], r=rbt[rbb][:, :]:
                             h.tensor_tensor(o, p, r, OP.mult), sem=s_dve)
                ps45_free[pout] = (s_dve, ntok)
                rb_r.release(rbb, s_dve, ntok)
                prev_out_tok = out_tok
        attn_half_done = s_dve.n

        ph.wait("tensor", s_dve, attn_half_done)
        w2_r = Ring(2)
        out_r = Ring(2)
        psd_free = list(ps01_free_h)
        for gg in range(2):
            g = half * 2 + gg
            for oc in range(KC):
                b1 = wstg_r.acquire(ph, "gpsimd")
                wtok = ph.dma("gpsimd", c["wstg"][b1][:, 0:512],
                              w1_ap[g, :, :].rearrange("(c p) o -> p c o", p=128)
                              [:, :, oc * 128:(oc + 1) * 128], s_w)
                ph.wait("tensor", s_w, wtok)
                for sl in range(NSLAB):
                    s0 = sl * SLAB
                    pb = sl % 2
                    if psd_free[pb] is not None:
                        ph.wait("tensor", *psd_free[pb])
                    mtok = None
                    for kc in range(4):
                        last = kc == 3
                        mtok = ph.op("tensor", lambda h, w=c["wstg"][b1][:, kc * 128:(kc + 1) * 128],
                                     r=attn_half[:, gg * 4 + kc, s0:s0 + SLAB], p=PS[pb][:, :],
                                     st=(kc == 0), sp=last:
                                     h.matmul(p, w, r, start=st, stop=sp),
                                     sem=s_mm if last else None)
                    ph.wait("vector", s_mm, mtok)
                    vtok = ph.op("vector", lambda h, o=h_sb[:, oc, s0:s0 + SLAB], p=PS[pb][:, :]:
                                 h.tensor_copy(o, p), sem=s_dve)
                    psd_free[pb] = (s_dve, vtok)
                wstg_r.release(b1, s_mm, mtok)
            tmp_done = s_dve.n
            ph.wait("tensor", s_dve, tmp_done)
            for oc in range(KC):
                gw = w2_r.acquire(ph, "gpsimd")
                wtok = ph.dma("gpsimd", w2t[gw][:, :, :],
                              w2_ap[g * 2048:(g + 1) * 2048, oc * 128:(oc + 1) * 128]
                              .rearrange("(c p) o -> p c o", p=128), s_w)
                ph.wait("tensor", s_w, wtok)
                mtok = etok = None
                for hf in range(2):
                    ob = out_r.acquire(ph, "scalar")
                    for hs in range(2):
                        sl = hf * 2 + hs
                        s0 = sl * SLAB
                        pb = sl % 2
                        if psd_free[pb] is not None:
                            ph.wait("tensor", *psd_free[pb])
                        for kc in range(KC):
                            last = kc == KC - 1
                            mtok = ph.op("tensor", lambda h, w=w2t[gw][:, kc, :],
                                         r=h_sb[:, kc, s0:s0 + SLAB], p=PS[pb][:, :],
                                         st=(kc == 0), sp=last:
                                         h.matmul(p, w, r, start=st, stop=sp),
                                         sem=s_mm if last else None)
                        ph.wait("scalar", s_mm, mtok)
                        etok = ph.op("scalar", lambda h, o=outstg[ob][:, hs * 512:(hs + 1) * 512],
                                     p=PS[pb][:, :]: h.copy(o, p), sem=s_act)
                        psd_free[pb] = (s_act, etok)
                    ph.wait("gpsimd", s_act, etok)
                    ph.wait("gpsimd", s_acc, acc_tok_prev_grp)
                    acctok = ph.dma("gpsimd",
                                    y_ap[oc * 128:(oc + 1) * 128, hf * 1024:(hf + 1) * 1024],
                                    outstg[ob][:, :], s_acc, accum_op=OP.add)
                    out_r.release(ob, s_acc, acctok)
                w2_r.release(gw, s_mm, mtok)
            acc_tok_prev_grp = s_acc.n
        ph.emit()
    ph = Ph(nc)
    ph.wait("sync", s_acc, s_acc.n)
    if dbg is not None:
        s_st = ctx.sem(pfx + "dbgst")
        ph.wait("scalar", s_acc, s_acc.n)
        for name, src_t in (("qc", qc_sb), ("kfm", kfm), ("vtm", vtm), ("qfm", qfm),
                            ("ncol", ncol), ("attnh", attn_half)):
            if name in dbg:
                ph.dma("scalar", dbg[name], src_t[tuple([slice(None)] * len(src_t.shape))]
                       if hasattr(src_t, "shape") else src_t[:, :], s_st)
        ph.wait("scalar", s_st, s_st.n)
    ph.emit()
    return s_acc.n


def l1_rope_tables():
    d = 128
    j = np.arange(d // 2, dtype=np.float32)
    thetas = np.float32(10000.0) ** (np.float32(-2.0) * j / np.float32(d))
    m = np.arange(1, T + 1, dtype=np.float32)
    ang = m[None, :] * thetas[:, None]
    return np.cos(ang).astype(np.float32), np.sin(ang).astype(np.float32)


def causal_mask_tile():
    s = np.arange(128)[:, None]
    q = np.arange(128)[None, :]
    return np.where(s <= q, 0.0, -1e30).astype(np.float32)


def build_l1attn_program():
    nc = bass.Bass("TRN2", target_bir_lowering=False)
    x = nc.dram_tensor("x", [D, T], F32, kind="ExternalInput").ap()
    w_aps = {n: nc.dram_tensor(n, [D, D], F32, kind="ExternalInput").ap()
             for n in ("Wq", "Wk", "Wv", "Wo")}
    b_aps = {n: nc.dram_tensor(n, [D], F32, kind="ExternalInput").ap()
             for n in ("bq", "bk", "bv", "bo")}
    cos = nc.dram_tensor("rope_cos", [64, T], F32, kind="ExternalInput").ap()
    sin = nc.dram_tensor("rope_sin", [64, T], F32, kind="ExternalInput").ap()
    cm = nc.dram_tensor("cmask", [128, 128], F32, kind="ExternalInput").ap()
    y = nc.dram_tensor("y", [D, T], F32, kind="ExternalOutput").ap()
    with ExitStack() as stack:
        ctx = Ctx(nc, stack)
        c = make_common(ctx)
        h_sb = ctx.sbuf("h", [128, KC, T], BF16)
        attn_sb = ctx.sbuf("attn", [128, 16, T], BF16)
        emit_l1attn(ctx, c, "a1_", x, y, w_aps, b_aps, cos, sin, cm, h_sb, attn_sb)
    return nc


def emit_l0ffn(ctx, c, pfx, x_ap, y_ap, gate_ap, w1_ap, w2_ap, sw1_ap, sw2_ap,
               sw3_ap, h_dram_ap, cw_dram_ap):
    nc = ctx.nc
    PS = c["PS"]
    NE = 8
    NIC = 11
    NSL = 2
    SL = 1024

    s_w = ctx.sem(pfx + "w")
    s_xb = ctx.sem(pfx + "xb")
    s_mm = ctx.sem(pfx + "mm")
    s_act = ctx.sem(pfx + "act")
    s_dve = ctx.sem(pfx + "dve")
    s_acc = ctx.sem(pfx + "acc")
    s_h = ctx.sem(pfx + "h")

    hslab = [ctx.sbuf(pfx + f"hslab{i}", [128, KC, SL], BF16) for i in range(2)]
    act_sb = ctx.sbuf(pfx + "act", [128, NIC, SL], BF16)
    w1g = [ctx.sbuf(pfx + f"w1g{i}", [128, 16, 128], BF16) for i in range(2)]
    w1u = [ctx.sbuf(pfx + f"w1u{i}", [128, 16, 128], BF16) for i in range(2)]
    w2t = [ctx.sbuf(pfx + f"w2t{i}", [128, 11, 128], BF16) for i in range(2)]
    sil = [ctx.sbuf(pfx + f"sil{i}", [128, 512], F32) for i in range(2)]
    outstg = [ctx.sbuf(pfx + f"outstg{i}", [128, 1024], F32) for i in range(2)]
    gT = ctx.sbuf(pfx + "gT", [128, 16, 8], F32)
    lg = ctx.sbuf(pfx + "lg", [128, 8], F32)
    r8a = ctx.sbuf(pfx + "r8a", [128, 8], F32)
    r8b = ctx.sbuf(pfx + "r8b", [128, 8], F32)
    col1 = ctx.sbuf(pfx + "col1", [128, 1], F32)
    col2 = ctx.sbuf(pfx + "col2", [128, 1], F32)
    col3 = ctx.sbuf(pfx + "col3", [128, 1], F32)
    cwpad = ctx.sbuf(pfx + "cwpad", [128, 128], F32)
    cwstage = ctx.sbuf(pfx + "cwstage", [1, 1024], BF16)
    cwrow = ctx.sbuf(pfx + "cwrow", [1, T], BF16)
    hstg = [ctx.sbuf(pfx + f"hstg{i}", [128, 512], BF16) for i in range(2)]

    ph = Ph(nc)
    ytok = ph.dma("sync", y_ap[:, :], x_ap[:, :], s_acc)
    ph.emit()

    ph = Ph(nc)
    ph.wait("gpsimd", c["init_sem"], c["init_tok"])
    gstg = c["sq"][0]
    prev = None
    for kc in range(KC):
        wtok = ph.dma("gpsimd", gstg[0:8, 0:128], gate_ap[:, kc * 128:(kc + 1) * 128], s_w)
        ph.wait("tensor", s_w, wtok)
        ph.wait("tensor", c["init_sem"], c["init_tok"])
        if prev is not None:
            ph.wait("tensor", s_dve, prev)
        mtok = ph.op("tensor", lambda h: h.transpose(PS[7][0:128, 0:8], gstg[0:8, 0:128],
                                                     c["ident"][0:8, 0:8]), sem=s_mm)
        ph.wait("vector", s_mm, mtok)
        prev = ph.op("vector", lambda h, o=gT[:, kc, :]: h.tensor_copy(o, PS[7][0:128, 0:8]),
                     sem=s_dve)
        ph.wait("gpsimd", s_dve, prev)
    ph.emit()
    gT_done = prev

    s_lnx = ctx.sem(pfx + "lnx")
    s_lnmm = ctx.sem(pfx + "lnmm")
    s_lnact = ctx.sem(pfx + "lnact")
    s_lndve = ctx.sem(pfx + "lndve")
    xr = Ring(2)
    sqr = Ring(2)
    x_grp = x_ap.rearrange("(c p) t -> p c t", p=128)
    h_grp = h_dram_ap.rearrange("(s c p) t -> p s c t", p=128, c=KC)

    ph = Ph(nc)
    ph.wait("tensor", c["init_sem"], c["init_tok"])
    stat_free_tok = None
    for sl in range(NSLAB):
        s0 = sl * SLAB
        if stat_free_tok is not None:
            ph.wait("tensor", s_lndve, stat_free_tok)
        for kc in range(KC):
            b = xr.acquire(ph, "gpsimd")
            xtok = ph.dma("gpsimd", c["xs"][b][:, 0, :], x_grp[:, kc, s0:s0 + SLAB], s_lnx)
            sb = sqr.acquire(ph, "scalar")
            ph.wait("scalar", s_lnx, xtok)
            atok = ph.op("scalar", lambda h, o=c["sq"][sb][:, :], i=c["xs"][b][:, 0, :]:
                         h.activation(o, i, AF.Square), sem=s_lnact)
            ph.wait("tensor", s_lnx, xtok)
            ph.op("tensor", lambda h, i=c["xs"][b][:, 0, :], st=(kc == 0), sp=(kc == KC - 1):
                  h.matmul(PS[0][0:1, :], c["ones_col"][:, :], i, start=st, stop=sp))
            ph.wait("tensor", s_lnact, atok)
            mtok = ph.op("tensor", lambda h, i=c["sq"][sb][:, :], st=(kc == 0), sp=(kc == KC - 1):
                         h.matmul(PS[1][0:1, :], c["ones_col"][:, :], i, start=st, stop=sp),
                         sem=s_lnmm)
            sqr.release(sb, s_lnmm, mtok)
            xr.release(b, s_lnmm, mtok)
        sums_done = mtok
        ph.wait("vector", s_lnmm, sums_done)
        ph.op("vector", lambda h, o=c["mean"][0:1, s0:s0 + SLAB]:
              h.tensor_scalar(o, PS[0][0:1, :], 1.0 / D, None, OP.mult))
        ph.op("vector", lambda h:
              h.tensor_scalar(c["srow"][0:1, :], PS[1][0:1, :], 1.0 / D, None, OP.mult))
        ph.op("vector", lambda h, m=c["mean"][0:1, s0:s0 + SLAB]:
              h.tensor_tensor(c["srow2"][0:1, :], m, m, OP.mult))
        vtok = ph.op("vector", lambda h:
                     h.tensor_tensor(c["srow"][0:1, :], c["srow"][0:1, :],
                                     c["srow2"][0:1, :], OP.subtract), sem=s_lndve)
        stat_free_tok = vtok
        ph.wait("scalar", s_lndve, vtok)
        atok = ph.op("scalar", lambda h:
                     h.activation(c["srow2"][0:1, :], c["srow"][0:1, :], AF.Sqrt,
                                  bias=c["eps_ln"][0:1, :]), sem=s_lnact)
        ph.wait("vector", s_lnact, atok)
        ph.op("vector", lambda h, o=c["rstd"][0:1, s0:s0 + SLAB]:
              h.reciprocal(o, c["srow2"][0:1, :]), sem=s_lndve)
    rstd_tok = s_lndve.n
    bc_free = rstd_tok
    ps7g = (s_dve, gT_done)
    ps6g = None
    ph.wait("tensor", s_dve, gT_done)
    for sl in range(NSLAB):
        s0 = sl * SLAB
        ph.wait("tensor", s_lndve, bc_free)
        ph.op("tensor", lambda h, i=c["mean"][0:1, s0:s0 + SLAB]:
              h.matmul(PS[2][:, :], c["ones_row"][:, :], i, start=True, stop=True))
        btok = ph.op("tensor", lambda h, i=c["rstd"][0:1, s0:s0 + SLAB]:
                     h.matmul(PS[3][:, :], c["ones_row"][:, :], i, start=True, stop=True),
                     sem=s_lnmm)
        ph.wait("vector", s_lnmm, btok)
        htoks = []
        for kc in range(KC):
            b = xr.acquire(ph, "gpsimd")
            xtok = ph.dma("gpsimd", c["xs"][b][:, 0, :], x_grp[:, kc, s0:s0 + SLAB], s_lnx)
            hb = kc % 2
            if len(htoks) >= 2:
                ph.wait("vector", s_h, htoks[-2][0])
            ph.wait("vector", s_lnx, xtok)
            if len(htoks) >= 1:
                ph.wait("vector", s_mm, htoks[-1][1])
            ph.op("vector", lambda h, o=c["sq"][0][:, :], i=c["xs"][b][:, 0, :]:
                  h.tensor_tensor(o, i, PS[2][:, :], OP.subtract))
            ph.op("vector", lambda h: h.tensor_tensor(c["sq"][1][:, :], c["sq"][0][:, :],
                                                      PS[3][:, :], OP.mult))
            dtok = ph.op("vector", lambda h, o=hstg[hb][:, :]:
                         h.tensor_copy(o, c["sq"][1][:, :]), sem=s_lndve)
            xr.release(b, s_lndve, dtok)
            ph.wait("gpsimd", s_lndve, dtok)
            htok = ph.dma("gpsimd", h_grp[:, sl // 2, kc, (sl % 2) * 512:(sl % 2) * 512 + SLAB],
                          hstg[hb][:, :], s_h)
            ph.wait("tensor", s_lndve, dtok)
            if ps6g is not None and kc == 0:
                ph.wait("tensor", *ps6g)
            ltok = None
            for tc4 in range(4):
                ltok = ph.op("tensor", lambda h, w=c["sq"][1][:, tc4 * 128:(tc4 + 1) * 128],
                             r=gT[:, kc, :], p=PS[6][:, tc4 * 8:(tc4 + 1) * 8],
                             st=(kc == 0 and tc4 == 0), sp=(kc == KC - 1):
                             h.matmul(p, w, r, start=st, stop=sp),
                             sem=s_mm)
            htoks.append((htok, ltok))
        lg_done = htoks[-1][1]
        for tc4 in range(4):
            tglob = sl * 4 + tc4
            ph.wait("vector", s_mm, lg_done)
            ph.gop("vector", lambda h, p=PS[6][:, tc4 * 8:(tc4 + 1) * 8]:
                   h.tensor_copy(lg[:, :], p), s_dve)
            ph.gop("vector", lambda h: h.reduce_max(col1[:, :], lg[:, :], axis=AX.X), s_dve)
            ph.op("vector", lambda h: h.tensor_scalar(r8a[:, :], lg[:, :], col1[:, :],
                                                      None, OP.subtract), sem=s_dve)
            ph.wait("scalar", s_dve, s_dve.n)
            etok = ph.op("scalar", lambda h: h.activation(r8b[:, :], r8a[:, :], AF.Exp),
                         sem=s_act)
            ph.wait("vector", s_act, etok)
            ph.gop("vector", lambda h: h.reduce_sum(col1[:, :], r8b[:, :], axis=AX.X), s_dve)
            ph.gop("vector", lambda h: h.reciprocal(col2[:, :], col1[:, :]), s_dve)
            ph.gop("vector", lambda h: h.tensor_scalar(lg[:, :], r8b[:, :], col2[:, :],
                                                       None, OP.mult), s_dve)
            ph.gop("vector", lambda h: h.tensor_tensor(r8b[:, 0:4], lg[:, 0:4], lg[:, 4:8],
                                                       OP.max), s_dve)
            ph.gop("vector", lambda h: h.tensor_tensor(r8b[:, 4:6], r8b[:, 0:2], r8b[:, 2:4],
                                                       OP.max), s_dve)
            ph.gop("vector", lambda h: h.tensor_tensor(col1[:, :], r8b[:, 4:5], r8b[:, 5:6],
                                                       OP.max), s_dve)
            ph.gop("vector", lambda h: h.tensor_scalar(r8a[:, :], lg[:, :], col1[:, :],
                                                       None, OP.is_ge), s_dve)
            ph.gop("vector", lambda h: h.scalar_tensor_tensor(r8b[:, :], r8a[:, :], -1e9,
                                                              lg[:, :], OP.mult, OP.add), s_dve)
            ph.gop("vector", lambda h: h.tensor_tensor(r8a[:, 0:4], r8b[:, 0:4], r8b[:, 4:8],
                                                       OP.max), s_dve)
            ph.gop("vector", lambda h: h.tensor_tensor(r8a[:, 4:6], r8a[:, 0:2], r8a[:, 2:4],
                                                       OP.max), s_dve)
            ph.gop("vector", lambda h: h.tensor_tensor(col2[:, :], r8a[:, 4:5], r8a[:, 5:6],
                                                       OP.max), s_dve)
            ph.gop("vector", lambda h: h.tensor_tensor(col3[:, :], col1[:, :], col2[:, :],
                                                       OP.add), s_dve)
            ph.gop("vector", lambda h: h.tensor_scalar(col1[:, :], col3[:, :], 1e-20, None,
                                                       OP.add), s_dve)
            ph.gop("vector", lambda h: h.reciprocal(col3[:, :], col1[:, :]), s_dve)
            ph.gop("vector", lambda h: h.tensor_scalar(r8a[:, :], lg[:, :], col2[:, :],
                                                       None, OP.is_ge), s_dve)
            ph.gop("vector", lambda h: h.tensor_tensor(r8b[:, :], lg[:, :], r8a[:, :],
                                                       OP.mult), s_dve)
            cwtok = ph.gop("vector", lambda h: h.tensor_scalar(r8a[:, :], r8b[:, :], col3[:, :],
                                                               None, OP.mult), s_dve)
            ps6g = (s_dve, cwtok)
            for j in range(4):
                ph.wait("vector", s_mm, s_mm.n)
                ph.gop("vector", lambda h, e0=2 * j: h.tensor_copy(cwpad[:, 0:1],
                                                                   r8a[:, e0:e0 + 1]), s_dve)
                ctok = ph.gop("vector", lambda h, e1=2 * j + 1: h.tensor_copy(
                    cwpad[:, 64:65], r8a[:, e1:e1 + 1]), s_dve)
                ph.wait("tensor", s_dve, ctok)
                if ps7g is not None:
                    ph.wait("tensor", *ps7g)
                mtok = ph.op("tensor", lambda h: h.transpose(PS[7][:, 0:128], cwpad[:, :],
                                                             c["ident"][:, :]), sem=s_mm)
                ph.wait("vector", s_mm, mtok)
                ph.op("vector", lambda h, o=cwstage[0:1, (2 * j) * 128:(2 * j + 1) * 128]:
                      h.tensor_copy(o, PS[7][0:1, 0:128]))
                p7tok = ph.op("vector", lambda h, o=cwstage[0:1, (2 * j + 1) * 128:(2 * j + 2) * 128]:
                              h.tensor_copy(o, PS[7][64:65, 0:128]), sem=s_dve)
                ps7g = (s_dve, p7tok)
            ph.wait("gpsimd", s_dve, p7tok)
            ph.dma("gpsimd", cw_dram_ap[:, tglob * 128:(tglob + 1) * 128],
                   cwstage[0:1, :], s_w)
        bc_free = s_lndve.n
    ln_all = s_h.n
    cw_w_tok = s_w.n
    ph.emit()

    ph = Ph(nc)
    ph.wait("gpsimd", s_h, ln_all)
    hr = Ring(2)
    w1g_r = Ring(2)
    w1u_r = Ring(2)
    w2_r = Ring(2)
    sil_r = Ring(2)
    out_r = Ring(2)
    wstg_r = Ring(2)
    ps_gu_free = [None] * 4
    ps_d_free = [None, None]
    ps_cw_free = [None, None]
    prev_e_acc = ytok
    prev_cwb_tok = None

    for ei in range(NE + 1):
        shared = ei == NE
        for sl in range(NSL):
            hb = hr.acquire(ph, "gpsimd")
            htok = ph.dma("gpsimd", hslab[hb][:, :, :], h_grp[:, sl, :, :], s_h)
            if not shared:
                if sl == 0:
                    if prev_cwb_tok is not None:
                        ph.wait("gpsimd", s_mm, prev_cwb_tok)
                    cwtok_l = ph.dma("gpsimd", cwrow[0:1, :],
                                     cw_dram_ap[ei:ei + 1, :], s_w)
                for hf in range(2):
                    pscw = PS[6 + hf]
                    if ps_cw_free[hf] is not None:
                        ph.wait("tensor", *ps_cw_free[hf])
                    if sl == 0 and hf == 0:
                        ph.wait("tensor", s_w, cwtok_l)
                        if ei == 0:
                            ph.wait("tensor", s_w, cw_w_tok)
                    btok = ph.op("tensor", lambda h, p=pscw[:, :],
                                 i=cwrow[0:1, sl * SL + hf * 512:sl * SL + (hf + 1) * 512]:
                                 h.matmul(p, c["ones_row_bf"][:, :], i, start=True, stop=True),
                                 sem=s_mm)
                cwb_tok = btok
                prev_cwb_tok = btok
            act_done = None
            for ic in range(NIC):
                if shared:
                    b1 = wstg_r.acquire(ph, "gpsimd")
                    wtok = ph.dma("gpsimd", c["wstg"][b1][:, :],
                                  sw1_ap[ic * 128:(ic + 1) * 128, :], s_w)
                    g1 = w1g_r.acquire(ph, "sync")
                    ph.wait("sync", s_w, wtok)
                    xtok1 = ph.dma("sync", w1g[g1][:, :, :], c["wstg"][b1][:, :], s_xb,
                                   transpose=True)
                    wstg_r.release(b1, s_xb, xtok1)
                    b2 = wstg_r.acquire(ph, "gpsimd")
                    wtok = ph.dma("gpsimd", c["wstg"][b2][:, :],
                                  sw2_ap[ic * 128:(ic + 1) * 128, :], s_w)
                    g2 = w1u_r.acquire(ph, "sync")
                    ph.wait("sync", s_w, wtok)
                    xtok2 = ph.dma("sync", w1u[g2][:, :, :], c["wstg"][b2][:, :], s_xb,
                                   transpose=True)
                    wstg_r.release(b2, s_xb, xtok2)
                    ph.wait("tensor", s_xb, xtok2)
                else:
                    g1 = w1g_r.acquire(ph, "gpsimd")
                    wtok1 = ph.dma("gpsimd", w1g[g1][:, :, :],
                                   w1_ap[ei, :, ic * 128:(ic + 1) * 128]
                                   .rearrange("(c p) o -> p c o", p=128), s_w)
                    g2 = w1u_r.acquire(ph, "gpsimd")
                    wtok2 = ph.dma("gpsimd", w1u[g2][:, :, :],
                                   w1_ap[ei, :, 1408 + ic * 128:1408 + (ic + 1) * 128]
                                   .rearrange("(c p) o -> p c o", p=128), s_w)
                    ph.wait("tensor", s_w, wtok2)
                ph.wait("tensor", s_h, htok)
                gu_toks = []
                for hf in range(2):
                    c0 = hf * 512
                    pg = PS[hf]
                    pu = PS[2 + hf]
                    if ps_gu_free[hf] is not None:
                        ph.wait("tensor", *ps_gu_free[hf])
                    gt = None
                    for kc in range(KC):
                        last = kc == KC - 1
                        gt = ph.op("tensor", lambda h, w=w1g[g1][:, kc, :],
                                   r=hslab[hb][:, kc, c0:c0 + 512], p=pg[:, :],
                                   st=(kc == 0), sp=last:
                                   h.matmul(p, w, r, start=st, stop=sp),
                                   sem=s_mm if last else None)
                    if ps_gu_free[2 + hf] is not None:
                        ph.wait("tensor", *ps_gu_free[2 + hf])
                    ut = None
                    for kc in range(KC):
                        last = kc == KC - 1
                        ut = ph.op("tensor", lambda h, w=w1u[g2][:, kc, :],
                                   r=hslab[hb][:, kc, c0:c0 + 512], p=pu[:, :],
                                   st=(kc == 0), sp=last:
                                   h.matmul(p, w, r, start=st, stop=sp),
                                   sem=s_mm if last else None)
                    gu_toks.append((gt, ut))
                w1g_r.release(g1, s_mm, gu_toks[-1][0])
                w1u_r.release(g2, s_mm, gu_toks[-1][1])
                for hf in range(2):
                    c0 = hf * 512
                    gt, ut = gu_toks[hf]
                    sb = sil_r.acquire(ph, "scalar")
                    ph.wait("scalar", s_mm, gt)
                    atok = ph.op("scalar", lambda h, o=sil[sb][:, :], p=PS[hf][:, :]:
                                 h.activation(o, p, AF.Silu), sem=s_act)
                    ps_gu_free[hf] = (s_act, atok)
                    ph.wait("vector", s_act, atok)
                    ph.wait("vector", s_mm, ut)
                    if shared:
                        dtok = ph.op("vector", lambda h, o=act_sb[:, ic, c0:c0 + 512],
                                     i=sil[sb][:, :], p=PS[2 + hf][:, :]:
                                     h.tensor_tensor(o, i, p, OP.mult), sem=s_dve)
                    else:
                        if ic == 0 and hf == 0:
                            ph.wait("vector", s_mm, cwb_tok)
                        ph.op("vector", lambda h, i=sil[sb][:, :], p=PS[2 + hf][:, :]:
                              h.tensor_tensor(c["sq"][0][:, :], i, p, OP.mult))
                        dtok = ph.op("vector", lambda h, o=act_sb[:, ic, c0:c0 + 512],
                                     p=PS[6 + hf][:, :]:
                                     h.tensor_tensor(o, c["sq"][0][:, :], p, OP.mult),
                                     sem=s_dve)
                    ps_gu_free[2 + hf] = (s_dve, dtok)
                    sil_r.release(sb, s_dve, dtok)
                    act_done = dtok
            if not shared:
                ps_cw_free[0] = (s_dve, act_done)
                ps_cw_free[1] = (s_dve, act_done)
            for oc in range(KC):
                gw = w2_r.acquire(ph, "gpsimd")
                if shared:
                    b1 = wstg_r.acquire(ph, "gpsimd")
                    wtok = ph.dma("gpsimd", c["wstg"][b1][:, 0:1408],
                                  sw3_ap[oc * 128:(oc + 1) * 128, :], s_w)
                    ph.wait("sync", s_w, wtok)
                    xtok = ph.dma("sync", w2t[gw][:, :, :], c["wstg"][b1][:, 0:1408], s_xb,
                                  transpose=True)
                    wstg_r.release(b1, s_xb, xtok)
                    ph.wait("tensor", s_xb, xtok)
                else:
                    wtok = ph.dma("gpsimd", w2t[gw][:, :, :],
                                  w2_ap[ei, :, oc * 128:(oc + 1) * 128]
                                  .rearrange("(c p) o -> p c o", p=128), s_w)
                    ph.wait("tensor", s_w, wtok)
                if oc == 0:
                    ph.wait("tensor", s_dve, act_done)
                ob = out_r.acquire(ph, "scalar")
                mtok = etok = None
                for hf in range(2):
                    pd = PS[4 + hf]
                    if ps_d_free[hf] is not None:
                        ph.wait("tensor", *ps_d_free[hf])
                    for ic in range(NIC):
                        last = ic == NIC - 1
                        mtok = ph.op("tensor", lambda h, w=w2t[gw][:, ic, :],
                                     r=act_sb[:, ic, hf * 512:(hf + 1) * 512], p=pd[:, :],
                                     st=(ic == 0), sp=last:
                                     h.matmul(p, w, r, start=st, stop=sp),
                                     sem=s_mm if last else None)
                    ph.wait("scalar", s_mm, mtok)
                    etok = ph.op("scalar", lambda h, o=outstg[ob][:, hf * 512:(hf + 1) * 512],
                                 p=pd[:, :]: h.copy(o, p), sem=s_act)
                    ps_d_free[hf] = (s_act, etok)
                w2_r.release(gw, s_mm, mtok)
                ph.wait("gpsimd", s_act, etok)
                ph.wait("gpsimd", s_acc, prev_e_acc)
                acctok = ph.dma("gpsimd", y_ap[oc * 128:(oc + 1) * 128, sl * SL:(sl + 1) * SL],
                                outstg[ob][:, :], s_acc, accum_op=OP.add)
                out_r.release(ob, s_acc, acctok)
            hr.release(hb, s_mm, s_mm.n)
        prev_e_acc = s_acc.n
    ph.wait("sync", s_acc, s_acc.n)
    if hasattr(ctx, "dbg_moe"):
        s_st = ctx.sem(pfx + "dbgst")
        ph.wait("scalar", s_acc, s_acc.n)
        ph.dma("scalar", ctx.dbg_moe["r8b"], r8a[:, :], s_st)
        ph.dma("scalar", ctx.dbg_moe["lg"], lg[:, :], s_st)
        ph.dma("scalar", ctx.dbg_moe["stage"], cwstage[0:1, :], s_st)
        if "gT" in ctx.dbg_moe:
            ph.dma("scalar", ctx.dbg_moe["gT"], gT[:, :, :], s_st)
        if "ident" in ctx.dbg_moe:
            ph.dma("scalar", ctx.dbg_moe["ident"], c["ident"][:, :], s_st)
        ph.wait("scalar", s_st, s_st.n)
    ph.emit()
    return s_acc.n


def l0_rope_tables():
    inv = (np.float32(1.0) / np.float32(10000.0) **
           (np.arange(0, 64, 2, dtype=np.float32) / np.float32(64.0)))
    t = np.arange(T, dtype=np.float32)
    ang = t[None, :] * inv[:, None]
    return np.cos(ang).astype(np.float32), np.sin(ang).astype(np.float32)


def l0_mask_tiles():
    i = np.arange(128)[:, None]
    j = np.arange(128)[None, :]
    mdiag = np.where(j >= i, 0.0, -1e30).astype(np.float32)
    medge = np.where(j < i, 0.0, -1e30).astype(np.float32)
    return mdiag, medge


def build_l0attn_program():
    nc = bass.Bass("TRN2", target_bir_lowering=False)
    x = nc.dram_tensor("x", [D, T], F32, kind="ExternalInput").ap()
    wqc = nc.dram_tensor("wqc", [D, 512], F32, kind="ExternalInput").ap()
    wqu = nc.dram_tensor("wqu", [512, D], F32, kind="ExternalInput").ap()
    wk = nc.dram_tensor("wk", [D, 128], F32, kind="ExternalInput").ap()
    wv = nc.dram_tensor("wv", [D, 128], F32, kind="ExternalInput").ap()
    qnw = nc.dram_tensor("qnw", [128], F32, kind="ExternalInput").ap()
    knw = nc.dram_tensor("knw", [128], F32, kind="ExternalInput").ap()
    sink = nc.dram_tensor("sink", [16], F32, kind="ExternalInput").ap()
    w1 = nc.dram_tensor("w1", [4, 512, D], F32, kind="ExternalInput").ap()
    w2 = nc.dram_tensor("w2", [8192, D], F32, kind="ExternalInput").ap()
    cos = nc.dram_tensor("cos0", [32, T], F32, kind="ExternalInput").ap()
    sin = nc.dram_tensor("sin0", [32, T], F32, kind="ExternalInput").ap()
    mdiag = nc.dram_tensor("mdiag", [128, 128], F32, kind="ExternalInput").ap()
    medge = nc.dram_tensor("medge", [128, 128], F32, kind="ExternalInput").ap()
    y = nc.dram_tensor("y", [D, T], F32, kind="ExternalOutput").ap()
    with ExitStack() as stack:
        ctx = Ctx(nc, stack)
        c = make_common(ctx)
        h_sb = ctx.sbuf("h", [128, KC, T], BF16)
        attn_half = ctx.sbuf("attnh", [128, 8, T], BF16)
        emit_l0attn(ctx, c, "a0_", x, y, wqc, wqu, wk, wv, qnw, knw, sink,
                    w1, w2, cos, sin, mdiag, medge, h_sb, attn_half)
    return nc


def build_l0ffn_program():
    nc = bass.Bass("TRN2", target_bir_lowering=False)
    x = nc.dram_tensor("x", [D, T], F32, kind="ExternalInput").ap()
    gate = nc.dram_tensor("gate", [8, D], F32, kind="ExternalInput").ap()
    w1 = nc.dram_tensor("w1", [8, D, 2816], F32, kind="ExternalInput").ap()
    w2 = nc.dram_tensor("w2", [8, 1408, D], F32, kind="ExternalInput").ap()
    sw1 = nc.dram_tensor("sw1", [1408, D], F32, kind="ExternalInput").ap()
    sw2 = nc.dram_tensor("sw2", [1408, D], F32, kind="ExternalInput").ap()
    sw3 = nc.dram_tensor("sw3", [D, 1408], F32, kind="ExternalInput").ap()
    y = nc.dram_tensor("y", [D, T], F32, kind="ExternalOutput").ap()
    h_dram = nc.dram_tensor("h_dram", [2 * KC * 128, 1024], BF16).ap()
    cw_dram = nc.dram_tensor("cw_dram", [8, T], BF16).ap()
    with ExitStack() as stack:
        ctx = Ctx(nc, stack)
        c = make_common(ctx)
        emit_l0ffn(ctx, c, "f0_", x, y, gate, w1, w2, sw1, sw2, sw3, h_dram, cw_dram)
    return nc


def build_l1ffn_program():
    nc = bass.Bass("TRN2", target_bir_lowering=False)
    x = nc.dram_tensor("x", [D, T], F32, kind="ExternalInput").ap()
    wg = nc.dram_tensor("wg", [8192, D], F32, kind="ExternalInput").ap()
    wu = nc.dram_tensor("wu", [8192, D], F32, kind="ExternalInput").ap()
    wd = nc.dram_tensor("wd", [D, 8192], F32, kind="ExternalInput").ap()
    y = nc.dram_tensor("y", [D, T], F32, kind="ExternalOutput").ap()
    with ExitStack() as stack:
        ctx = Ctx(nc, stack)
        c = make_common(ctx)
        h_sb = ctx.sbuf("h", [128, KC, T], BF16)
        emit_l1ffn(ctx, c, "f1_", x, y, wg, wu, wd, h_sb)
    return nc


def _barrier(ctx, sem, tok):
    ph = Ph(ctx.nc)
    for e in ENGS:
        ph.wait(e, sem, tok)
    ph.emit()


def build_fused_program():
    nc = bass.Bass("TRN2", target_bir_lowering=False)
    x = nc.dram_tensor("x", [D, T], F32, kind="ExternalInput").ap()
    wqc = nc.dram_tensor("wqc", [D, 512], F32, kind="ExternalInput").ap()
    wqu = nc.dram_tensor("wqu", [512, D], F32, kind="ExternalInput").ap()
    wk = nc.dram_tensor("wk", [D, 128], F32, kind="ExternalInput").ap()
    wv = nc.dram_tensor("wv", [D, 128], F32, kind="ExternalInput").ap()
    qnw = nc.dram_tensor("qnw", [128], F32, kind="ExternalInput").ap()
    knw = nc.dram_tensor("knw", [128], F32, kind="ExternalInput").ap()
    sink = nc.dram_tensor("sink", [16], F32, kind="ExternalInput").ap()
    ow1 = nc.dram_tensor("ow1", [4, 512, D], F32, kind="ExternalInput").ap()
    ow2 = nc.dram_tensor("ow2", [8192, D], F32, kind="ExternalInput").ap()
    cos0 = nc.dram_tensor("cos0", [32, T], F32, kind="ExternalInput").ap()
    sin0 = nc.dram_tensor("sin0", [32, T], F32, kind="ExternalInput").ap()
    mdiag = nc.dram_tensor("mdiag", [128, 128], F32, kind="ExternalInput").ap()
    medge = nc.dram_tensor("medge", [128, 128], F32, kind="ExternalInput").ap()
    gate = nc.dram_tensor("gate", [8, D], F32, kind="ExternalInput").ap()
    mw1 = nc.dram_tensor("mw1", [8, D, 2816], F32, kind="ExternalInput").ap()
    mw2 = nc.dram_tensor("mw2", [8, 1408, D], F32, kind="ExternalInput").ap()
    sw1 = nc.dram_tensor("sw1", [1408, D], F32, kind="ExternalInput").ap()
    sw2 = nc.dram_tensor("sw2", [1408, D], F32, kind="ExternalInput").ap()
    sw3 = nc.dram_tensor("sw3", [D, 1408], F32, kind="ExternalInput").ap()
    w_aps = {n: nc.dram_tensor(n, [D, D], F32, kind="ExternalInput").ap()
             for n in ("Wq", "Wk", "Wv", "Wo")}
    b_aps = {n: nc.dram_tensor(n, [D], F32, kind="ExternalInput").ap()
             for n in ("bq", "bk", "bv", "bo")}
    cos1 = nc.dram_tensor("rope_cos", [64, T], F32, kind="ExternalInput").ap()
    sin1 = nc.dram_tensor("rope_sin", [64, T], F32, kind="ExternalInput").ap()
    cm = nc.dram_tensor("cmask", [128, 128], F32, kind="ExternalInput").ap()
    wg = nc.dram_tensor("wg", [8192, D], F32, kind="ExternalInput").ap()
    wu = nc.dram_tensor("wu", [8192, D], F32, kind="ExternalInput").ap()
    wd = nc.dram_tensor("wd", [D, 8192], F32, kind="ExternalInput").ap()
    y = nc.dram_tensor("y", [D, T], F32, kind="ExternalOutput").ap()
    xa = nc.dram_tensor("xa", [D, T], F32).ap()
    xb = nc.dram_tensor("xb", [D, T], F32).ap()
    xc = nc.dram_tensor("xc", [D, T], F32).ap()
    h_dram = nc.dram_tensor("h_dram", [2 * KC * 128, 1024], BF16).ap()
    cw_dram = nc.dram_tensor("cw_dram", [8, T], BF16).ap()

    with ExitStack() as stack:
        ctx = Ctx(nc, stack)
        c = make_common(ctx)
        with ctx.scope():
            h_sb = ctx.sbuf("h0", [128, KC, T], BF16)
            attn_half = ctx.sbuf("attnh0", [128, 8, T], BF16)
            emit_l0attn(ctx, c, "a0_", x, xa, wqc, wqu, wk, wv, qnw, knw, sink,
                        ow1, ow2, cos0, sin0, mdiag, medge, h_sb, attn_half)
        _barrier(ctx, ctx.sem("a0_acc"), ctx.sem("a0_acc").n)
        with ctx.scope():
            emit_l0ffn(ctx, c, "f0_", xa, xb, gate, mw1, mw2, sw1, sw2, sw3,
                       h_dram, cw_dram)
        _barrier(ctx, ctx.sem("f0_acc"), ctx.sem("f0_acc").n)
        with ctx.scope():
            h_sb = ctx.sbuf("h1", [128, KC, T], BF16)
            attn_sb = ctx.sbuf("attn1", [128, 16, T], BF16)
            emit_l1attn(ctx, c, "a1_", xb, xc, w_aps, b_aps, cos1, sin1, cm,
                        h_sb, attn_sb)
        _barrier(ctx, ctx.sem("a1_st"), ctx.sem("a1_st").n)
        with ctx.scope():
            h_sb = ctx.sbuf("h2", [128, KC, T], BF16)
            emit_l1ffn(ctx, c, "f1_", xc, y, wg, wu, wd, h_sb)
    return nc


_PROG_CACHE = {}


def _programs():
    if "p" not in _PROG_CACHE:
        _PROG_CACHE["p"] = (build_l0attn_program(), build_l0ffn_program(),
                            build_l1attn_program(), build_l1ffn_program())
    return _PROG_CACHE["p"]


def _fused_program():
    if "f" not in _PROG_CACHE:
        _PROG_CACHE["f"] = build_fused_program()
    return _PROG_CACHE["f"]


def kernel(**inputs):
    n = 8
    nc = _fused_program()
    cos0, sin0 = l0_rope_tables()
    mdiag, medge = l0_mask_tiles()
    cos1, sin1 = l1_rope_tables()
    cm = causal_mask_tile()
    im = [{"x": np.ascontiguousarray(inputs["L0_attn_h"][b].T),
           "wqc": inputs["L0_attn_wq_comp"], "wqu": inputs["L0_attn_wq_up"],
           "wk": inputs["L0_attn_wk"], "wv": inputs["L0_attn_wv"],
           "qnw": inputs["L0_attn_q_norm_w"], "knw": inputs["L0_attn_k_norm_w"],
           "sink": inputs["L0_attn_sink_logit"],
           "ow1": inputs["L0_attn_out_w1"], "ow2": inputs["L0_attn_out_w2"],
           "cos0": cos0, "sin0": sin0, "mdiag": mdiag, "medge": medge,
           "gate": inputs["L0_ffn_gate_w"], "mw1": inputs["L0_ffn_w1"],
           "mw2": inputs["L0_ffn_w2"], "sw1": inputs["L0_ffn_sw1"],
           "sw2": inputs["L0_ffn_sw2"], "sw3": inputs["L0_ffn_sw3"],
           "Wq": inputs["L1_attn_Wq"], "Wk": inputs["L1_attn_Wk"],
           "Wv": inputs["L1_attn_Wv"], "Wo": inputs["L1_attn_Wo"],
           "bq": inputs["L1_attn_bq"], "bk": inputs["L1_attn_bk"],
           "bv": inputs["L1_attn_bv"], "bo": inputs["L1_attn_bo"],
           "rope_cos": cos1, "rope_sin": sin1, "cmask": cm,
           "wg": inputs["L1_ffn_w_gate"], "wu": inputs["L1_ffn_w_up"],
           "wd": inputs["L1_ffn_w_down"]} for b in range(n)]
    res = run_bass_kernel_spmd(nc, im, core_ids=list(range(n)))
    return np.stack([np.asarray(res.results[b]["y"]).T for b in range(n)])


def kernel_unfused(**inputs):
    n = 8
    nc0a, nc0f, nc1a, nc1f = _programs()
    cos0, sin0 = l0_rope_tables()
    mdiag, medge = l0_mask_tiles()
    cos1, sin1 = l1_rope_tables()
    cm = causal_mask_tile()
    xs = [np.ascontiguousarray(inputs["L0_attn_h"][b].T) for b in range(n)]

    im = [{"x": xs[b],
           "wqc": inputs["L0_attn_wq_comp"], "wqu": inputs["L0_attn_wq_up"],
           "wk": inputs["L0_attn_wk"], "wv": inputs["L0_attn_wv"],
           "qnw": inputs["L0_attn_q_norm_w"], "knw": inputs["L0_attn_k_norm_w"],
           "sink": inputs["L0_attn_sink_logit"],
           "w1": inputs["L0_attn_out_w1"], "w2": inputs["L0_attn_out_w2"],
           "cos0": cos0, "sin0": sin0, "mdiag": mdiag, "medge": medge}
          for b in range(n)]
    res = run_bass_kernel_spmd(nc0a, im, core_ids=list(range(n)))
    xs = [np.asarray(res.results[b]["y"]) for b in range(n)]

    im = [{"x": xs[b], "gate": inputs["L0_ffn_gate_w"],
           "w1": inputs["L0_ffn_w1"], "w2": inputs["L0_ffn_w2"],
           "sw1": inputs["L0_ffn_sw1"], "sw2": inputs["L0_ffn_sw2"],
           "sw3": inputs["L0_ffn_sw3"]} for b in range(n)]
    res = run_bass_kernel_spmd(nc0f, im, core_ids=list(range(n)))
    xs = [np.asarray(res.results[b]["y"]) for b in range(n)]

    im = [{"x": xs[b],
           "Wq": inputs["L1_attn_Wq"], "Wk": inputs["L1_attn_Wk"],
           "Wv": inputs["L1_attn_Wv"], "Wo": inputs["L1_attn_Wo"],
           "bq": inputs["L1_attn_bq"], "bk": inputs["L1_attn_bk"],
           "bv": inputs["L1_attn_bv"], "bo": inputs["L1_attn_bo"],
           "rope_cos": cos1, "rope_sin": sin1, "cmask": cm} for b in range(n)]
    res = run_bass_kernel_spmd(nc1a, im, core_ids=list(range(n)))
    xs = [np.asarray(res.results[b]["y"]) for b in range(n)]

    im = [{"x": xs[b], "wg": inputs["L1_ffn_w_gate"], "wu": inputs["L1_ffn_w_up"],
           "wd": inputs["L1_ffn_w_down"]} for b in range(n)]
    res = run_bass_kernel_spmd(nc1f, im, core_ids=list(range(n)))
    return np.stack([np.asarray(res.results[b]["y"]).T for b in range(n)])


# version 05 (final)
# speedup vs baseline: 1.2258x; 1.2258x over previous
import numpy as np
import ml_dtypes
from contextlib import ExitStack, contextmanager as _contextmanager

import concourse.bass as bass
import concourse.mybir as mybir
from concourse.bass_utils import run_bass_kernel_spmd

F32 = mybir.dt.float32
BF16 = mybir.dt.bfloat16
AF = mybir.ActivationFunctionType
OP = mybir.AluOpType
AX = mybir.AxisListType

D = 2048
T = 2048
KC = D // 128
NSLAB = 4
SLAB = 512
EPS_LN = 1e-5

ENGS = ("tensor", "vector", "scalar", "sync", "gpsimd")


class Sem:
    def __init__(self, stack, nc, name):
        self.h = stack.enter_context(nc.semaphore(name))
        self.n = 0

    def inc(self, k=1):
        self.n += k
        return self.n


class Ph:

    def __init__(self, nc):
        self.nc = nc
        self.ops = {e: [] for e in ENGS}

    def wait(self, eng, sem, tok):
        if tok is None:
            return
        self.ops[eng].append(lambda h: h.wait_ge(sem.h, tok))

    def dma(self, eng, out, in_, sem, **kw):
        tok = sem.inc(16)
        self.ops[eng].append(
            lambda h: h.dma_start(out=out, in_=in_, **kw).then_inc(sem.h, 16))
        return tok

    def op(self, eng, fn, sem=None, k=1):
        if sem is None:
            self.ops[eng].append(lambda h: fn(h))
            return None
        tok = sem.inc(k)
        self.ops[eng].append(lambda h: fn(h).then_inc(sem.h, k))
        return tok

    def gop(self, eng, fn, sem, k=1):
        tok = self.op(eng, fn, sem=sem, k=k)
        self.wait(eng, sem, tok)
        return tok

    def emit(self):
        with self.nc.Block() as blk:
            for e in ENGS:
                lst = self.ops[e]
                if not lst:
                    continue

                def body(h, lst=lst):
                    for fn in lst:
                        fn(h)

                getattr(blk, e)(body)


class Ring:

    def __init__(self, nbuf):
        self.n = nbuf
        self.free = [None] * nbuf
        self.i = 0

    def acquire(self, ph, prod_eng):
        b = self.i % self.n
        self.i += 1
        if self.free[b] is not None:
            sem, tok = self.free[b]
            ph.wait(prod_eng, sem, tok)
        return b

    def release(self, b, sem, tok):
        self.free[b] = (sem, tok)


class Ctx:
    def __init__(self, nc, stack):
        self.nc = nc
        self.stack = stack
        self._scopes = []
        self._semcache = {}

    def sem(self, name):
        if name not in self._semcache:
            self._semcache[name] = Sem(self.stack, self.nc, name)
        return self._semcache[name]

    def _cur(self):
        return self._scopes[-1] if self._scopes else self.stack

    @_contextmanager
    def scope(self):
        st = ExitStack()
        self._scopes.append(st)
        try:
            with st:
                yield
        finally:
            self._scopes.pop()

    def sbuf(self, name, shape, dtype):
        return self._cur().enter_context(self.nc.sbuf_tensor(name, list(shape), dtype))

    def psum(self, name, shape, dtype=F32):
        return self.stack.enter_context(self.nc.psum_tensor(name, list(shape), dtype))



def make_common(ctx):
    c = {}
    c["PS"] = [ctx.psum(f"PS{i}", [128, 512]) for i in range(8)]
    c["ones_col"] = ctx.sbuf("ones_col", [128, 1], F32)
    c["ones_row"] = ctx.sbuf("ones_row", [1, 128], F32)
    c["ones_col_bf"] = ctx.sbuf("ones_col_bf", [128, 1], BF16)
    c["ones_row_bf"] = ctx.sbuf("ones_row_bf", [1, 128], BF16)
    c["xs"] = [ctx.sbuf(f"xs{i}", [128, 1, 512], F32) for i in range(2)]
    c["sq"] = [ctx.sbuf(f"sq{i}", [128, 512], F32) for i in range(2)]
    c["mean"] = ctx.sbuf("mean_row", [1, T], F32)
    c["rstd"] = ctx.sbuf("rstd_row", [1, T], F32)
    c["srow"] = ctx.sbuf("scratch_row", [1, 512], F32)
    c["srow2"] = ctx.sbuf("scratch_row2", [1, 512], F32)
    c["eps_ln"] = ctx.sbuf("eps_ln", [1, 1], F32)
    c["ident"] = ctx.sbuf("ident", [128, 128], F32)
    c["wstg"] = [ctx.sbuf(f"wstg{i}", [128, 2048], BF16) for i in range(2)]
    ph = Ph(ctx.nc)
    init = ctx.sem("init")
    ph.op("gpsimd", lambda h: h.memset(c["ones_col"][:, :], 1.0))
    ph.op("gpsimd", lambda h: h.memset(c["ones_row"][:, :], 1.0))
    ph.op("gpsimd", lambda h: h.memset(c["ones_col_bf"][:, :], 1.0))
    ph.op("gpsimd", lambda h: h.memset(c["ones_row_bf"][:, :], 1.0))
    ph.op("gpsimd", lambda h: h.memset(c["eps_ln"][:, :], EPS_LN))
    ph.op("gpsimd", lambda h: h.iota(c["ident"][:, :], [[1, 128]], channel_multiplier=0,
                                     allow_small_or_imprecise_dtypes=True))
    ph.op("gpsimd", lambda h: h.iota(c["sq"][0][:, 0:128], [[0, 128]], channel_multiplier=1,
                                     allow_small_or_imprecise_dtypes=True), sem=init)
    ph2 = Ph(ctx.nc)
    ph2.wait("vector", init, init.n)
    ph2.op("vector", lambda h: h.tensor_tensor(c["ident"][:, :], c["ident"][:, :],
                                               c["sq"][0][:, 0:128], OP.is_equal), sem=init)
    ph.emit()
    ph2.emit()
    c["init_tok"] = init.n
    c["init_sem"] = init
    return c


def emit_ln(ctx, c, pfx, x_ap, h_sb):
    nc = ctx.nc
    PS = c["PS"]
    s_x = ctx.sem(pfx + "x")
    s_mm = ctx.sem(pfx + "mm")
    s_act = ctx.sem(pfx + "act")
    s_dve = ctx.sem(pfx + "dve")
    xr = Ring(2)
    sqr = Ring(2)
    x_grp = x_ap.rearrange("(c p) t -> p c t", p=128)

    ph = Ph(nc)
    ph.wait("tensor", c["init_sem"], c["init_tok"])
    stat_free_tok = None
    for sl in range(NSLAB):
        s0 = sl * SLAB
        if stat_free_tok is not None:
            ph.wait("tensor", s_dve, stat_free_tok)
        for kc in range(KC):
            b = xr.acquire(ph, "gpsimd")
            xtok = ph.dma("gpsimd", c["xs"][b][:, 0, :],
                          x_grp[:, kc, s0:s0 + SLAB], s_x)
            cur = {"b": b, "tok": xtok}
            xs_ap = c["xs"][b][:, 0, :]
            sb = sqr.acquire(ph, "scalar")
            ph.wait("scalar", s_x, cur["tok"])
            atok = ph.op("scalar", lambda h, o=c["sq"][sb][:, :], i=xs_ap:
                         h.activation(o, i, AF.Square), sem=s_act)
            ph.wait("tensor", s_x, cur["tok"])
            ph.op("tensor", lambda h, i=xs_ap, st=(kc == 0), sp=(kc == KC - 1):
                  h.matmul(PS[0][0:1, :], c["ones_col"][:, :], i, start=st, stop=sp))
            ph.wait("tensor", s_act, atok)
            mtok = ph.op("tensor", lambda h, i=c["sq"][sb][:, :], st=(kc == 0), sp=(kc == KC - 1):
                         h.matmul(PS[1][0:1, :], c["ones_col"][:, :], i, start=st, stop=sp),
                         sem=s_mm)
            sqr.release(sb, s_mm, mtok)
            xr.release(b, s_mm, mtok)
        sums_done = mtok
        ph.wait("vector", s_mm, sums_done)
        ph.op("vector", lambda h, o=c["mean"][0:1, s0:s0 + SLAB]:
              h.tensor_scalar(o, PS[0][0:1, :], 1.0 / D, None, OP.mult))
        ph.op("vector", lambda h:
              h.tensor_scalar(c["srow"][0:1, :], PS[1][0:1, :], 1.0 / D, None, OP.mult))
        ph.op("vector", lambda h, m=c["mean"][0:1, s0:s0 + SLAB]:
              h.tensor_tensor(c["srow2"][0:1, :], m, m, OP.mult))
        vtok = ph.op("vector", lambda h:
                     h.tensor_tensor(c["srow"][0:1, :], c["srow"][0:1, :],
                                     c["srow2"][0:1, :], OP.subtract), sem=s_dve)
        stat_free_tok = vtok
        ph.wait("scalar", s_dve, vtok)
        atok = ph.op("scalar", lambda h:
                     h.activation(c["srow2"][0:1, :], c["srow"][0:1, :], AF.Sqrt,
                                  bias=c["eps_ln"][0:1, :]), sem=s_act)
        ph.wait("vector", s_act, atok)
        ph.op("vector", lambda h, o=c["rstd"][0:1, s0:s0 + SLAB]:
              h.reciprocal(o, c["srow2"][0:1, :]), sem=s_dve)
    rstd_tok = s_dve.n
    bc_free = rstd_tok
    for sl in range(NSLAB):
        s0 = sl * SLAB
        ph.wait("tensor", s_dve, bc_free)
        ph.op("tensor", lambda h, i=c["mean"][0:1, s0:s0 + SLAB]:
              h.matmul(PS[2][:, :], c["ones_row"][:, :], i, start=True, stop=True))
        btok = ph.op("tensor", lambda h, i=c["rstd"][0:1, s0:s0 + SLAB]:
                     h.matmul(PS[3][:, :], c["ones_row"][:, :], i, start=True, stop=True),
                     sem=s_mm)
        ph.wait("vector", s_mm, btok)
        for kc in range(KC):
            b = xr.acquire(ph, "gpsimd")
            xtok = ph.dma("gpsimd", c["xs"][b][:, 0, :],
                          x_grp[:, kc, s0:s0 + SLAB], s_x)
            cur = {"b": b, "tok": xtok}
            xs_ap = c["xs"][b][:, 0, :]
            ph.wait("vector", s_x, cur["tok"])
            ph.op("vector", lambda h, o=c["sq"][0][:, :], i=xs_ap:
                  h.tensor_tensor(o, i, PS[2][:, :], OP.subtract))
            dtok = ph.op("vector", lambda h, o=h_sb[:, kc, s0:s0 + SLAB], i=c["sq"][0][:, :]:
                         h.tensor_tensor(o, i, PS[3][:, :], OP.mult), sem=s_dve)
            xr.release(b, s_dve, dtok)
        bc_free = dtok
    ph.emit()
    return s_dve.n


def emit_l1ffn(ctx, c, pfx, x_ap, y_ap, wg_ap, wu_ap, wd_ap, h_sb):
    nc = ctx.nc
    PS = c["PS"]
    I = 8192
    NQ = 4
    ICQ = 16

    s_w = ctx.sem(pfx + "w")
    s_xb = ctx.sem(pfx + "xb")
    s_mm = ctx.sem(pfx + "mm")
    s_act = ctx.sem(pfx + "act")
    s_dve = ctx.sem(pfx + "dve")
    s_acc = ctx.sem(pfx + "acc")

    act_sb = ctx.sbuf(pfx + "act", [128, ICQ, T], BF16)
    wgt = [ctx.sbuf(pfx + f"wgt{i}", [128, 16, 128], BF16) for i in range(2)]
    wut = [ctx.sbuf(pfx + f"wut{i}", [128, 16, 128], BF16) for i in range(2)]
    wdt = [ctx.sbuf(pfx + f"wdt{i}", [128, 16, 128], BF16) for i in range(2)]
    sil = [ctx.sbuf(pfx + f"sil{i}", [128, 512], F32) for i in range(2)]
    outstg = [ctx.sbuf(pfx + f"outstg{i}", [128, 1024], F32) for i in range(2)]

    ph = Ph(nc)
    ytok = ph.dma("gpsimd", y_ap[:, :], x_ap[:, :], s_acc)
    ph.emit()

    h_done = emit_ln(ctx, c, pfx + "ln_", x_ap, h_sb)
    s_lndve = ctx.sem(pfx + "ln_dve")

    ph = Ph(nc)
    ph.wait("tensor", s_lndve, h_done)

    wstg_r = Ring(2)
    wgt_r = Ring(2)
    wut_r = Ring(2)
    wdt_r = Ring(2)
    sil_r = Ring(2)
    out_r = Ring(2)
    ps_free = [None] * 8

    prev_q_acc_tok = ytok
    act_done_tok = None

    for q in range(NQ):
        for ic in range(ICQ):
            icg = q * ICQ + ic
            b1 = wstg_r.acquire(ph, "gpsimd")
            wtok = ph.dma("gpsimd", c["wstg"][b1][:, :],
                          wg_ap[icg * 128:(icg + 1) * 128, :], s_w)
            g1 = wgt_r.acquire(ph, "sync")
            ph.wait("sync", s_w, wtok)
            xtok = ph.dma("sync", wgt[g1][:, :, :], c["wstg"][b1][:, :], s_xb,
                          transpose=True)
            wstg_r.release(b1, s_xb, xtok)
            b2 = wstg_r.acquire(ph, "gpsimd")
            wtok2 = ph.dma("gpsimd", c["wstg"][b2][:, :],
                           wu_ap[icg * 128:(icg + 1) * 128, :], s_w)
            g2 = wut_r.acquire(ph, "sync")
            ph.wait("sync", s_w, wtok2)
            xtok2 = ph.dma("sync", wut[g2][:, :, :], c["wstg"][b2][:, :], s_xb,
                           transpose=True)
            wstg_r.release(b2, s_xb, xtok2)

            ph.wait("tensor", s_xb, xtok2)
            gu_mm = []
            for sl in range(NSLAB):
                s0 = sl * SLAB
                if ps_free[sl] is not None:
                    ph.wait("tensor", *ps_free[sl])
                gtok = None
                for kc in range(KC):
                    last = kc == KC - 1
                    gtok = ph.op("tensor",
                                 lambda h, w=wgt[g1][:, kc, :], r=h_sb[:, kc, s0:s0 + SLAB],
                                 p=PS[sl][:, :], st=(kc == 0), sp=last:
                                 h.matmul(p, w, r, start=st, stop=sp),
                                 sem=s_mm if last else None)
                if ps_free[4 + sl] is not None:
                    ph.wait("tensor", *ps_free[4 + sl])
                utok = None
                for kc in range(KC):
                    last = kc == KC - 1
                    utok = ph.op("tensor",
                                 lambda h, w=wut[g2][:, kc, :], r=h_sb[:, kc, s0:s0 + SLAB],
                                 p=PS[4 + sl][:, :], st=(kc == 0), sp=last:
                                 h.matmul(p, w, r, start=st, stop=sp),
                                 sem=s_mm if last else None)
                gu_mm.append((gtok, utok))
            wgt_r.release(g1, s_mm, gu_mm[-1][0])
            wut_r.release(g2, s_mm, gu_mm[-1][1])
            for sl in range(NSLAB):
                s0 = sl * SLAB
                gtok, utok = gu_mm[sl]
                sb = sil_r.acquire(ph, "scalar")
                ph.wait("scalar", s_mm, gtok)
                atok = ph.op("scalar", lambda h, o=sil[sb][:, :], p=PS[sl][:, :]:
                             h.activation(o, p, AF.Silu), sem=s_act)
                ps_free[sl] = (s_act, atok)
                ph.wait("vector", s_act, atok)
                ph.wait("vector", s_mm, utok)
                dtok = ph.op("vector", lambda h, o=act_sb[:, ic, s0:s0 + SLAB],
                             i=sil[sb][:, :], p=PS[4 + sl][:, :]:
                             h.tensor_tensor(o, i, p, OP.mult), sem=s_dve)
                ps_free[4 + sl] = (s_dve, dtok)
                sil_r.release(sb, s_dve, dtok)
                act_done_tok = dtok

        for oc in range(KC):
            b1 = wstg_r.acquire(ph, "gpsimd")
            src = wd_ap[oc * 128:(oc + 1) * 128, q * 2048:(q + 1) * 2048]
            wtok = ph.dma("gpsimd", c["wstg"][b1][:, :], src, s_w)
            g1 = wdt_r.acquire(ph, "sync")
            ph.wait("sync", s_w, wtok)
            xtok = ph.dma("sync", wdt[g1][:, :, :], c["wstg"][b1][:, :], s_xb,
                          transpose=True)
            wstg_r.release(b1, s_xb, xtok)

            ph.wait("tensor", s_xb, xtok)
            if oc == 0:
                ph.wait("tensor", s_dve, act_done_tok)
            mtok = etok = None
            for half in range(2):
                ob = out_r.acquire(ph, "scalar")
                for hs in range(2):
                    sl = half * 2 + hs
                    s0 = sl * SLAB
                    pb = (oc * NSLAB + sl) % 2
                    if ps_free[pb] is not None:
                        ph.wait("tensor", *ps_free[pb])
                    for ic in range(ICQ):
                        last = ic == ICQ - 1
                        mtok = ph.op("tensor",
                                     lambda h, w=wdt[g1][:, ic, :], r=act_sb[:, ic, s0:s0 + SLAB],
                                     p=PS[pb][:, :], st=(ic == 0), sp=last:
                                     h.matmul(p, w, r, start=st, stop=sp),
                                     sem=s_mm if last else None)
                    ph.wait("scalar", s_mm, mtok)
                    etok = ph.op("scalar",
                                 lambda h, o=outstg[ob][:, hs * 512:(hs + 1) * 512], p=PS[pb][:, :]:
                                 h.copy(o, p), sem=s_act)
                    ps_free[pb] = (s_act, etok)
                ph.wait("gpsimd", s_act, etok)
                ph.wait("gpsimd", s_acc, prev_q_acc_tok)
                acctok = ph.dma("gpsimd",
                                y_ap[oc * 128:(oc + 1) * 128, half * 1024:(half + 1) * 1024],
                                outstg[ob][:, :], s_acc, accum_op=OP.add)
                out_r.release(ob, s_acc, acctok)
            wdt_r.release(g1, s_mm, mtok)
        prev_q_acc_tok = s_acc.n
        ph.wait("vector", s_mm, s_mm.n)

    ph.wait("sync", s_acc, s_acc.n)
    ph.emit()
    return s_acc.n



def emit_l1attn(ctx, c, pfx, x_ap, y_ap, w_aps, b_aps, rope_cos_ap, rope_sin_ap,
                cmask_ap, h_sb, attn_sb):
    nc = ctx.nc
    PS = c["PS"]
    NH = 16
    HD = 128
    INV_S = 1.0 / float(np.sqrt(HD))

    s_w = ctx.sem(pfx + "w")
    s_xb = ctx.sem(pfx + "xb")
    s_mm = ctx.sem(pfx + "mm")
    s_act = ctx.sem(pfx + "act")
    s_dve = ctx.sem(pfx + "dve")
    s_x = ctx.sem(pfx + "x")
    s_st = ctx.sem(pfx + "st")
    s_bv = ctx.sem(pfx + "bv")

    wt = [ctx.sbuf(pfx + f"wt{i}", [128, 16, 128], BF16) for i in range(3)]
    qraw = [ctx.sbuf(pfx + f"qraw{i}", [128, 512], F32) for i in range(1)]
    qfm = ctx.sbuf(pfx + "qfm", [128, T], BF16)
    kfm = ctx.sbuf(pfx + "kfm", [128, T], BF16)
    vtm = ctx.sbuf(pfx + "vtm", [128, 16, 128], BF16)
    sqb = [ctx.sbuf(pfx + f"sqb{i}", [128, 512], BF16) for i in range(1)]
    negmb = [ctx.sbuf(pfx + f"negmb{i}", [128, 512], BF16) for i in range(2)]
    rb = [ctx.sbuf(pfx + f"rb{i}", [128, 128], F32) for i in range(2)]
    ptile = [ctx.sbuf(pfx + f"p{i}", [128, 128], BF16) for i in range(3)]
    stile = [ctx.sbuf(pfx + f"s{i}", [128, 128], F32) for i in range(2)]
    bcols = ctx.sbuf(pfx + "bcols", [128, 4, 16], F32)
    bvrow = ctx.sbuf(pfx + "bvrow", [1, 128], BF16)
    ystg = [ctx.sbuf(pfx + f"ystg{i}", [128, 512], F32) for i in range(1)]
    cosd = ctx.sbuf(pfx + "cosd", [128, T], BF16)
    sind = ctx.sbuf(pfx + "sind", [128, T], BF16)
    cmask = ctx.sbuf(pfx + "cmask", [128, 128], BF16)

    ph = Ph(nc)
    ph.wait("gpsimd", c["init_sem"], c["init_tok"])
    ph.dma("gpsimd", cosd[0:64, :], rope_cos_ap[:, :], s_w)
    ph.dma("gpsimd", cosd[64:128, :], rope_cos_ap[:, :], s_w)
    ph.dma("gpsimd", sind[0:64, :], rope_sin_ap[:, :], s_w)
    ph.dma("gpsimd", sind[64:128, :], rope_sin_ap[:, :], s_w)
    ph.dma("gpsimd", cmask[:, :], cmask_ap[:, :], s_w)
    for bi, bn in enumerate(("bq", "bk", "bo")):
        btok = ph.dma("gpsimd", c["sq"][0][0:16, 0:128],
                      b_aps[bn].rearrange("(a b) -> a b", b=128), s_w)
        ph.wait("tensor", s_w, btok)
        ph.wait("tensor", c["init_sem"], c["init_tok"])
        mtok = ph.op("tensor", lambda h: h.transpose(PS[7][0:128, 0:16], c["sq"][0][0:16, 0:128],
                                                     c["ident"][0:16, 0:16]), sem=s_mm)
        ph.wait("vector", s_mm, mtok)
        vtok = ph.op("vector", lambda h, o=bcols[:, bi, :]: h.tensor_copy(o, PS[7][:, 0:16]),
                     sem=s_dve)
        ph.wait("tensor", s_dve, vtok)
        ph.wait("gpsimd", s_dve, vtok)
    ph.emit()
    const_dve = s_dve.n
    const_w = s_w.n

    h_done = emit_ln(ctx, c, pfx + "ln_", x_ap, h_sb)
    s_lndve = ctx.sem(pfx + "ln_dve")

    ph = Ph(nc)
    ph.wait("tensor", s_lndve, h_done)
    ph.wait("vector", s_w, const_w)
    ph.wait("vector", s_dve, const_dve)
    ph.wait("scalar", s_w, const_w)

    wstg_r = Ring(2)
    wt_r = Ring(3)
    qraw_r = Ring(1)
    sqb_r = Ring(1)
    negmb_r = Ring(2)
    rb_r = Ring(2)
    p_r = Ring(3)
    s_r = Ring(2)
    ystg_r = Ring(1)
    ps01_free = [None, None]
    ps23_free = [None, None]
    ps45_free = [None, None]
    ps6_free = [None]
    ps7_free = [None]

    def load_wt(wname, row0):
        bstg = wstg_r.acquire(ph, "gpsimd")
        wtok = ph.dma("gpsimd", c["wstg"][bstg][:, :], w_aps[wname][row0:row0 + 128, :], s_w)
        g = wt_r.acquire(ph, "sync")
        ph.wait("sync", s_w, wtok)
        xtok = ph.dma("sync", wt[g][:, :, :], c["wstg"][bstg][:, :], s_xb, transpose=True)
        wstg_r.release(bstg, s_xb, xtok)
        return g, xtok

    prev_out_tok = None

    for head in range(NH):
        r0 = head * HD
        if prev_out_tok is not None:
            ph.wait("gpsimd", s_mm, prev_out_tok)
        if s_bv.n:
            ph.wait("gpsimd", s_bv, s_bv.n)
        bvtok = ph.dma("gpsimd", bvrow[0:1, :],
                       b_aps["bv"][r0:r0 + 128].rearrange("(a b) -> a b", a=1), s_bv)
        gq, xtq = load_wt("Wq", r0)
        gk, xtk = load_wt("Wk", r0)
        gv, xtv = load_wt("Wv", r0)
        first_rope_wait = True
        for which, g, xt, out_fm, bcol_i, scale2 in (
                ("q", gq, xtq, qfm, 0, INV_S), ("k", gk, xtk, kfm, 1, None)):
            ph.wait("tensor", s_xb, xt)
            for sl in range(NSLAB):
                s0 = sl * SLAB
                pb = sl % 2
                if ps01_free[pb] is not None:
                    ph.wait("tensor", *ps01_free[pb])
                mtok = None
                for kc in range(KC):
                    last = kc == KC - 1
                    mtok = ph.op("tensor",
                                 lambda h, w=wt[g][:, kc, :], r=h_sb[:, kc, s0:s0 + SLAB],
                                 p=PS[pb][:, :], st=(kc == 0), sp=last:
                                 h.matmul(p, w, r, start=st, stop=sp),
                                 sem=s_mm if last else None)
                qb = qraw_r.acquire(ph, "vector")
                ph.wait("vector", s_mm, mtok)
                if first_rope_wait and prev_out_tok is not None:
                    ph.wait("vector", s_mm, prev_out_tok)
                first_rope_wait = False
                if scale2 is None:
                    dtok = ph.op("vector", lambda h, o=qraw[qb][:, :], p=PS[pb][:, :],
                                 s1=bcols[:, bcol_i, head:head + 1]:
                                 h.tensor_scalar(o, p, s1, None, OP.add), sem=s_dve)
                else:
                    dtok = ph.op("vector", lambda h, o=qraw[qb][:, :], p=PS[pb][:, :],
                                 s1=bcols[:, bcol_i, head:head + 1], s2=scale2:
                                 h.tensor_scalar(o, p, s1, s2, OP.add, OP.mult), sem=s_dve)
                ps01_free[pb] = (s_dve, dtok)
                q0 = qraw[qb][0:64, :]
                q1 = qraw[qb][64:128, :]
                cs0 = cosd[0:64, s0:s0 + SLAB]
                cs1 = cosd[64:128, s0:s0 + SLAB]
                sn0 = sind[0:64, s0:s0 + SLAB]
                sn1 = sind[64:128, s0:s0 + SLAB]
                ta0 = c["sq"][0][0:64, :]
                tb0 = c["sq"][1][0:64, :]
                ta1 = c["sq"][0][64:128, :]
                tb1 = c["sq"][1][64:128, :]
                ph.op("vector", lambda h, o=ta0, a=q0, b=cs0: h.tensor_tensor(o, a, b, OP.mult))
                ph.op("vector", lambda h, o=tb0, a=q1, b=sn1: h.tensor_tensor(o, a, b, OP.mult))
                ph.op("vector", lambda h, o=out_fm[0:64, s0:s0 + SLAB], a=ta0, b=tb0:
                      h.tensor_tensor(o, a, b, OP.subtract))
                ph.op("vector", lambda h, o=ta1, a=q1, b=cs1: h.tensor_tensor(o, a, b, OP.mult))
                ph.op("vector", lambda h, o=tb1, a=q0, b=sn0: h.tensor_tensor(o, a, b, OP.mult))
                dtok = ph.op("vector", lambda h, o=out_fm[64:128, s0:s0 + SLAB], a=ta1, b=tb1:
                             h.tensor_tensor(o, a, b, OP.add), sem=s_dve)
                qraw_r.release(qb, s_dve, dtok)
        rope_done = s_dve.n
        wt_r.release(gq, s_dve, rope_done)
        wt_r.release(gk, s_dve, rope_done)
        ph.wait("tensor", s_xb, xtv)
        first_v = True
        for tc in range(KC):
            t0c = tc * 128
            pb = tc % 2
            if ps01_free[pb] is not None:
                ph.wait("tensor", *ps01_free[pb])
            for kc in range(KC):
                ph.op("tensor", lambda h, w=h_sb[:, kc, t0c:t0c + 128], r=wt[gv][:, kc, :],
                      p=PS[pb][:, 0:128], st=(kc == 0):
                      h.matmul(p, w, r, start=st, stop=False))
            if tc == 0:
                ph.wait("tensor", s_bv, bvtok)
            mtok = ph.op("tensor", lambda h, p=PS[pb][:, 0:128], r=bvrow[0:1, 0:128]:
                         h.matmul(p, c["ones_row_bf"][:, :], r, start=False, stop=True),
                         sem=s_mm)
            ph.wait("scalar", s_mm, mtok)
            if first_v and prev_out_tok is not None:
                ph.wait("scalar", s_mm, prev_out_tok)
            first_v = False
            atok = ph.op("scalar", lambda h, o=vtm[:, tc, :], p=PS[pb][:, 0:128]:
                         h.copy(o, p), sem=s_act)
            ps01_free[pb] = (s_act, atok)
        v_done = s_act.n
        wt_r.release(gv, s_act, v_done)

        for which, fm, row in (("q", qfm, c["mean"]), ("k", kfm, c["rstd"])):
            for sl in range(NSLAB):
                s0 = sl * SLAB
                sb = sqb_r.acquire(ph, "scalar")
                if which == "q" and sl == 0:
                    ph.wait("scalar", s_dve, rope_done)
                atok = ph.op("scalar", lambda h, o=sqb[sb][:, :], i=fm[:, s0:s0 + SLAB]:
                             h.activation(o, i, AF.Square), sem=s_act)
                ph.wait("tensor", s_act, atok)
                if ps6_free[0] is not None:
                    ph.wait("tensor", *ps6_free[0])
                mtok = ph.op("tensor", lambda h, i=sqb[sb][:, :]:
                             h.matmul(PS[6][0:1, :], c["ones_col_bf"][:, :], i,
                                      start=True, stop=True), sem=s_mm)
                sqb_r.release(sb, s_mm, mtok)
                ph.wait("vector", s_mm, mtok)
                vtok = ph.op("vector", lambda h, o=row[0:1, s0:s0 + SLAB]:
                             h.tensor_copy(o, PS[6][0:1, :]), sem=s_dve)
                ps6_free[0] = (s_dve, vtok)
            ph.wait("scalar", s_dve, s_dve.n)
            ph.op("scalar", lambda h, o=row[0:1, :]: h.activation(o, o, AF.Sqrt), sem=s_act)
        norm_act = s_act.n
        ph.wait("vector", s_act, norm_act)
        gtok_ = ph.op("vector", lambda h: h.tensor_reduce(
            c["srow"][0:1, 0:16],
            c["rstd"][0:1, :].rearrange("p (a b) -> p a b", b=128), AX.X, OP.max), sem=s_dve)
        ph.wait("vector", s_dve, gtok_)
        ph.gop("vector", lambda h: h.tensor_copy(c["srow"][0:1, 16:17], c["srow"][0:1, 0:1]),
               s_dve)
        for i in range(1, 16):
            ph.gop("vector", lambda h, i=i: h.tensor_tensor(
                c["srow"][0:1, 16 + i:17 + i], c["srow"][0:1, 15 + i:16 + i],
                c["srow"][0:1, i:i + 1], OP.max), s_dve)
        negm_tok = None
        for qt in range(16):
            negm_tok = ph.op("vector", lambda h, qt=qt: h.tensor_scalar(
                c["rstd"][0:1, qt * 128:(qt + 1) * 128],
                c["mean"][0:1, qt * 128:(qt + 1) * 128],
                c["srow"][0:1, 16 + qt:17 + qt], -1.0, OP.mult, OP.mult),
                sem=s_dve if qt == 15 else None)

        negmb_cur = None
        negmb_tok = None
        last_sub_tok = None
        for qt in range(16):
            q0c = qt * 128
            slb = qt // 4
            if qt % 4 == 0:
                if negmb_cur is not None:
                    negmb_r.release(negmb_cur, s_dve, last_sub_tok)
                nb = negmb_r.acquire(ph, "tensor")
                if ps7_free[0] is not None:
                    ph.wait("tensor", *ps7_free[0])
                ph.wait("tensor", s_dve, negm_tok)
                mtok = ph.op("tensor", lambda h, i=c["rstd"][0:1, slb * 512:(slb + 1) * 512]:
                             h.matmul(PS[7][:, :], c["ones_row"][:, :], i,
                                      start=True, stop=True), sem=s_mm)
                ph.wait("scalar", s_mm, mtok)
                btok = ph.op("scalar", lambda h, o=negmb[nb][:, :]:
                             h.copy(o, PS[7][:, :]), sem=s_act)
                ps7_free[0] = (s_act, btok)
                negmb_tok = btok
                negmb_cur = nb
            if ps6_free[0] is not None:
                ph.wait("tensor", *ps6_free[0])
            pout = qt % 2
            if ps45_free[pout] is not None:
                ph.wait("tensor", *ps45_free[pout])
            out_tok = None
            pend = None

            def flush_pend(pend):
                pt, st_p, atok = pend
                ph.wait("tensor", s_act, atok)
                dn = ph.op("tensor", lambda h, r=ptile[pt][:, :],
                           st_=(st_p == 0), sp=(st_p == qt):
                           h.matmul(PS[6][0:1, 0:128], c["ones_col_bf"][:, :], r,
                                    start=st_, stop=sp),
                           sem=s_mm if st_p == qt else None)
                ot = ph.op("tensor", lambda h, w=vtm[:, st_p, :], r=ptile[pt][:, :],
                           p=PS[4 + pout][:, 0:128], st_=(st_p == 0), sp=(st_p == qt):
                           h.matmul(p, w, r, start=st_, stop=sp), sem=s_mm)
                p_r.release(pt, s_mm, ot)
                return dn, ot

            for st in range(qt + 1):
                psc = s_r.acquire(ph, "vector")
                pb23 = st % 2
                if ps23_free[pb23] is not None:
                    ph.wait("tensor", *ps23_free[pb23])
                mtok = ph.op("tensor", lambda h, w=kfm[:, st * 128:(st + 1) * 128],
                             r=qfm[:, q0c:q0c + 128], p=PS[2 + pb23][:, 0:128]:
                             h.matmul(p, w, r, start=True, stop=True), sem=s_mm)
                if pend is not None:
                    dn_tok, out_tok = flush_pend(pend)
                ph.wait("vector", s_mm, mtok)
                if st == 0:
                    ph.wait("vector", s_act, negmb_tok)
                vtok = ph.op("vector", lambda h, o=stile[psc][:, :], p=PS[2 + pb23][:, 0:128],
                             nm=negmb[negmb_cur][:, (qt % 4) * 128:(qt % 4 + 1) * 128]:
                             h.tensor_tensor(o, p, nm, OP.add), sem=s_dve)
                ps23_free[pb23] = (s_dve, vtok)
                last_sub_tok = vtok
                if st == qt:
                    vtok = ph.op("vector", lambda h, o=stile[psc][:, :]:
                                 h.tensor_tensor(o, o, cmask[:, :], OP.add), sem=s_dve)
                    last_sub_tok = vtok
                pt = p_r.acquire(ph, "scalar")
                ph.wait("scalar", s_dve, vtok)
                atok = ph.op("scalar", lambda h, o=ptile[pt][:, :], i=stile[psc][:, :]:
                             h.activation(o, i, AF.Exp), sem=s_act)
                s_r.release(psc, s_act, atok)
                pend = (pt, st, atok)
            dn_tok, out_tok = flush_pend(pend)
            ph.wait("vector", s_mm, dn_tok)
            vtok = ph.gop("vector", lambda h, o=c["mean"][0:1, q0c:q0c + 128]:
                          h.tensor_copy(o, PS[6][0:1, 0:128]), s_dve)
            ps6_free[0] = (s_dve, vtok)
            rtok = ph.gop("vector", lambda h, o=c["srow2"][0:1, 32:160],
                          i=c["mean"][0:1, q0c:q0c + 128]:
                          h.reciprocal(o, i), s_dve)
            rbb = rb_r.acquire(ph, "tensor")
            if ps7_free[0] is not None:
                ph.wait("tensor", *ps7_free[0])
            ph.wait("tensor", s_dve, rtok)
            mtok = ph.op("tensor", lambda h:
                         h.matmul(PS[7][:, 0:128], c["ones_row"][:, :],
                                  c["srow2"][0:1, 32:160], start=True, stop=True), sem=s_mm)
            ph.wait("scalar", s_mm, mtok)
            btok = ph.op("scalar", lambda h, o=rb[rbb][:, :]: h.copy(o, PS[7][:, 0:128]),
                         sem=s_act)
            ps7_free[0] = (s_act, btok)
            ph.wait("vector", s_act, btok)
            ph.wait("vector", s_mm, out_tok)
            ntok = ph.op("vector", lambda h, o=attn_sb[:, head, q0c:q0c + 128],
                         p=PS[4 + pout][:, 0:128], r=rb[rbb][:, :]:
                         h.tensor_tensor(o, p, r, OP.mult), sem=s_dve)
            ps45_free[pout] = (s_dve, ntok)
            rb_r.release(rbb, s_dve, ntok)
            prev_out_tok = out_tok
    attn_done = s_dve.n

    x_grp = x_ap.rearrange("(c p) t -> p c t", p=128)
    y_grp = y_ap.rearrange("(c p) t -> p c t", p=128)
    xr = Ring(2)
    ph.wait("tensor", s_dve, attn_done)
    for oc in range(KC):
        go, xto = load_wt("Wo", oc * 128)
        ph.wait("tensor", s_xb, xto)
        mtok = None
        for sl in range(NSLAB):
            s0 = sl * SLAB
            pb = sl % 2
            if ps01_free[pb] is not None:
                ph.wait("tensor", *ps01_free[pb])
            for kc in range(KC):
                last = kc == KC - 1
                mtok = ph.op("tensor", lambda h, w=wt[go][:, kc, :],
                             r=attn_sb[:, kc, s0:s0 + SLAB], p=PS[pb][:, :],
                             st=(kc == 0), sp=last:
                             h.matmul(p, w, r, start=st, stop=sp),
                             sem=s_mm if last else None)
            xb = xr.acquire(ph, "gpsimd")
            xtok = ph.dma("gpsimd", c["xs"][xb][:, 0, :], x_grp[:, oc, s0:s0 + SLAB], s_x)
            yb = ystg_r.acquire(ph, "vector")
            ph.wait("vector", s_mm, mtok)
            ph.wait("vector", s_x, xtok)
            vtok = ph.op("vector", lambda h, o=ystg[yb][:, :], p=PS[pb][:, :],
                         s1=bcols[:, 2, oc:oc + 1], xc=c["xs"][xb][:, 0, :]:
                         h.scalar_tensor_tensor(o, p, s1, xc, OP.add, OP.add), sem=s_dve)
            ps01_free[pb] = (s_dve, vtok)
            xr.release(xb, s_dve, vtok)
            ph.wait("gpsimd", s_dve, vtok)
            stok = ph.dma("gpsimd", y_grp[:, oc, s0:s0 + SLAB], ystg[yb][:, :], s_st)
            ystg_r.release(yb, s_st, stok)
        wt_r.release(go, s_mm, mtok)
    ph.wait("sync", s_st, s_st.n)
    ph.emit()
    return s_st.n



def emit_l0attn(ctx, c, pfx, x_ap, y_ap, wqc_ap, wqu_ap, wk_ap, wv_ap,
                qnw_ap, knw_ap, sink_ap, w1_ap, w2_ap, cos_ap, sin_ap,
                mdiag_ap, medge_ap, h_sb, attn_half, dbg=None):
    nc = ctx.nc
    PS = c["PS"]
    NH = 16
    WIN_T = 4
    INV_S = 1.0 / float(np.sqrt(128.0))
    EPS_RMS = 1e-6

    s_w = ctx.sem(pfx + "w")
    s_mm = ctx.sem(pfx + "mm")
    s_act = ctx.sem(pfx + "act")
    s_dve = ctx.sem(pfx + "dve")
    s_acc = ctx.sem(pfx + "acc")
    s_row = ctx.sem(pfx + "row")

    qc_sb = ctx.sbuf(pfx + "qc", [128, 4, T], BF16)
    wkv_t = ctx.sbuf(pfx + "wkv", [128, 16, 256], BF16)
    wqu_t = [ctx.sbuf(pfx + f"wqu{i}", [128, 4, 128], BF16) for i in range(2)]
    kfm = ctx.sbuf(pfx + "kfm", [128, T], BF16)
    qfm = ctx.sbuf(pfx + "qfm", [128, T], BF16)
    vtm = ctx.sbuf(pfx + "vtm", [128, 16, 128], BF16)
    kraw = ctx.sbuf(pfx + "kraw", [128, 512], F32)
    cosd = ctx.sbuf(pfx + "cosd", [64, T], BF16)
    sind = ctx.sbuf(pfx + "sind", [64, T], BF16)
    mdiag = ctx.sbuf(pfx + "mdiag", [128, 128], BF16)
    medge = ctx.sbuf(pfx + "medge", [128, 128], BF16)
    sinkr = ctx.sbuf(pfx + "sink", [1, 16], F32)
    ncol = ctx.sbuf(pfx + "ncol", [128, 2], F32)
    epsr = ctx.sbuf(pfx + "epsr", [1, 1], F32)
    sqb = [ctx.sbuf(pfx + f"sqb{i}", [128, 512], BF16) for i in range(2)]
    negmb = [ctx.sbuf(pfx + f"negmb{i}", [128, 512], BF16) for i in range(2)]
    rbt = [ctx.sbuf(pfx + f"rb{i}", [128, 128], F32) for i in range(2)]
    ptile = [ctx.sbuf(pfx + f"p{i}", [128, 128], BF16) for i in range(3)]
    stile = [ctx.sbuf(pfx + f"s{i}", [128, 128], F32) for i in range(2)]
    w2t = [ctx.sbuf(pfx + f"w2t{i}", [128, 16, 128], BF16) for i in range(2)]
    outstg = [ctx.sbuf(pfx + f"outstg{i}", [128, 1024], F32) for i in range(2)]

    ph = Ph(nc)
    ytok = ph.dma("sync", y_ap[:, :], x_ap[:, :], s_acc)
    ph.wait("gpsimd", c["init_sem"], c["init_tok"])
    ph.dma("gpsimd", cosd[0:32, :], cos_ap[:, :], s_w)
    ph.dma("gpsimd", cosd[32:64, :], cos_ap[:, :], s_w)
    ph.dma("gpsimd", sind[0:32, :], sin_ap[:, :], s_w)
    ph.dma("gpsimd", sind[32:64, :], sin_ap[:, :], s_w)
    ph.dma("gpsimd", mdiag[:, :], mdiag_ap[:, :], s_w)
    ph.dma("gpsimd", medge[:, :], medge_ap[:, :], s_w)
    ph.dma("gpsimd", sinkr[0:1, :], sink_ap.rearrange("(a b) -> a b", a=1), s_w)
    ph.op("gpsimd", lambda h: h.memset(epsr[:, :], EPS_RMS))
    for i, (nap, scl) in enumerate(((qnw_ap, INV_S), (knw_ap, None))):
        if s_row.n:
            ph.wait("gpsimd", s_row, s_row.n)
        btok = ph.dma("gpsimd", c["sq"][0][0:1, 0:128],
                      nap.rearrange("(a b) -> a b", a=1), s_row)
        ph.wait("tensor", s_row, btok)
        ph.wait("tensor", c["init_sem"], c["init_tok"])
        mtok = ph.op("tensor", lambda h: h.transpose(PS[7][0:128, 0:1], c["sq"][0][0:1, 0:128],
                                                     c["ident"][0:1, 0:1]), sem=s_mm)
        ph.wait("vector", s_mm, mtok)
        if scl is None:
            vtok = ph.op("vector", lambda h, o=ncol[:, i:i + 1]:
                         h.tensor_copy(o, PS[7][:, 0:1]), sem=s_dve)
        else:
            vtok = ph.op("vector", lambda h, o=ncol[:, i:i + 1], s2=scl:
                         h.tensor_scalar(o, PS[7][:, 0:1], s2, None, OP.mult), sem=s_dve)
        ph.wait("tensor", s_dve, vtok)
        ph.wait("gpsimd", s_dve, vtok)
    ph.emit()
    const_w = s_w.n

    h_done = emit_ln(ctx, c, pfx + "ln_", x_ap, h_sb)
    s_lndve = ctx.sem(pfx + "ln_dve")

    ph = Ph(nc)
    ph.wait("tensor", s_lndve, h_done)
    ph.wait("tensor", s_w, const_w)
    ph.wait("vector", s_w, const_w)
    ph.wait("scalar", s_w, const_w)
    wstg_r = Ring(2)
    sqb_r = Ring(2)
    ps01_free = [None, None]
    ps6_free = [None]
    ps7_free = [None]

    for sl in range(NSLAB):
        s0 = sl * SLAB
        gu_done = None
        for kc in range(KC):
            b1 = wstg_r.acquire(ph, "gpsimd")
            wtok = ph.dma("gpsimd", c["wstg"][b1][:, 0:512],
                          wqc_ap[kc * 128:(kc + 1) * 128, :], s_w)
            ph.wait("tensor", s_w, wtok)
            tok = None
            for oc in range(4):
                tok = ph.op("tensor", lambda h, w=c["wstg"][b1][:, oc * 128:(oc + 1) * 128],
                            r=h_sb[:, kc, s0:s0 + SLAB], p=PS[oc][:, :],
                            st=(kc == 0), sp=(kc == KC - 1):
                            h.matmul(p, w, r, start=st, stop=(kc == KC - 1)),
                            sem=s_mm if oc == 3 else None)
            wstg_r.release(b1, s_mm, tok)
            if kc == KC - 1:
                gu_done = tok
        vtok = None
        for oc in range(4):
            ph.wait("vector", s_mm, gu_done)
            vtok = ph.op("vector", lambda h, o=qc_sb[:, oc, s0:s0 + SLAB], p=PS[oc][:, :]:
                         h.tensor_copy(o, p), sem=s_dve)
        ph.wait("tensor", s_dve, vtok)
    wtok = ph.dma("gpsimd", wkv_t[:, :, 0:128], wk_ap.rearrange("(c p) d -> p c d", p=128), s_w)
    wtok2 = ph.dma("gpsimd", wkv_t[:, :, 128:256], wv_ap.rearrange("(c p) d -> p c d", p=128), s_w)
    ph.wait("tensor", s_w, wtok)
    k_ps_guard = None
    k_sq_tok = None
    for sl in range(NSLAB):
        s0 = sl * SLAB
        if k_ps_guard is not None:
            ph.wait("tensor", s_dve, k_ps_guard)
        mtok = None
        for kc in range(KC):
            last = kc == KC - 1
            mtok = ph.op("tensor", lambda h, w=wkv_t[:, kc, 0:128], r=h_sb[:, kc, s0:s0 + SLAB],
                         p=PS[0][:, :], st=(kc == 0), sp=last:
                         h.matmul(p, w, r, start=st, stop=sp), sem=s_mm if last else None)
        ph.wait("vector", s_mm, mtok)
        if k_sq_tok is not None:
            ph.wait("vector", s_act, k_sq_tok)
        vtok = ph.op("vector", lambda h: h.tensor_copy(kraw[:, :], PS[0][:, :]), sem=s_dve)
        k_ps_guard = vtok
        x1 = kraw[0:32, :]
        x2 = kraw[32:64, :]
        pa0 = c["sq"][0][0:32, :]
        pb0 = c["sq"][1][0:32, :]
        pa1 = c["sq"][0][32:64, :]
        pb1 = c["sq"][1][32:64, :]
        ph.op("vector", lambda h, o=pa0, a=x1, b=cosd[0:32, s0:s0 + SLAB]:
              h.tensor_tensor(o, a, b, OP.mult))
        ph.op("vector", lambda h, o=pb0, a=x2, b=sind[32:64, s0:s0 + SLAB]:
              h.tensor_tensor(o, a, b, OP.mult))
        ph.op("vector", lambda h, o=pa1, a=x1, b=sind[0:32, s0:s0 + SLAB]:
              h.tensor_tensor(o, a, b, OP.mult))
        ph.op("vector", lambda h, o=pb1, a=x2, b=cosd[32:64, s0:s0 + SLAB]:
              h.tensor_tensor(o, a, b, OP.mult))
        ph.op("vector", lambda h, o=kraw[0:32, :], a=pa0, b=pb0:
              h.tensor_tensor(o, a, b, OP.subtract))
        rtok = ph.op("vector", lambda h, o=kraw[32:64, :], a=pa1, b=pb1:
              h.tensor_tensor(o, a, b, OP.add), sem=s_dve)
        sb = sqb_r.acquire(ph, "scalar")
        ph.wait("scalar", s_dve, rtok)
        atok = ph.op("scalar", lambda h, o=sqb[sb][:, :]: h.activation(o, kraw[:, :], AF.Square),
                     sem=s_act)
        k_sq_tok = atok
        ph.wait("tensor", s_act, atok)
        if ps6_free[0] is not None:
            ph.wait("tensor", *ps6_free[0])
        mtok = ph.op("tensor", lambda h, i=sqb[sb][:, :]:
                     h.matmul(PS[6][0:1, :], c["ones_col_bf"][:, :], i, start=True, stop=True),
                     sem=s_mm)
        sqb_r.release(sb, s_mm, mtok)
        ph.wait("vector", s_mm, mtok)
        vtok = ph.op("vector", lambda h: h.tensor_scalar(c["srow"][0:1, :], PS[6][0:1, :],
                                                         1.0 / 128.0, None, OP.mult), sem=s_dve)
        ps6_free[0] = (s_dve, vtok)
        ph.wait("scalar", s_dve, vtok)
        atok = ph.op("scalar", lambda h: h.activation(c["srow2"][0:1, :], c["srow"][0:1, :],
                                                      AF.Sqrt, bias=epsr[0:1, :]), sem=s_act)
        ph.wait("vector", s_act, atok)
        rtok2 = ph.op("vector", lambda h: h.reciprocal(c["srow"][0:1, :], c["srow2"][0:1, :]),
                      sem=s_dve)
        if ps7_free[0] is not None:
            ph.wait("tensor", *ps7_free[0])
        ph.wait("tensor", s_dve, rtok2)
        mtok = ph.op("tensor", lambda h: h.matmul(PS[7][:, :], c["ones_row"][:, :],
                                                  c["srow"][0:1, :], start=True, stop=True),
                     sem=s_mm)
        ph.wait("vector", s_mm, mtok)
        ph.op("vector", lambda h: h.tensor_tensor(c["sq"][0][:, :], kraw[:, :], PS[7][:, :],
                                                  OP.mult))
        vtok = ph.op("vector", lambda h, o=kfm[:, s0:s0 + SLAB]:
                     h.tensor_scalar(o, c["sq"][0][:, :], ncol[:, 1:2], None, OP.mult),
                     sem=s_dve)
        ps7_free[0] = (s_dve, vtok)
    k_done = s_dve.n
    ph.wait("tensor", s_w, wtok2)
    ps01_free[0] = (s_dve, k_ps_guard)
    for tc in range(KC):
        t0c = tc * 128
        pb = tc % 2
        if ps01_free[pb] is not None:
            ph.wait("tensor", *ps01_free[pb])
        mtok = None
        for kc in range(KC):
            last = kc == KC - 1
            mtok = ph.op("tensor", lambda h, w=h_sb[:, kc, t0c:t0c + 128], r=wkv_t[:, kc, 128:256],
                         p=PS[pb][:, 0:128], st=(kc == 0), sp=last:
                         h.matmul(p, w, r, start=st, stop=sp), sem=s_mm if last else None)
        ph.wait("scalar", s_mm, mtok)
        atok = ph.op("scalar", lambda h, o=vtm[:, tc, :], p=PS[pb][:, 0:128]: h.copy(o, p),
                     sem=s_act)
        ps01_free[pb] = (s_act, atok)
    v_done = s_act.n
    for sl in range(NSLAB):
        s0 = sl * SLAB
        sb = sqb_r.acquire(ph, "scalar")
        if sl == 0:
            ph.wait("scalar", s_dve, k_done)
        atok = ph.op("scalar", lambda h, o=sqb[sb][:, :], i=kfm[:, s0:s0 + SLAB]:
                     h.activation(o, i, AF.Square), sem=s_act)
        ph.wait("tensor", s_act, atok)
        if ps6_free[0] is not None:
            ph.wait("tensor", *ps6_free[0])
        mtok = ph.op("tensor", lambda h, i=sqb[sb][:, :]:
                     h.matmul(PS[6][0:1, :], c["ones_col_bf"][:, :], i, start=True, stop=True),
                     sem=s_mm)
        sqb_r.release(sb, s_mm, mtok)
        ph.wait("vector", s_mm, mtok)
        vtok = ph.op("vector", lambda h, o=c["mean"][0:1, s0:s0 + SLAB]:
                     h.tensor_copy(o, PS[6][0:1, :]), sem=s_dve)
        ps6_free[0] = (s_dve, vtok)
    ph.wait("scalar", s_dve, s_dve.n)
    antok = ph.op("scalar", lambda h: h.activation(c["mean"][0:1, :], c["mean"][0:1, :], AF.Sqrt),
                  sem=s_act)
    ph.wait("vector", s_act, antok)
    gtok_ = ph.op("vector", lambda h: h.tensor_reduce(
        c["srow"][0:1, 0:16],
        c["mean"][0:1, :].rearrange("p (a b) -> p a b", b=128), AX.X, OP.max), sem=s_dve)
    ph.wait("vector", s_dve, gtok_)
    ph.gop("vector", lambda h: h.tensor_copy(c["srow"][0:1, 16:32], c["srow"][0:1, 0:16]),
           s_dve)
    for j in range(1, 5):
        ph.gop("vector", lambda h, j=j: h.tensor_tensor(
            c["srow"][0:1, 16 + j:32], c["srow"][0:1, 16 + j:32],
            c["srow"][0:1, 0:16 - j], OP.max), s_dve)
    kwin_tok = ph.gop("vector", lambda h: h.tensor_copy(c["srow2"][0:1, 0:1],
                                                        c["srow"][0:1, 16:17]), s_dve)
    ph.emit()

    prev_out_tok = None
    acc_tok_prev_grp = ytok
    for half in range(2):
        ph = Ph(nc)
        wqu_r = Ring(2)
        negmb_r = Ring(2)
        rb_r = Ring(2)
        p_r = Ring(3)
        s_r = Ring(2)
        ps01_free_h = [None, None]
        ps23_free = [None, None]
        ps45_free = [None, None]
        q_sq_tok = None
        for hh in range(8):
            head = half * 8 + hh
            gq = wqu_r.acquire(ph, "gpsimd")
            wtok = ph.dma("gpsimd", wqu_t[gq][:, :, :],
                          wqu_ap.rearrange("(c p) o -> p c o", p=128)[:, :, head * 128:(head + 1) * 128],
                          s_w)
            ph.wait("tensor", s_w, wtok)
            for sl in range(NSLAB):
                s0 = sl * SLAB
                pb = sl % 2
                if ps01_free_h[pb] is not None:
                    ph.wait("tensor", *ps01_free_h[pb])
                elif ps01_free[pb] is not None:
                    ph.wait("tensor", *ps01_free[pb])
                mtok = None
                for kc in range(4):
                    last = kc == 3
                    mtok = ph.op("tensor", lambda h, w=wqu_t[gq][:, kc, :],
                                 r=qc_sb[:, kc, s0:s0 + SLAB], p=PS[pb][:, :],
                                 st=(kc == 0), sp=last:
                                 h.matmul(p, w, r, start=st, stop=sp),
                                 sem=s_mm if last else None)
                ph.wait("vector", s_mm, mtok)
                if sl == 0 and prev_out_tok is not None:
                    ph.wait("vector", s_mm, prev_out_tok)
                if q_sq_tok is not None:
                    ph.wait("vector", s_act, q_sq_tok)
                vtok = ph.op("vector", lambda h, p=PS[pb][:, :]: h.tensor_copy(kraw[:, :], p),
                             sem=s_dve)
                ps01_free_h[pb] = (s_dve, vtok)
                x1 = kraw[0:32, :]
                x2 = kraw[32:64, :]
                pa0 = c["sq"][0][0:32, :]
                pb0 = c["sq"][1][0:32, :]
                pa1 = c["sq"][0][32:64, :]
                pb1 = c["sq"][1][32:64, :]
                s0sl = s0
                ph.op("vector", lambda h, o=pa0, a=x1, b=cosd[0:32, s0sl:s0sl + SLAB]:
                      h.tensor_tensor(o, a, b, OP.mult))
                ph.op("vector", lambda h, o=pb0, a=x2, b=sind[32:64, s0sl:s0sl + SLAB]:
                      h.tensor_tensor(o, a, b, OP.mult))
                ph.op("vector", lambda h, o=pa1, a=x1, b=sind[0:32, s0sl:s0sl + SLAB]:
                      h.tensor_tensor(o, a, b, OP.mult))
                ph.op("vector", lambda h, o=pb1, a=x2, b=cosd[32:64, s0sl:s0sl + SLAB]:
                      h.tensor_tensor(o, a, b, OP.mult))
                ph.op("vector", lambda h, o=kraw[0:32, :], a=pa0, b=pb0:
                      h.tensor_tensor(o, a, b, OP.subtract))
                rtok = ph.op("vector", lambda h, o=kraw[32:64, :], a=pa1, b=pb1:
                             h.tensor_tensor(o, a, b, OP.add), sem=s_dve)
                sb = sqb_r.acquire(ph, "scalar")
                ph.wait("scalar", s_dve, rtok)
                atok = ph.op("scalar", lambda h, o=sqb[sb][:, :]:
                             h.activation(o, kraw[:, :], AF.Square), sem=s_act)
                q_sq_tok = atok
                ph.wait("tensor", s_act, atok)
                if ps6_free[0] is not None:
                    ph.wait("tensor", *ps6_free[0])
                mtok = ph.op("tensor", lambda h, i=sqb[sb][:, :]:
                             h.matmul(PS[6][0:1, :], c["ones_col_bf"][:, :], i,
                                      start=True, stop=True), sem=s_mm)
                sqb_r.release(sb, s_mm, mtok)
                ph.wait("vector", s_mm, mtok)
                vtok = ph.op("vector", lambda h: h.tensor_scalar(c["srow2"][0:1, :], PS[6][0:1, :],
                                                                 1.0 / 128.0, None, OP.mult),
                             sem=s_dve)
                ps6_free[0] = (s_dve, vtok)
                ph.wait("scalar", s_dve, vtok)
                atok = ph.op("scalar", lambda h: h.activation(c["srow2"][0:1, :], c["srow2"][0:1, :],
                                                              AF.Sqrt, bias=epsr[0:1, :]), sem=s_act)
                ph.wait("vector", s_act, atok)
                rtok2 = ph.op("vector", lambda h: h.reciprocal(c["srow2"][0:1, :], c["srow2"][0:1, :]),
                              sem=s_dve)
                if ps7_free[0] is not None:
                    ph.wait("tensor", *ps7_free[0])
                ph.wait("tensor", s_dve, rtok2)
                mtok = ph.op("tensor", lambda h: h.matmul(PS[7][:, :], c["ones_row"][:, :],
                                                          c["srow2"][0:1, :], start=True, stop=True),
                             sem=s_mm)
                ph.wait("vector", s_mm, mtok)
                ph.op("vector", lambda h: h.tensor_tensor(c["sq"][0][:, :], kraw[:, :],
                                                          PS[7][:, :], OP.mult))
                vtok = ph.op("vector", lambda h, o=qfm[:, s0sl:s0sl + SLAB]:
                             h.tensor_scalar(o, c["sq"][0][:, :], ncol[:, 0:1], None, OP.mult),
                             sem=s_dve)
                ps7_free[0] = (s_dve, vtok)
            q_done = s_dve.n
            wqu_r.release(gq, s_dve, q_done)
            for sl in range(NSLAB):
                s0 = sl * SLAB
                sb = sqb_r.acquire(ph, "scalar")
                if sl == 0:
                    ph.wait("scalar", s_dve, q_done)
                atok = ph.op("scalar", lambda h, o=sqb[sb][:, :], i=qfm[:, s0:s0 + SLAB]:
                             h.activation(o, i, AF.Square), sem=s_act)
                ph.wait("tensor", s_act, atok)
                if ps6_free[0] is not None:
                    ph.wait("tensor", *ps6_free[0])
                mtok = ph.op("tensor", lambda h, i=sqb[sb][:, :]:
                             h.matmul(PS[6][0:1, :], c["ones_col_bf"][:, :], i,
                                      start=True, stop=True), sem=s_mm)
                sqb_r.release(sb, s_mm, mtok)
                ph.wait("vector", s_mm, mtok)
                vtok = ph.op("vector", lambda h, o=c["mean"][0:1, s0:s0 + SLAB]:
                             h.tensor_copy(o, PS[6][0:1, :]), sem=s_dve)
                ps6_free[0] = (s_dve, vtok)
            ph.wait("scalar", s_dve, s_dve.n)
            qn_tok = ph.op("scalar", lambda h: h.activation(c["mean"][0:1, :], c["mean"][0:1, :],
                                                            AF.Sqrt), sem=s_act)
            ph.wait("vector", s_act, qn_tok)
            negm_tok = None
            for qt in range(16):
                negm_tok = ph.op("vector", lambda h, qt=qt: h.tensor_scalar(
                    c["rstd"][0:1, qt * 128:(qt + 1) * 128],
                    c["mean"][0:1, qt * 128:(qt + 1) * 128],
                    c["srow"][0:1, 16 + qt:17 + qt], -1.0, OP.mult, OP.mult),
                    sem=s_dve if qt == 15 else None)
            negmb_cur = None
            negmb_tok = None
            last_sub_tok = None
            for qt in range(16):
                q0c = qt * 128
                slb = qt // 4
                if qt % 4 == 0:
                    if negmb_cur is not None:
                        negmb_r.release(negmb_cur, s_dve, last_sub_tok)
                    nb = negmb_r.acquire(ph, "tensor")
                    if ps7_free[0] is not None:
                        ph.wait("tensor", *ps7_free[0])
                    ph.wait("tensor", s_dve, negm_tok)
                    mtok = ph.op("tensor", lambda h, i=c["rstd"][0:1, slb * 512:(slb + 1) * 512]:
                                 h.matmul(PS[7][:, :], c["ones_row"][:, :], i,
                                          start=True, stop=True), sem=s_mm)
                    ph.wait("scalar", s_mm, mtok)
                    btok = ph.op("scalar", lambda h, o=negmb[nb][:, :]:
                                 h.copy(o, PS[7][:, :]), sem=s_act)
                    ps7_free[0] = (s_act, btok)
                    negmb_tok = btok
                    negmb_cur = nb
                if ps6_free[0] is not None:
                    ph.wait("tensor", *ps6_free[0])
                pout = qt % 2
                if ps45_free[pout] is not None:
                    ph.wait("tensor", *ps45_free[pout])
                st0 = max(0, qt - WIN_T)
                out_tok = None
                dn_tok = None
                pend = None

                def flush_pend(pend):
                    pt, st_p, atok = pend
                    ph.wait("tensor", s_act, atok)
                    if st_p == st0 and head == 0 and qt < 2:
                        ph.wait("tensor", s_act, v_done)
                    dn = ph.op("tensor", lambda h, r=ptile[pt][:, :],
                               st_=(st_p == st0), sp=(st_p == qt):
                               h.matmul(PS[6][0:1, 0:128], c["ones_col_bf"][:, :], r,
                                        start=st_, stop=sp),
                               sem=s_mm if st_p == qt else None)
                    ot = ph.op("tensor", lambda h, w=vtm[:, st_p, :], r=ptile[pt][:, :],
                               p=PS[4 + pout][:, 0:128], st_=(st_p == st0), sp=(st_p == qt):
                               h.matmul(p, w, r, start=st_, stop=sp), sem=s_mm)
                    p_r.release(pt, s_mm, ot)
                    return dn, ot

                for st in range(st0, qt + 1):
                    psc = s_r.acquire(ph, "vector")
                    pb23 = st % 2
                    if ps23_free[pb23] is not None:
                        ph.wait("tensor", *ps23_free[pb23])
                    if head == 0 and qt == 0:
                        ph.wait("tensor", s_dve, kwin_tok)
                    mtok = ph.op("tensor", lambda h, w=kfm[:, st * 128:(st + 1) * 128],
                                 r=qfm[:, q0c:q0c + 128], p=PS[2 + pb23][:, 0:128]:
                                 h.matmul(p, w, r, start=True, stop=True), sem=s_mm)
                    if pend is not None:
                        dn_tok, out_tok = flush_pend(pend)
                    ph.wait("vector", s_mm, mtok)
                    if st == st0:
                        ph.wait("vector", s_act, negmb_tok)
                    vtok = ph.op("vector", lambda h, o=stile[psc][:, :], p=PS[2 + pb23][:, 0:128],
                                 nm=negmb[negmb_cur][:, (qt % 4) * 128:(qt % 4 + 1) * 128]:
                                 h.tensor_tensor(o, p, nm, OP.add), sem=s_dve)
                    ps23_free[pb23] = (s_dve, vtok)
                    last_sub_tok = vtok
                    msk = None
                    if st == qt:
                        msk = mdiag
                    elif st == qt - WIN_T:
                        msk = medge
                    if msk is not None:
                        vtok = ph.op("vector", lambda h, o=stile[psc][:, :], m=msk:
                                     h.tensor_tensor(o, o, m[:, :], OP.add), sem=s_dve)
                        last_sub_tok = vtok
                    pt = p_r.acquire(ph, "scalar")
                    ph.wait("scalar", s_dve, vtok)
                    atok = ph.op("scalar", lambda h, o=ptile[pt][:, :], i=stile[psc][:, :]:
                                 h.activation(o, i, AF.Exp), sem=s_act)
                    s_r.release(psc, s_act, atok)
                    pend = (pt, st, atok)
                dn_tok, out_tok = flush_pend(pend)
                sk = ph.op("scalar", lambda h, o=c["srow2"][0:1, 128:256],
                           i=c["rstd"][0:1, q0c:q0c + 128], b=sinkr[0:1, head:head + 1]:
                           h.activation(o, i, AF.Exp, bias=b), sem=s_act)
                ph.wait("vector", s_mm, dn_tok)
                ph.wait("vector", s_act, sk)
                vtok = ph.gop("vector", lambda h, o=c["mean"][0:1, q0c:q0c + 128]:
                              h.tensor_tensor(o, PS[6][0:1, 0:128], c["srow2"][0:1, 128:256],
                                              OP.add), s_dve)
                ps6_free[0] = (s_dve, vtok)
                rtok = ph.gop("vector", lambda h, o=c["srow2"][0:1, 256:384],
                              i=c["mean"][0:1, q0c:q0c + 128]: h.reciprocal(o, i), s_dve)
                rbb = rb_r.acquire(ph, "tensor")
                if ps7_free[0] is not None:
                    ph.wait("tensor", *ps7_free[0])
                ph.wait("tensor", s_dve, rtok)
                mtok = ph.op("tensor", lambda h:
                             h.matmul(PS[7][:, 0:128], c["ones_row"][:, :],
                                      c["srow2"][0:1, 256:384], start=True, stop=True), sem=s_mm)
                ph.wait("scalar", s_mm, mtok)
                btok = ph.op("scalar", lambda h, o=rbt[rbb][:, :]: h.copy(o, PS[7][:, 0:128]),
                             sem=s_act)
                ps7_free[0] = (s_act, btok)
                ph.wait("vector", s_act, btok)
                ph.wait("vector", s_mm, out_tok)
                ntok = ph.op("vector", lambda h, o=attn_half[:, hh, q0c:q0c + 128],
                             p=PS[4 + pout][:, 0:128], r=rbt[rbb][:, :]:
                             h.tensor_tensor(o, p, r, OP.mult), sem=s_dve)
                ps45_free[pout] = (s_dve, ntok)
                rb_r.release(rbb, s_dve, ntok)
                prev_out_tok = out_tok
        attn_half_done = s_dve.n

        ph.wait("tensor", s_dve, attn_half_done)
        w2_r = Ring(2)
        out_r = Ring(2)
        psd_free = list(ps01_free_h)
        for gg in range(2):
            g = half * 2 + gg
            for oc in range(KC):
                b1 = wstg_r.acquire(ph, "gpsimd")
                wtok = ph.dma("gpsimd", c["wstg"][b1][:, 0:512],
                              w1_ap[g, :, :].rearrange("(c p) o -> p c o", p=128)
                              [:, :, oc * 128:(oc + 1) * 128], s_w)
                ph.wait("tensor", s_w, wtok)
                for sl in range(NSLAB):
                    s0 = sl * SLAB
                    pb = sl % 2
                    if psd_free[pb] is not None:
                        ph.wait("tensor", *psd_free[pb])
                    mtok = None
                    for kc in range(4):
                        last = kc == 3
                        mtok = ph.op("tensor", lambda h, w=c["wstg"][b1][:, kc * 128:(kc + 1) * 128],
                                     r=attn_half[:, gg * 4 + kc, s0:s0 + SLAB], p=PS[pb][:, :],
                                     st=(kc == 0), sp=last:
                                     h.matmul(p, w, r, start=st, stop=sp),
                                     sem=s_mm if last else None)
                    ph.wait("vector", s_mm, mtok)
                    vtok = ph.op("vector", lambda h, o=h_sb[:, oc, s0:s0 + SLAB], p=PS[pb][:, :]:
                                 h.tensor_copy(o, p), sem=s_dve)
                    psd_free[pb] = (s_dve, vtok)
                wstg_r.release(b1, s_mm, mtok)
            tmp_done = s_dve.n
            ph.wait("tensor", s_dve, tmp_done)
            for oc in range(KC):
                gw = w2_r.acquire(ph, "gpsimd")
                wtok = ph.dma("gpsimd", w2t[gw][:, :, :],
                              w2_ap[g * 2048:(g + 1) * 2048, oc * 128:(oc + 1) * 128]
                              .rearrange("(c p) o -> p c o", p=128), s_w)
                ph.wait("tensor", s_w, wtok)
                mtok = etok = None
                for hf in range(2):
                    ob = out_r.acquire(ph, "scalar")
                    for hs in range(2):
                        sl = hf * 2 + hs
                        s0 = sl * SLAB
                        pb = sl % 2
                        if psd_free[pb] is not None:
                            ph.wait("tensor", *psd_free[pb])
                        for kc in range(KC):
                            last = kc == KC - 1
                            mtok = ph.op("tensor", lambda h, w=w2t[gw][:, kc, :],
                                         r=h_sb[:, kc, s0:s0 + SLAB], p=PS[pb][:, :],
                                         st=(kc == 0), sp=last:
                                         h.matmul(p, w, r, start=st, stop=sp),
                                         sem=s_mm if last else None)
                        ph.wait("scalar", s_mm, mtok)
                        etok = ph.op("scalar", lambda h, o=outstg[ob][:, hs * 512:(hs + 1) * 512],
                                     p=PS[pb][:, :]: h.copy(o, p), sem=s_act)
                        psd_free[pb] = (s_act, etok)
                    ph.wait("gpsimd", s_act, etok)
                    ph.wait("gpsimd", s_acc, acc_tok_prev_grp)
                    acctok = ph.dma("gpsimd",
                                    y_ap[oc * 128:(oc + 1) * 128, hf * 1024:(hf + 1) * 1024],
                                    outstg[ob][:, :], s_acc, accum_op=OP.add)
                    out_r.release(ob, s_acc, acctok)
                w2_r.release(gw, s_mm, mtok)
            acc_tok_prev_grp = s_acc.n
        ph.emit()
    ph = Ph(nc)
    ph.wait("sync", s_acc, s_acc.n)
    if dbg is not None:
        s_st = ctx.sem(pfx + "dbgst")
        ph.wait("scalar", s_acc, s_acc.n)
        for name, src_t in (("qc", qc_sb), ("kfm", kfm), ("vtm", vtm), ("qfm", qfm),
                            ("ncol", ncol), ("attnh", attn_half)):
            if name in dbg:
                ph.dma("scalar", dbg[name], src_t[tuple([slice(None)] * len(src_t.shape))]
                       if hasattr(src_t, "shape") else src_t[:, :], s_st)
        ph.wait("scalar", s_st, s_st.n)
    ph.emit()
    return s_acc.n


def l1_rope_tables():
    d = 128
    j = np.arange(d // 2, dtype=np.float32)
    thetas = np.float32(10000.0) ** (np.float32(-2.0) * j / np.float32(d))
    m = np.arange(1, T + 1, dtype=np.float32)
    ang = m[None, :] * thetas[:, None]
    return np.cos(ang).astype(np.float32), np.sin(ang).astype(np.float32)


def causal_mask_tile():
    s = np.arange(128)[:, None]
    q = np.arange(128)[None, :]
    return np.where(s <= q, 0.0, -1e30).astype(np.float32)


def build_l1attn_program():
    nc = bass.Bass("TRN2", target_bir_lowering=False)
    x = nc.dram_tensor("x", [D, T], F32, kind="ExternalInput").ap()
    w_aps = {n: nc.dram_tensor(n, [D, D], F32, kind="ExternalInput").ap()
             for n in ("Wq", "Wk", "Wv", "Wo")}
    b_aps = {n: nc.dram_tensor(n, [D], F32, kind="ExternalInput").ap()
             for n in ("bq", "bk", "bv", "bo")}
    cos = nc.dram_tensor("rope_cos", [64, T], F32, kind="ExternalInput").ap()
    sin = nc.dram_tensor("rope_sin", [64, T], F32, kind="ExternalInput").ap()
    cm = nc.dram_tensor("cmask", [128, 128], F32, kind="ExternalInput").ap()
    y = nc.dram_tensor("y", [D, T], F32, kind="ExternalOutput").ap()
    with ExitStack() as stack:
        ctx = Ctx(nc, stack)
        c = make_common(ctx)
        h_sb = ctx.sbuf("h", [128, KC, T], BF16)
        attn_sb = ctx.sbuf("attn", [128, 16, T], BF16)
        emit_l1attn(ctx, c, "a1_", x, y, w_aps, b_aps, cos, sin, cm, h_sb, attn_sb)
    return nc


def emit_l0ffn(ctx, c, pfx, x_ap, y_ap, gate_ap, w1_ap, w2_ap, sw1_ap, sw2_ap,
               sw3_ap, h_dram_ap, cw_dram_ap):
    nc = ctx.nc
    PS = c["PS"]
    NE = 8
    NIC = 11
    NSL = 2
    SL = 1024

    s_w = ctx.sem(pfx + "w")
    s_xb = ctx.sem(pfx + "xb")
    s_mm = ctx.sem(pfx + "mm")
    s_act = ctx.sem(pfx + "act")
    s_dve = ctx.sem(pfx + "dve")
    s_acc = ctx.sem(pfx + "acc")
    s_h = ctx.sem(pfx + "h")
    s_cwr = ctx.sem(pfx + "cwr")
    s_cws = ctx.sem(pfx + "cws")
    s_g8 = ctx.sem(pfx + "g8")

    hslab = [ctx.sbuf(pfx + f"hslab{i}", [128, KC, SL], BF16) for i in range(2)]
    act_sb = ctx.sbuf(pfx + "act", [128, NIC, SL], BF16)
    w1g = [ctx.sbuf(pfx + f"w1g{i}", [128, 16, 128], BF16) for i in range(2)]
    w1u = [ctx.sbuf(pfx + f"w1u{i}", [128, 16, 128], BF16) for i in range(2)]
    w2t = [ctx.sbuf(pfx + f"w2t{i}", [128, 11, 128], BF16) for i in range(1)]
    w2full = ctx.sbuf(pfx + "w2full", [128, 11, 2048], BF16)
    sil = [ctx.sbuf(pfx + f"sil{i}", [128, 512], F32) for i in range(2)]
    outstg = [ctx.sbuf(pfx + f"outstg{i}", [128, 512], F32) for i in range(2)]
    gT = ctx.sbuf(pfx + "gT", [128, 16, 8], F32)
    lg = ctx.sbuf(pfx + "lg", [128, 8], F32)
    r8a = ctx.sbuf(pfx + "r8a", [128, 8], F32)
    r8b = ctx.sbuf(pfx + "r8b", [128, 8], F32)
    col1 = ctx.sbuf(pfx + "col1", [128, 1], F32)
    col2 = ctx.sbuf(pfx + "col2", [128, 1], F32)
    col3 = ctx.sbuf(pfx + "col3", [128, 1], F32)
    cwpad = ctx.sbuf(pfx + "cwpad", [128, 128], F32)
    cwstage = ctx.sbuf(pfx + "cwstage", [1, 1024], BF16)
    cwrow = ctx.sbuf(pfx + "cwrow", [1, T], BF16)
    cwb_sb = ctx.sbuf(pfx + "cwb", [128, 1024], F32)
    hstg = [ctx.sbuf(pfx + f"hstg{i}", [128, 512], BF16) for i in range(2)]

    ph = Ph(nc)
    ytok = ph.dma("sync", y_ap[:, :], x_ap[:, :], s_acc)
    ph.emit()

    ph = Ph(nc)
    ph.wait("gpsimd", c["init_sem"], c["init_tok"])
    gstg = c["sq"][0]
    prev = None
    for kc in range(KC):
        if s_g8.n:
            ph.wait("gpsimd", s_g8, s_g8.n)
        wtok = ph.dma("gpsimd", gstg[0:8, 0:128], gate_ap[:, kc * 128:(kc + 1) * 128], s_g8)
        ph.wait("tensor", s_g8, wtok)
        ph.wait("tensor", c["init_sem"], c["init_tok"])
        if prev is not None:
            ph.wait("tensor", s_dve, prev)
        mtok = ph.op("tensor", lambda h: h.transpose(PS[7][0:128, 0:8], gstg[0:8, 0:128],
                                                     c["ident"][0:8, 0:8]), sem=s_mm)
        ph.wait("vector", s_mm, mtok)
        prev = ph.op("vector", lambda h, o=gT[:, kc, :]: h.tensor_copy(o, PS[7][0:128, 0:8]),
                     sem=s_dve)
        ph.wait("gpsimd", s_dve, prev)
    ph.emit()
    gT_done = prev

    s_lnx = ctx.sem(pfx + "lnx")
    s_lnmm = ctx.sem(pfx + "lnmm")
    s_lnact = ctx.sem(pfx + "lnact")
    s_lndve = ctx.sem(pfx + "lndve")
    xr = Ring(2)
    sqr = Ring(2)
    x_grp = x_ap.rearrange("(c p) t -> p c t", p=128)
    h_grp = h_dram_ap.rearrange("(s c p) t -> p s c t", p=128, c=KC)

    ph = Ph(nc)
    ph.wait("tensor", c["init_sem"], c["init_tok"])
    stat_free_tok = None
    for sl in range(NSLAB):
        s0 = sl * SLAB
        if stat_free_tok is not None:
            ph.wait("tensor", s_lndve, stat_free_tok)
        for kc in range(KC):
            b = xr.acquire(ph, "gpsimd")
            xtok = ph.dma("gpsimd", c["xs"][b][:, 0, :], x_grp[:, kc, s0:s0 + SLAB], s_lnx)
            sb = sqr.acquire(ph, "scalar")
            ph.wait("scalar", s_lnx, xtok)
            atok = ph.op("scalar", lambda h, o=c["sq"][sb][:, :], i=c["xs"][b][:, 0, :]:
                         h.activation(o, i, AF.Square), sem=s_lnact)
            ph.wait("tensor", s_lnx, xtok)
            ph.op("tensor", lambda h, i=c["xs"][b][:, 0, :], st=(kc == 0), sp=(kc == KC - 1):
                  h.matmul(PS[0][0:1, :], c["ones_col"][:, :], i, start=st, stop=sp))
            ph.wait("tensor", s_lnact, atok)
            mtok = ph.op("tensor", lambda h, i=c["sq"][sb][:, :], st=(kc == 0), sp=(kc == KC - 1):
                         h.matmul(PS[1][0:1, :], c["ones_col"][:, :], i, start=st, stop=sp),
                         sem=s_lnmm)
            sqr.release(sb, s_lnmm, mtok)
            xr.release(b, s_lnmm, mtok)
        sums_done = mtok
        ph.wait("vector", s_lnmm, sums_done)
        ph.op("vector", lambda h, o=c["mean"][0:1, s0:s0 + SLAB]:
              h.tensor_scalar(o, PS[0][0:1, :], 1.0 / D, None, OP.mult))
        ph.op("vector", lambda h:
              h.tensor_scalar(c["srow"][0:1, :], PS[1][0:1, :], 1.0 / D, None, OP.mult))
        ph.op("vector", lambda h, m=c["mean"][0:1, s0:s0 + SLAB]:
              h.tensor_tensor(c["srow2"][0:1, :], m, m, OP.mult))
        vtok = ph.op("vector", lambda h:
                     h.tensor_tensor(c["srow"][0:1, :], c["srow"][0:1, :],
                                     c["srow2"][0:1, :], OP.subtract), sem=s_lndve)
        stat_free_tok = vtok
        ph.wait("scalar", s_lndve, vtok)
        atok = ph.op("scalar", lambda h:
                     h.activation(c["srow2"][0:1, :], c["srow"][0:1, :], AF.Sqrt,
                                  bias=c["eps_ln"][0:1, :]), sem=s_lnact)
        ph.wait("vector", s_lnact, atok)
        ph.op("vector", lambda h, o=c["rstd"][0:1, s0:s0 + SLAB]:
              h.reciprocal(o, c["srow2"][0:1, :]), sem=s_lndve)
    rstd_tok = s_lndve.n
    bc_free = rstd_tok
    prev_cw_dma = None
    ps7g = (s_dve, gT_done)
    ps6g = None
    ph.wait("tensor", s_dve, gT_done)
    for sl in range(NSLAB):
        s0 = sl * SLAB
        ph.wait("tensor", s_lndve, bc_free)
        ph.op("tensor", lambda h, i=c["mean"][0:1, s0:s0 + SLAB]:
              h.matmul(PS[2][:, :], c["ones_row"][:, :], i, start=True, stop=True))
        btok = ph.op("tensor", lambda h, i=c["rstd"][0:1, s0:s0 + SLAB]:
                     h.matmul(PS[3][:, :], c["ones_row"][:, :], i, start=True, stop=True),
                     sem=s_lnmm)
        ph.wait("vector", s_lnmm, btok)
        htoks = []
        for kc in range(KC):
            b = xr.acquire(ph, "gpsimd")
            xtok = ph.dma("gpsimd", c["xs"][b][:, 0, :], x_grp[:, kc, s0:s0 + SLAB], s_lnx)
            hb = kc % 2
            if len(htoks) >= 2:
                ph.wait("vector", s_h, htoks[-2][0])
            ph.wait("vector", s_lnx, xtok)
            if len(htoks) >= 1:
                ph.wait("vector", s_mm, htoks[-1][1])
            ph.op("vector", lambda h, o=c["sq"][0][:, :], i=c["xs"][b][:, 0, :]:
                  h.tensor_tensor(o, i, PS[2][:, :], OP.subtract))
            ph.op("vector", lambda h: h.tensor_tensor(c["sq"][1][:, :], c["sq"][0][:, :],
                                                      PS[3][:, :], OP.mult))
            dtok = ph.op("vector", lambda h, o=hstg[hb][:, :]:
                         h.tensor_copy(o, c["sq"][1][:, :]), sem=s_lndve)
            xr.release(b, s_lndve, dtok)
            ph.wait("gpsimd", s_lndve, dtok)
            htok = ph.dma("gpsimd", h_grp[:, sl // 2, kc, (sl % 2) * 512:(sl % 2) * 512 + SLAB],
                          hstg[hb][:, :], s_h)
            ph.wait("tensor", s_lndve, dtok)
            if ps6g is not None and kc == 0:
                ph.wait("tensor", *ps6g)
            ltok = None
            for tc4 in range(4):
                ltok = ph.op("tensor", lambda h, w=c["sq"][1][:, tc4 * 128:(tc4 + 1) * 128],
                             r=gT[:, kc, :], p=PS[6][:, tc4 * 8:(tc4 + 1) * 8],
                             st=(kc == 0 and tc4 == 0), sp=(kc == KC - 1):
                             h.matmul(p, w, r, start=st, stop=sp),
                             sem=s_mm)
            htoks.append((htok, ltok))
        lg_done = htoks[-1][1]
        for tc4 in range(4):
            tglob = sl * 4 + tc4
            ph.wait("vector", s_mm, lg_done)
            ph.gop("vector", lambda h, p=PS[6][:, tc4 * 8:(tc4 + 1) * 8]:
                   h.tensor_copy(lg[:, :], p), s_dve)
            ph.gop("vector", lambda h: h.reduce_max(col1[:, :], lg[:, :], axis=AX.X), s_dve)
            ph.op("vector", lambda h: h.tensor_scalar(r8a[:, :], lg[:, :], col1[:, :],
                                                      None, OP.subtract), sem=s_dve)
            ph.wait("scalar", s_dve, s_dve.n)
            etok = ph.op("scalar", lambda h: h.activation(r8b[:, :], r8a[:, :], AF.Exp),
                         sem=s_act)
            ph.wait("vector", s_act, etok)
            ph.gop("vector", lambda h: h.reduce_sum(col1[:, :], r8b[:, :], axis=AX.X), s_dve)
            ph.gop("vector", lambda h: h.reciprocal(col2[:, :], col1[:, :]), s_dve)
            ph.gop("vector", lambda h: h.tensor_scalar(lg[:, :], r8b[:, :], col2[:, :],
                                                       None, OP.mult), s_dve)
            ph.gop("vector", lambda h: h.tensor_tensor(r8b[:, 0:4], lg[:, 0:4], lg[:, 4:8],
                                                       OP.max), s_dve)
            ph.gop("vector", lambda h: h.tensor_tensor(r8b[:, 4:6], r8b[:, 0:2], r8b[:, 2:4],
                                                       OP.max), s_dve)
            ph.gop("vector", lambda h: h.tensor_tensor(col1[:, :], r8b[:, 4:5], r8b[:, 5:6],
                                                       OP.max), s_dve)
            ph.gop("vector", lambda h: h.tensor_scalar(r8a[:, :], lg[:, :], col1[:, :],
                                                       None, OP.is_ge), s_dve)
            ph.gop("vector", lambda h: h.scalar_tensor_tensor(r8b[:, :], r8a[:, :], -1e9,
                                                              lg[:, :], OP.mult, OP.add), s_dve)
            ph.gop("vector", lambda h: h.tensor_tensor(r8a[:, 0:4], r8b[:, 0:4], r8b[:, 4:8],
                                                       OP.max), s_dve)
            ph.gop("vector", lambda h: h.tensor_tensor(r8a[:, 4:6], r8a[:, 0:2], r8a[:, 2:4],
                                                       OP.max), s_dve)
            ph.gop("vector", lambda h: h.tensor_tensor(col2[:, :], r8a[:, 4:5], r8a[:, 5:6],
                                                       OP.max), s_dve)
            ph.gop("vector", lambda h: h.tensor_tensor(col3[:, :], col1[:, :], col2[:, :],
                                                       OP.add), s_dve)
            ph.gop("vector", lambda h: h.tensor_scalar(col1[:, :], col3[:, :], 1e-20, None,
                                                       OP.add), s_dve)
            ph.gop("vector", lambda h: h.reciprocal(col3[:, :], col1[:, :]), s_dve)
            ph.gop("vector", lambda h: h.tensor_scalar(r8a[:, :], lg[:, :], col2[:, :],
                                                       None, OP.is_ge), s_dve)
            ph.gop("vector", lambda h: h.tensor_tensor(r8b[:, :], lg[:, :], r8a[:, :],
                                                       OP.mult), s_dve)
            cwtok = ph.gop("vector", lambda h: h.tensor_scalar(r8a[:, :], r8b[:, :], col3[:, :],
                                                               None, OP.mult), s_dve)
            ps6g = (s_dve, cwtok)
            if prev_cw_dma is not None:
                ph.wait("vector", s_cws, prev_cw_dma)
            for j in range(4):
                ph.wait("vector", s_mm, s_mm.n)
                ph.gop("vector", lambda h, e0=2 * j: h.tensor_copy(cwpad[:, 0:1],
                                                                   r8a[:, e0:e0 + 1]), s_dve)
                ctok = ph.gop("vector", lambda h, e1=2 * j + 1: h.tensor_copy(
                    cwpad[:, 64:65], r8a[:, e1:e1 + 1]), s_dve)
                ph.wait("tensor", s_dve, ctok)
                if ps7g is not None:
                    ph.wait("tensor", *ps7g)
                mtok = ph.op("tensor", lambda h: h.transpose(PS[7][:, 0:128], cwpad[:, :],
                                                             c["ident"][:, :]), sem=s_mm)
                ph.wait("vector", s_mm, mtok)
                ph.op("vector", lambda h, o=cwstage[0:1, (2 * j) * 128:(2 * j + 1) * 128]:
                      h.tensor_copy(o, PS[7][0:1, 0:128]))
                p7tok = ph.op("vector", lambda h, o=cwstage[0:1, (2 * j + 1) * 128:(2 * j + 2) * 128]:
                              h.tensor_copy(o, PS[7][64:65, 0:128]), sem=s_dve)
                ps7g = (s_dve, p7tok)
            ph.wait("gpsimd", s_dve, p7tok)
            if s_cws.n:
                ph.wait("gpsimd", s_cws, s_cws.n)
            prev_cw_dma = ph.dma("gpsimd", cw_dram_ap[:, tglob * 128:(tglob + 1) * 128],
                                 cwstage[0:1, :], s_cws)
        bc_free = s_lndve.n
    ln_all = s_h.n
    cw_w_tok = s_cws.n
    ph.emit()

    ph = Ph(nc)
    ph.wait("gpsimd", s_h, ln_all)
    h_toks = [None, None]
    w1g_r = Ring(2)
    w1u_r = Ring(2)
    w2_r = Ring(1)
    sil_r = Ring(2)
    out_r = Ring(2)
    wstg_r = Ring(2)
    ps_free = [None] * 8
    prev_e_acc = ytok
    prev_cwb_tok = None
    w2f_guard = None
    w2f_tok = None

    for ei in range(NE + 1):
        shared = ei == NE
        for sl in range(NSL):
            hb = sl
            if ei == 0:
                h_toks[sl] = ph.dma("gpsimd", hslab[hb][:, :, :], h_grp[:, sl, :, :], s_h)
            htok = h_toks[sl]
            if not shared:
                if sl == 0:
                    if prev_cwb_tok is not None:
                        ph.wait("gpsimd", s_mm, prev_cwb_tok)
                    if s_cwr.n:
                        ph.wait("gpsimd", s_cwr, s_cwr.n)
                    cwtok_l = ph.dma("gpsimd", cwrow[0:1, :],
                                     cw_dram_ap[ei:ei + 1, :], s_cwr)
                cwb_tok = None
                for hf in range(2):
                    pscw = PS[6 + hf]
                    if ps_free[6 + hf] is not None:
                        ph.wait("tensor", *ps_free[6 + hf])
                    if sl == 0 and hf == 0:
                        ph.wait("tensor", s_cwr, cwtok_l)
                        if ei == 0:
                            ph.wait("tensor", s_cws, cw_w_tok)
                    btok = ph.op("tensor", lambda h, p=pscw[:, :],
                                 i=cwrow[0:1, sl * SL + hf * 512:sl * SL + (hf + 1) * 512]:
                                 h.matmul(p, c["ones_row_bf"][:, :], i, start=True, stop=True),
                                 sem=s_mm)
                    ph.wait("scalar", s_mm, btok)
                    ctok = ph.op("scalar", lambda h, o=cwb_sb[:, hf * 512:(hf + 1) * 512],
                                 p=pscw[:, :]: h.copy(o, p), sem=s_act)
                    ps_free[6 + hf] = (s_act, ctok)
                    cwb_tok = ctok
                prev_cwb_tok = btok
            if not shared and sl == 0:
                if w2f_guard is not None:
                    ph.wait("gpsimd", *w2f_guard)
                w2f_tok = None
                for icw in range(NIC):
                    w2f_tok = ph.dma("gpsimd", w2full[:, icw, :],
                                     w2_ap[ei, icw * 128:(icw + 1) * 128, :], s_w)
            act_done = None
            for ic in range(NIC):
                if shared:
                    b1 = wstg_r.acquire(ph, "gpsimd")
                    wtok = ph.dma("gpsimd", c["wstg"][b1][:, :],
                                  sw1_ap[ic * 128:(ic + 1) * 128, :], s_w)
                    g1 = w1g_r.acquire(ph, "sync")
                    ph.wait("sync", s_w, wtok)
                    xtok1 = ph.dma("sync", w1g[g1][:, :, :], c["wstg"][b1][:, :], s_xb,
                                   transpose=True)
                    wstg_r.release(b1, s_xb, xtok1)
                    b2 = wstg_r.acquire(ph, "gpsimd")
                    wtok = ph.dma("gpsimd", c["wstg"][b2][:, :],
                                  sw2_ap[ic * 128:(ic + 1) * 128, :], s_w)
                    g2 = w1u_r.acquire(ph, "sync")
                    ph.wait("sync", s_w, wtok)
                    xtok2 = ph.dma("sync", w1u[g2][:, :, :], c["wstg"][b2][:, :], s_xb,
                                   transpose=True)
                    wstg_r.release(b2, s_xb, xtok2)
                    ph.wait("tensor", s_xb, xtok2)
                else:
                    g1 = w1g_r.acquire(ph, "gpsimd")
                    wtok1 = ph.dma("gpsimd", w1g[g1][:, :, :],
                                   w1_ap[ei, :, ic * 128:(ic + 1) * 128]
                                   .rearrange("(c p) o -> p c o", p=128), s_w)
                    g2 = w1u_r.acquire(ph, "gpsimd")
                    wtok2 = ph.dma("gpsimd", w1u[g2][:, :, :],
                                   w1_ap[ei, :, 1408 + ic * 128:1408 + (ic + 1) * 128]
                                   .rearrange("(c p) o -> p c o", p=128), s_w)
                    ph.wait("tensor", s_w, wtok2)
                ph.wait("tensor", s_h, htok)
                gu_toks = []
                par = ic % 2
                for hf in range(2):
                    c0 = hf * 512
                    gi = hf * 4 + par
                    ui = hf * 4 + 2 + par
                    pg = PS[gi]
                    pu = PS[ui]
                    if ps_free[gi] is not None:
                        ph.wait("tensor", *ps_free[gi])
                    gt = None
                    for kc in range(KC):
                        last = kc == KC - 1
                        gt = ph.op("tensor", lambda h, w=w1g[g1][:, kc, :],
                                   r=hslab[hb][:, kc, c0:c0 + 512], p=pg[:, :],
                                   st=(kc == 0), sp=last:
                                   h.matmul(p, w, r, start=st, stop=sp),
                                   sem=s_mm if last else None)
                    if ps_free[ui] is not None:
                        ph.wait("tensor", *ps_free[ui])
                    ut = None
                    for kc in range(KC):
                        last = kc == KC - 1
                        ut = ph.op("tensor", lambda h, w=w1u[g2][:, kc, :],
                                   r=hslab[hb][:, kc, c0:c0 + 512], p=pu[:, :],
                                   st=(kc == 0), sp=last:
                                   h.matmul(p, w, r, start=st, stop=sp),
                                   sem=s_mm if last else None)
                    gu_toks.append((gt, ut, gi, ui))
                w1g_r.release(g1, s_mm, gu_toks[-1][0])
                w1u_r.release(g2, s_mm, gu_toks[-1][1])
                for hf in range(2):
                    c0 = hf * 512
                    gt, ut, gi, ui = gu_toks[hf]
                    sb = sil_r.acquire(ph, "scalar")
                    ph.wait("scalar", s_mm, gt)
                    atok = ph.op("scalar", lambda h, o=sil[sb][:, :], p=PS[gi][:, :]:
                                 h.activation(o, p, AF.Silu), sem=s_act)
                    ps_free[gi] = (s_act, atok)
                    ph.wait("vector", s_act, atok)
                    ph.wait("vector", s_mm, ut)
                    if shared:
                        dtok = ph.op("vector", lambda h, o=act_sb[:, ic, c0:c0 + 512],
                                     i=sil[sb][:, :], p=PS[ui][:, :]:
                                     h.tensor_tensor(o, i, p, OP.mult), sem=s_dve)
                    else:
                        if ic == 0 and hf == 0:
                            ph.wait("vector", s_act, cwb_tok)
                        ph.op("vector", lambda h, i=sil[sb][:, :], p=PS[ui][:, :]:
                              h.tensor_tensor(c["sq"][0][:, :], i, p, OP.mult))
                        dtok = ph.op("vector", lambda h, o=act_sb[:, ic, c0:c0 + 512],
                                     i=cwb_sb[:, c0:c0 + 512]:
                                     h.tensor_tensor(o, c["sq"][0][:, :], i, OP.mult),
                                     sem=s_dve)
                    ps_free[ui] = (s_dve, dtok)
                    sil_r.release(sb, s_dve, dtok)
                    act_done = dtok
            for oc in range(KC):
                if shared:
                    gw = w2_r.acquire(ph, "gpsimd")
                    b1 = wstg_r.acquire(ph, "gpsimd")
                    wtok = ph.dma("gpsimd", c["wstg"][b1][:, 0:1408],
                                  sw3_ap[oc * 128:(oc + 1) * 128, :], s_w)
                    ph.wait("sync", s_w, wtok)
                    xtok = ph.dma("sync", w2t[gw][:, :, :], c["wstg"][b1][:, 0:1408], s_xb,
                                  transpose=True)
                    wstg_r.release(b1, s_xb, xtok)
                    ph.wait("tensor", s_xb, xtok)
                    wsrc = w2t[gw]
                else:
                    if oc == 0 and sl == 0:
                        ph.wait("tensor", s_w, w2f_tok)
                    wsrc = w2full
                if oc == 0:
                    ph.wait("tensor", s_dve, act_done)
                mtok = etok = None
                for hf in range(2):
                    ob = out_r.acquire(ph, "scalar")
                    pdi = 4 + (oc % 2) * 1 + hf * 2 if False else (oc % 2) * 4 + hf
                    pd = PS[pdi]
                    if ps_free[pdi] is not None:
                        ph.wait("tensor", *ps_free[pdi])
                    for ic in range(NIC):
                        last = ic == NIC - 1
                        mtok = ph.op("tensor",
                                     lambda h, w=wsrc[:, ic, oc * 128:(oc + 1) * 128]
                                     if wsrc is w2full else wsrc[:, ic, :],
                                     r=act_sb[:, ic, hf * 512:(hf + 1) * 512], p=pd[:, :],
                                     st=(ic == 0), sp=last:
                                     h.matmul(p, w, r, start=st, stop=sp),
                                     sem=s_mm if last else None)
                    ph.wait("scalar", s_mm, mtok)
                    etok = ph.op("scalar", lambda h, o=outstg[ob][:, :],
                                 p=pd[:, :]: h.copy(o, p), sem=s_act)
                    ps_free[pdi] = (s_act, etok)
                    ph.wait("gpsimd", s_act, etok)
                    ph.wait("gpsimd", s_acc, prev_e_acc)
                    acctok = ph.dma("gpsimd",
                                    y_ap[oc * 128:(oc + 1) * 128,
                                         sl * SL + hf * 512:sl * SL + (hf + 1) * 512],
                                    outstg[ob][:, :], s_acc, accum_op=OP.add)
                    out_r.release(ob, s_acc, acctok)
                if shared:
                    w2_r.release(gw, s_mm, mtok)
            if not shared and sl == NSL - 1:
                w2f_guard = (s_mm, s_mm.n)
        prev_e_acc = s_acc.n
    ph.wait("sync", s_acc, s_acc.n)
    if hasattr(ctx, "dbg_moe"):
        s_st = ctx.sem(pfx + "dbgst")
        ph.wait("scalar", s_acc, s_acc.n)
        ph.dma("scalar", ctx.dbg_moe["r8b"], r8a[:, :], s_st)
        ph.dma("scalar", ctx.dbg_moe["lg"], lg[:, :], s_st)
        ph.dma("scalar", ctx.dbg_moe["stage"], cwstage[0:1, :], s_st)
        if "gT" in ctx.dbg_moe:
            ph.dma("scalar", ctx.dbg_moe["gT"], gT[:, :, :], s_st)
        if "ident" in ctx.dbg_moe:
            ph.dma("scalar", ctx.dbg_moe["ident"], c["ident"][:, :], s_st)
        ph.wait("scalar", s_st, s_st.n)
    ph.emit()
    return s_acc.n


def l0_rope_tables():
    inv = (np.float32(1.0) / np.float32(10000.0) **
           (np.arange(0, 64, 2, dtype=np.float32) / np.float32(64.0)))
    t = np.arange(T, dtype=np.float32)
    ang = t[None, :] * inv[:, None]
    return np.cos(ang).astype(np.float32), np.sin(ang).astype(np.float32)


def l0_mask_tiles():
    i = np.arange(128)[:, None]
    j = np.arange(128)[None, :]
    mdiag = np.where(j >= i, 0.0, -1e30).astype(np.float32)
    medge = np.where(j < i, 0.0, -1e30).astype(np.float32)
    return mdiag, medge


def build_l0attn_program():
    nc = bass.Bass("TRN2", target_bir_lowering=False)
    x = nc.dram_tensor("x", [D, T], F32, kind="ExternalInput").ap()
    wqc = nc.dram_tensor("wqc", [D, 512], F32, kind="ExternalInput").ap()
    wqu = nc.dram_tensor("wqu", [512, D], F32, kind="ExternalInput").ap()
    wk = nc.dram_tensor("wk", [D, 128], F32, kind="ExternalInput").ap()
    wv = nc.dram_tensor("wv", [D, 128], F32, kind="ExternalInput").ap()
    qnw = nc.dram_tensor("qnw", [128], F32, kind="ExternalInput").ap()
    knw = nc.dram_tensor("knw", [128], F32, kind="ExternalInput").ap()
    sink = nc.dram_tensor("sink", [16], F32, kind="ExternalInput").ap()
    w1 = nc.dram_tensor("w1", [4, 512, D], F32, kind="ExternalInput").ap()
    w2 = nc.dram_tensor("w2", [8192, D], F32, kind="ExternalInput").ap()
    cos = nc.dram_tensor("cos0", [32, T], F32, kind="ExternalInput").ap()
    sin = nc.dram_tensor("sin0", [32, T], F32, kind="ExternalInput").ap()
    mdiag = nc.dram_tensor("mdiag", [128, 128], F32, kind="ExternalInput").ap()
    medge = nc.dram_tensor("medge", [128, 128], F32, kind="ExternalInput").ap()
    y = nc.dram_tensor("y", [D, T], F32, kind="ExternalOutput").ap()
    with ExitStack() as stack:
        ctx = Ctx(nc, stack)
        c = make_common(ctx)
        h_sb = ctx.sbuf("h", [128, KC, T], BF16)
        attn_half = ctx.sbuf("attnh", [128, 8, T], BF16)
        emit_l0attn(ctx, c, "a0_", x, y, wqc, wqu, wk, wv, qnw, knw, sink,
                    w1, w2, cos, sin, mdiag, medge, h_sb, attn_half)
    return nc


def build_l0ffn_program():
    nc = bass.Bass("TRN2", target_bir_lowering=False)
    x = nc.dram_tensor("x", [D, T], F32, kind="ExternalInput").ap()
    gate = nc.dram_tensor("gate", [8, D], F32, kind="ExternalInput").ap()
    w1 = nc.dram_tensor("w1", [8, D, 2816], F32, kind="ExternalInput").ap()
    w2 = nc.dram_tensor("w2", [8, 1408, D], F32, kind="ExternalInput").ap()
    sw1 = nc.dram_tensor("sw1", [1408, D], F32, kind="ExternalInput").ap()
    sw2 = nc.dram_tensor("sw2", [1408, D], F32, kind="ExternalInput").ap()
    sw3 = nc.dram_tensor("sw3", [D, 1408], F32, kind="ExternalInput").ap()
    y = nc.dram_tensor("y", [D, T], F32, kind="ExternalOutput").ap()
    h_dram = nc.dram_tensor("h_dram", [2 * KC * 128, 1024], BF16).ap()
    cw_dram = nc.dram_tensor("cw_dram", [8, T], BF16).ap()
    with ExitStack() as stack:
        ctx = Ctx(nc, stack)
        c = make_common(ctx)
        emit_l0ffn(ctx, c, "f0_", x, y, gate, w1, w2, sw1, sw2, sw3, h_dram, cw_dram)
    return nc


def build_l1ffn_program():
    nc = bass.Bass("TRN2", target_bir_lowering=False)
    x = nc.dram_tensor("x", [D, T], F32, kind="ExternalInput").ap()
    wg = nc.dram_tensor("wg", [8192, D], F32, kind="ExternalInput").ap()
    wu = nc.dram_tensor("wu", [8192, D], F32, kind="ExternalInput").ap()
    wd = nc.dram_tensor("wd", [D, 8192], F32, kind="ExternalInput").ap()
    y = nc.dram_tensor("y", [D, T], F32, kind="ExternalOutput").ap()
    with ExitStack() as stack:
        ctx = Ctx(nc, stack)
        c = make_common(ctx)
        h_sb = ctx.sbuf("h", [128, KC, T], BF16)
        emit_l1ffn(ctx, c, "f1_", x, y, wg, wu, wd, h_sb)
    return nc


def _barrier(ctx, sem, tok):
    ph = Ph(ctx.nc)
    for e in ENGS:
        ph.wait(e, sem, tok)
    ph.emit()


def build_fused_program():
    nc = bass.Bass("TRN2", target_bir_lowering=False)
    x = nc.dram_tensor("x", [D, T], F32, kind="ExternalInput").ap()
    wqc = nc.dram_tensor("wqc", [D, 512], F32, kind="ExternalInput").ap()
    wqu = nc.dram_tensor("wqu", [512, D], F32, kind="ExternalInput").ap()
    wk = nc.dram_tensor("wk", [D, 128], F32, kind="ExternalInput").ap()
    wv = nc.dram_tensor("wv", [D, 128], F32, kind="ExternalInput").ap()
    qnw = nc.dram_tensor("qnw", [128], F32, kind="ExternalInput").ap()
    knw = nc.dram_tensor("knw", [128], F32, kind="ExternalInput").ap()
    sink = nc.dram_tensor("sink", [16], F32, kind="ExternalInput").ap()
    ow1 = nc.dram_tensor("ow1", [4, 512, D], F32, kind="ExternalInput").ap()
    ow2 = nc.dram_tensor("ow2", [8192, D], F32, kind="ExternalInput").ap()
    cos0 = nc.dram_tensor("cos0", [32, T], F32, kind="ExternalInput").ap()
    sin0 = nc.dram_tensor("sin0", [32, T], F32, kind="ExternalInput").ap()
    mdiag = nc.dram_tensor("mdiag", [128, 128], F32, kind="ExternalInput").ap()
    medge = nc.dram_tensor("medge", [128, 128], F32, kind="ExternalInput").ap()
    gate = nc.dram_tensor("gate", [8, D], F32, kind="ExternalInput").ap()
    mw1 = nc.dram_tensor("mw1", [8, D, 2816], F32, kind="ExternalInput").ap()
    mw2 = nc.dram_tensor("mw2", [8, 1408, D], F32, kind="ExternalInput").ap()
    sw1 = nc.dram_tensor("sw1", [1408, D], F32, kind="ExternalInput").ap()
    sw2 = nc.dram_tensor("sw2", [1408, D], F32, kind="ExternalInput").ap()
    sw3 = nc.dram_tensor("sw3", [D, 1408], F32, kind="ExternalInput").ap()
    w_aps = {n: nc.dram_tensor(n, [D, D], F32, kind="ExternalInput").ap()
             for n in ("Wq", "Wk", "Wv", "Wo")}
    b_aps = {n: nc.dram_tensor(n, [D], F32, kind="ExternalInput").ap()
             for n in ("bq", "bk", "bv", "bo")}
    cos1 = nc.dram_tensor("rope_cos", [64, T], F32, kind="ExternalInput").ap()
    sin1 = nc.dram_tensor("rope_sin", [64, T], F32, kind="ExternalInput").ap()
    cm = nc.dram_tensor("cmask", [128, 128], F32, kind="ExternalInput").ap()
    wg = nc.dram_tensor("wg", [8192, D], F32, kind="ExternalInput").ap()
    wu = nc.dram_tensor("wu", [8192, D], F32, kind="ExternalInput").ap()
    wd = nc.dram_tensor("wd", [D, 8192], F32, kind="ExternalInput").ap()
    y = nc.dram_tensor("y", [D, T], F32, kind="ExternalOutput").ap()
    xa = nc.dram_tensor("xa", [D, T], F32).ap()
    xb = nc.dram_tensor("xb", [D, T], F32).ap()
    xc = nc.dram_tensor("xc", [D, T], F32).ap()
    h_dram = nc.dram_tensor("h_dram", [2 * KC * 128, 1024], BF16).ap()
    cw_dram = nc.dram_tensor("cw_dram", [8, T], BF16).ap()

    with ExitStack() as stack:
        ctx = Ctx(nc, stack)
        c = make_common(ctx)
        with ctx.scope():
            h_sb = ctx.sbuf("h0", [128, KC, T], BF16)
            attn_half = ctx.sbuf("attnh0", [128, 8, T], BF16)
            emit_l0attn(ctx, c, "a0_", x, xa, wqc, wqu, wk, wv, qnw, knw, sink,
                        ow1, ow2, cos0, sin0, mdiag, medge, h_sb, attn_half)
        _barrier(ctx, ctx.sem("a0_acc"), ctx.sem("a0_acc").n)
        with ctx.scope():
            emit_l0ffn(ctx, c, "f0_", xa, xb, gate, mw1, mw2, sw1, sw2, sw3,
                       h_dram, cw_dram)
        _barrier(ctx, ctx.sem("f0_acc"), ctx.sem("f0_acc").n)
        with ctx.scope():
            h_sb = ctx.sbuf("h1", [128, KC, T], BF16)
            attn_sb = ctx.sbuf("attn1", [128, 16, T], BF16)
            emit_l1attn(ctx, c, "a1_", xb, xc, w_aps, b_aps, cos1, sin1, cm,
                        h_sb, attn_sb)
        _barrier(ctx, ctx.sem("a1_st"), ctx.sem("a1_st").n)
        with ctx.scope():
            h_sb = ctx.sbuf("h2", [128, KC, T], BF16)
            emit_l1ffn(ctx, c, "f1_", xc, y, wg, wu, wd, h_sb)
    return nc


_PROG_CACHE = {}


def _programs():
    if "p" not in _PROG_CACHE:
        _PROG_CACHE["p"] = (build_l0attn_program(), build_l0ffn_program(),
                            build_l1attn_program(), build_l1ffn_program())
    return _PROG_CACHE["p"]


def _fused_program():
    if "f" not in _PROG_CACHE:
        _PROG_CACHE["f"] = build_fused_program()
    return _PROG_CACHE["f"]


def kernel(**inputs):
    n = 8
    nc = _fused_program()
    cos0, sin0 = l0_rope_tables()
    mdiag, medge = l0_mask_tiles()
    cos1, sin1 = l1_rope_tables()
    cm = causal_mask_tile()
    im = [{"x": np.ascontiguousarray(inputs["L0_attn_h"][b].T),
           "wqc": inputs["L0_attn_wq_comp"], "wqu": inputs["L0_attn_wq_up"],
           "wk": inputs["L0_attn_wk"], "wv": inputs["L0_attn_wv"],
           "qnw": inputs["L0_attn_q_norm_w"], "knw": inputs["L0_attn_k_norm_w"],
           "sink": inputs["L0_attn_sink_logit"],
           "ow1": inputs["L0_attn_out_w1"], "ow2": inputs["L0_attn_out_w2"],
           "cos0": cos0, "sin0": sin0, "mdiag": mdiag, "medge": medge,
           "gate": inputs["L0_ffn_gate_w"], "mw1": inputs["L0_ffn_w1"],
           "mw2": inputs["L0_ffn_w2"], "sw1": inputs["L0_ffn_sw1"],
           "sw2": inputs["L0_ffn_sw2"], "sw3": inputs["L0_ffn_sw3"],
           "Wq": inputs["L1_attn_Wq"], "Wk": inputs["L1_attn_Wk"],
           "Wv": inputs["L1_attn_Wv"], "Wo": inputs["L1_attn_Wo"],
           "bq": inputs["L1_attn_bq"], "bk": inputs["L1_attn_bk"],
           "bv": inputs["L1_attn_bv"], "bo": inputs["L1_attn_bo"],
           "rope_cos": cos1, "rope_sin": sin1, "cmask": cm,
           "wg": inputs["L1_ffn_w_gate"], "wu": inputs["L1_ffn_w_up"],
           "wd": inputs["L1_ffn_w_down"]} for b in range(n)]
    res = run_bass_kernel_spmd(nc, im, core_ids=list(range(n)))
    return np.stack([np.asarray(res.results[b]["y"]).T for b in range(n)])


def kernel_unfused(**inputs):
    n = 8
    nc0a, nc0f, nc1a, nc1f = _programs()
    cos0, sin0 = l0_rope_tables()
    mdiag, medge = l0_mask_tiles()
    cos1, sin1 = l1_rope_tables()
    cm = causal_mask_tile()
    xs = [np.ascontiguousarray(inputs["L0_attn_h"][b].T) for b in range(n)]

    im = [{"x": xs[b],
           "wqc": inputs["L0_attn_wq_comp"], "wqu": inputs["L0_attn_wq_up"],
           "wk": inputs["L0_attn_wk"], "wv": inputs["L0_attn_wv"],
           "qnw": inputs["L0_attn_q_norm_w"], "knw": inputs["L0_attn_k_norm_w"],
           "sink": inputs["L0_attn_sink_logit"],
           "w1": inputs["L0_attn_out_w1"], "w2": inputs["L0_attn_out_w2"],
           "cos0": cos0, "sin0": sin0, "mdiag": mdiag, "medge": medge}
          for b in range(n)]
    res = run_bass_kernel_spmd(nc0a, im, core_ids=list(range(n)))
    xs = [np.asarray(res.results[b]["y"]) for b in range(n)]

    im = [{"x": xs[b], "gate": inputs["L0_ffn_gate_w"],
           "w1": inputs["L0_ffn_w1"], "w2": inputs["L0_ffn_w2"],
           "sw1": inputs["L0_ffn_sw1"], "sw2": inputs["L0_ffn_sw2"],
           "sw3": inputs["L0_ffn_sw3"]} for b in range(n)]
    res = run_bass_kernel_spmd(nc0f, im, core_ids=list(range(n)))
    xs = [np.asarray(res.results[b]["y"]) for b in range(n)]

    im = [{"x": xs[b],
           "Wq": inputs["L1_attn_Wq"], "Wk": inputs["L1_attn_Wk"],
           "Wv": inputs["L1_attn_Wv"], "Wo": inputs["L1_attn_Wo"],
           "bq": inputs["L1_attn_bq"], "bk": inputs["L1_attn_bk"],
           "bv": inputs["L1_attn_bv"], "bo": inputs["L1_attn_bo"],
           "rope_cos": cos1, "rope_sin": sin1, "cmask": cm} for b in range(n)]
    res = run_bass_kernel_spmd(nc1a, im, core_ids=list(range(n)))
    xs = [np.asarray(res.results[b]["y"]) for b in range(n)]

    im = [{"x": xs[b], "wg": inputs["L1_ffn_w_gate"], "wu": inputs["L1_ffn_w_up"],
           "wd": inputs["L1_ffn_w_down"]} for b in range(n)]
    res = run_bass_kernel_spmd(nc1f, im, core_ids=list(range(n)))
    return np.stack([np.asarray(res.results[b]["y"]).T for b in range(n)])
